# version 01
# baseline (speedup 1.0000x reference)
import math
import numpy as np
import ml_dtypes
import concourse.bass as bass
import concourse.mybir as mybir
from concourse.bass_utils import run_bass_kernel_spmd
from contextlib import ExitStack

F32 = mybir.dt.float32
BF16 = mybir.dt.bfloat16
AF = mybir.ActivationFunctionType
ALU = mybir.AluOpType
AX = mybir.AxisListType
nbf = ml_dtypes.bfloat16

D = 1024
N = 2048
H = 16
HD = 64
NDK = D // 128
NQT = N // 128
NC4 = N // 512
DI = 2730
W1C = 2 * DI
IDEPTH = 6
WIN = 256

BIGNEG = -1.0e30


def _f_band(delta):
    d = np.abs(np.asarray(delta, np.float64))
    m = 0.8 * np.exp(-((d - 1.0) ** 2) / (2.0 * 0.5 ** 2)) \
        + 0.4 * np.exp(-((d - 30.0) ** 2) / (2.0 * 2.0 ** 2))
    return m.astype(np.float32)


_HOST_CONSTS = None


def host_consts():
    global _HOST_CONSTS
    if _HOST_CONSTS is not None:
        return _HOST_CONSTS
    c = {}
    c["idn_bf"] = np.eye(128, dtype=np.float32).astype(nbf)
    c["idn_f32"] = np.eye(128, dtype=np.float32)
    p = np.arange(128)[:, None]
    j = np.arange(256)[None, :]
    c["mask8"] = (8.0 * _f_band(j - 61 - p)).astype(np.float32)
    c["f_max"] = float(_f_band(np.arange(-100, 100)).max())
    jj = np.arange(384)[None, :]
    c["mask01"] = (((jj >= p) & (jj <= p + WIN)).astype(np.float32)).astype(nbf)
    ind = np.zeros((128, NDK, 16), np.float32)
    for dk in range(NDK):
        ind[0:64, dk, 2 * dk] = 1.0
        ind[64:128, dk, 2 * dk + 1] = 1.0
    c["ind16"] = ind.reshape(128, NDK * 16).astype(nbf)
    selb = np.zeros((16, 128), np.float32)
    selb[0, 0:64] = 1.0
    selb[8, 64:128] = 1.0
    c["selb"] = selb.astype(nbf)
    sel = np.zeros((128, 16, 16), np.float32)
    for h in range(16):
        sel[:, h, h] = 1.0
    c["sel16"] = sel.reshape(128, 256).astype(nbf)
    pr = np.zeros((128, 64), np.float32)
    pr[np.arange(128), np.arange(128) % 64] = 1.0
    c["poolind"] = pr.astype(nbf)
    half = np.arange(0, HD, 2, dtype=np.float64)
    inv_freq = 1.0 / 10000.0 ** (half / HD)
    t = np.arange(N, dtype=np.float64)
    freqs = t[:, None] * inv_freq[None, :]
    freqs = np.concatenate([freqs, freqs], -1)
    base_scale = (half + 0.4 * HD) / (1.4 * HD)
    power = (t - N // 2) / (WIN // 2)
    scale = base_scale[None, :] ** power[:, None]
    scale = np.concatenate([scale, scale], -1)
    cos = np.cos(freqs)
    sin = np.sin(freqs)
    sign = np.where(np.arange(HD) < HD // 2, -1.0, 1.0)
    cs = (cos * scale).astype(np.float32)
    ss = (sin * scale * sign[None, :]).astype(np.float32)
    ci = (cos / scale).astype(np.float32)
    si = (sin / scale * sign[None, :]).astype(np.float32)

    def fm(tab):
        return np.concatenate([tab.T, tab.T], axis=0).astype(np.float32)
    c["rot_cs"] = fm(cs)
    c["rot_ss"] = fm(ss)
    c["rot_ci"] = fm(ci)
    c["rot_si"] = fm(si)
    _HOST_CONSTS = c
    return c


def percol(v):
    return np.ascontiguousarray(np.asarray(v, np.float32).reshape(NDK, 128).T)


class EC:

    def __init__(self, nc, blk):
        self.nc = nc
        self.blk = blk
        self.sems = {}
        self.cnt = {}
        self._exit = []
        self._fut = {}
        for c in ("cPE", "cDVE", "cACT", "cPL"):
            self.sem(c)

    def sem(self, name):
        if name not in self.sems:
            cm = self.nc.semaphore(name)
            h = cm.__enter__()
            self._exit.append(cm)
            self.sems[name] = h
            self.cnt[name] = 0
        return self.sems[name]

    def inc(self, name, inst, n=1):
        inst.then_inc(self.sem(name), n)
        self.cnt[name] += n
        return self.cnt[name]

    def wait(self, eng, name, val=None):
        if val is None:
            val = self.cnt[name]
        if val <= 0:
            return
        assert val <= self.cnt.get(name, 0), (name, val, self.cnt.get(name))
        eng.wait_ge(self.sem(name), val)

    def wait_future(self, eng, name, val):
        if val <= 0:
            return
        self._fut[name] = max(self._fut.get(name, 0), val)
        eng.wait_ge(self.sem(name), val)

    def close(self):
        for name, val in self._fut.items():
            assert val <= self.cnt.get(name, 0), \
                (f"unsatisfied future wait {name}", val, self.cnt.get(name))
        for cm in reversed(self._exit):
            cm.__exit__(None, None, None)


class _EngW:
    CHAIN = None

    def __init__(self, ec, raw):
        self.ec = ec
        self.r = raw
        ec.sem(self.CHAIN)

    def _do(self, f, *a, **kw):
        ec = self.ec
        c = self.CHAIN
        ec.wait(self.r, c)
        inst = f(*a, **kw)
        return ec.inc(c, inst)

    def wait(self, name, val=None):
        self.ec.wait(self.r, name, val)

    def wait_future(self, name, val):
        self.ec.wait_future(self.r, name, val)


class VW(_EngW):
    CHAIN = "cDVE"

    def tt(self, out, a, b, op=ALU.mult):
        return self._do(self.r.tensor_tensor, out, a, b, op)

    def add(self, out, a, b):
        return self._do(self.r.tensor_add, out, a, b)

    def sub(self, out, a, b):
        return self._do(self.r.tensor_sub, out, a, b)

    def ts(self, out, a, s1, s2=None, op0=ALU.mult, op1=ALU.bypass,
           accum_out=None):
        return self._do(self.r.tensor_scalar, out, a, s1, s2, op0, op1,
                        accum_out=accum_out)

    def stt(self, out, in0, scalar, in1, op0=ALU.mult, op1=ALU.mult,
            accum_out=None):
        return self._do(self.r.scalar_tensor_tensor, out, in0, scalar, in1,
                        op0, op1, accum_out=accum_out)

    def red(self, out, in_, op=ALU.max, negate=False):
        return self._do(self.r.tensor_reduce, out, in_, AX.X, op, negate=negate)

    def recip(self, out, in_):
        return self._do(self.r.reciprocal, out, in_)

    def copy(self, out, in_):
        return self._do(self.r.tensor_copy, out, in_)

    def memset(self, ap, v):
        return self._do(self.r.memset, ap, v)


class SW(_EngW):
    CHAIN = "cACT"

    def act(self, out, in_, func, bias=0.0, scale=1.0, accum_out=None):
        return self._do(self.r.activation, out, in_, func, bias=bias,
                        scale=scale, accum_out=accum_out)

    def copy(self, out, in_):
        return self._do(self.r.copy, out, in_)

    def mul(self, out, in_, c):
        return self._do(self.r.mul, out, in_, c)


class GW(_EngW):
    CHAIN = "cPL"

    def tt(self, out, a, b, op=ALU.mult):
        return self._do(self.r.tensor_tensor, out, a, b, op)

    def copy(self, out, in_):
        return self._do(self.r.tensor_copy, out, in_)

    def memset(self, ap, v):
        return self._do(self.r.memset, ap, v)

    def dma(self, sem, out, in_):
        return self.ec.inc(sem, self.r.dma_start(out=out, in_=in_), 16)


class TW:
    CHAIN = "cPE"

    def __init__(self, ec, raw):
        self.ec = ec
        self.r = raw
        ec.sem(self.CHAIN)

    def mm(self, out, lhsT, rhs, start, stop, mark=False, skip=False):
        inst = self.r.matmul(out, lhsT, rhs, start=start, stop=stop,
                             skip_group_check=skip)
        if mark:
            return self.ec.inc(self.CHAIN, inst)
        return None

    def trans(self, out, in_, idn, mark=False):
        inst = self.r.transpose(out, in_, idn)
        if mark:
            return self.ec.inc(self.CHAIN, inst)
        return None

    def wait(self, name, val=None):
        self.ec.wait(self.r, name, val)

    def wait_future(self, name, val):
        self.ec.wait_future(self.r, name, val)




def _alloc(nc, st, sbufs, psums):
    d = {}
    for name, shape, dt in sbufs:
        d[name] = st.enter_context(nc.sbuf_tensor(name, list(shape), dt))
    for name, shape in psums:
        d[name] = st.enter_context(nc.psum_tensor(name, list(shape), F32))
    return d


def ap3(t, off, dims):
    return bass.AP(t.tensor, t.offset + off, dims)


def emit_ln(ec, xbf_sl, sq_scr, ones_c, ones_r, ps_stats, ps_bc, sums_bf,
            mu_b, rstd_b, tmp_b, std_b, want_aug=False, n_tok=N,
            dep=None):
    blk = ec.blk
    nch = n_tok // 512
    sq_tok = [None] * NDK
    st_tok = [None]
    cons_tok = [None] * NDK

    def pool_fn(g):
        G = GW(ec, g)
        for d_ in (dep or []):
            G.wait_future(*d_) if d_[1] > ec.cnt.get(d_[0], 0) else G.wait(*d_)
        for dk in range(NDK):
            if dk >= 2:
                G.wait_future("cPE", cons_tok[dk - 2])
            x = xbf_sl(dk)
            sq_tok[dk] = G.tt(
                sq_scr[:, (dk % 2) * n_tok:(dk % 2) * n_tok + n_tok],
                x, x, ALU.mult)
    pe0 = ec.cnt["cPE"] if "cPE" in ec.cnt else 0
    ec.sem("cPE")
    pe0 = ec.cnt["cPE"]
    for dk in range(NDK):
        cons_tok[dk] = pe0 + dk + 1
    blk.gpsimd(pool_fn)

    def t_fn(t):
        T = TW(ec, t)
        for dk in range(NDK):
            T.wait("cPL", sq_tok[dk])
            for ncc in range(nch):
                sl = slice(ncc * 512, ncc * 512 + 512)
                t.matmul(ps_stats[0:1, sl], ones_c[:, :], xbf_sl(dk)[:, sl],
                         start=(dk == 0), stop=(dk == NDK - 1))
                mark = (ncc == nch - 1)
                tok = T.mm(ps_stats[32:33, sl], ones_c[:, :],
                           sq_scr[:, (dk % 2) * n_tok + ncc * 512:
                                  (dk % 2) * n_tok + ncc * 512 + 512],
                           start=(dk == 0), stop=(dk == NDK - 1), mark=mark)
                if mark:
                    assert tok == cons_tok[dk], (tok, cons_tok[dk])
        st_tok[0] = cons_tok[NDK - 1]
    blk.tensor(t_fn)

    hi_tok = [None] * nch
    lo_tok = [None] * nch
    bc_tok = [None] * nch
    be_tok = [None] * nch
    pe_now = ec.cnt["cPE"]
    for i in range(nch):
        bc_tok[i] = pe_now + i + 1

    def s_hilo(s):
        S = SW(ec, s)
        for ncc in range(nch):
            sl = slice(ncc * 512, ncc * 512 + 512)
            if ncc == 0:
                S.wait("cPE", st_tok[0])
            S.copy(sums_bf[0:1, 0:512], ps_stats[0:1, sl])
            hi_tok[ncc] = S.copy(sums_bf[0:1, 1024:1536], ps_stats[32:33, sl])
            S.wait_future("cPE", bc_tok[ncc])
            S.copy(mu_b[:, sl], ps_bc[:, 0:512])
            be_tok[ncc] = S.copy(std_b[:, sl], ps_bc[:, 512:1024])
    blk.scalar(s_hilo)

    def v_lo(v):
        V = VW(ec, v)
        for ncc in range(nch):
            sl = slice(ncc * 512, ncc * 512 + 512)
            V.wait_future("cACT", hi_tok[ncc])
            V.sub(sums_bf[0:1, 512:1024], ps_stats[0:1, sl],
                  sums_bf[0:1, 0:512])
            lo_tok[ncc] = V.sub(sums_bf[0:1, 1536:2048], ps_stats[32:33, sl],
                                sums_bf[0:1, 1024:1536])
    blk.vector(v_lo)

    def t_bc(t):
        T = TW(ec, t)
        for ncc in range(nch):
            T.wait_future("cDVE", lo_tok[ncc])
            if ncc >= 1:
                T.wait_future("cACT", be_tok[ncc - 1])
            t.matmul(ps_bc[:, 0:512], ones_r[:, :], sums_bf[0:1, 0:512],
                     start=True, stop=False)
            t.matmul(ps_bc[:, 0:512], ones_r[:, :], sums_bf[0:1, 512:1024],
                     start=False, stop=True)
            t.matmul(ps_bc[:, 512:1024], ones_r[:, :], sums_bf[0:1, 1024:1536],
                     start=True, stop=False)
            tok = T.mm(ps_bc[:, 512:1024], ones_r[:, :],
                       sums_bf[0:1, 1536:2048], start=False, stop=True,
                       mark=True)
            assert tok == bc_tok[ncc], (tok, bc_tok[ncc])
    blk.tensor(t_bc)

    out_tok = {}

    def v_fin(v):
        V = VW(ec, v)
        V.wait_future("cACT", be_tok[nch - 1])
        V.ts(mu_b[:, :], mu_b[:, :], 1.0 / D)
        V.ts(std_b[:, :], std_b[:, :], 1.0 / D)
        V.tt(tmp_b[:, :], mu_b[:, :], mu_b[:, :], ALU.mult)
        V.sub(std_b[:, :], std_b[:, :], tmp_b[:, :])
        out_tok["var"] = V.ts(std_b[:, :], std_b[:, :], 1e-5, op0=ALU.add)
    blk.vector(v_fin)

    def s_sqrt(s):
        S = SW(ec, s)
        S.wait("cDVE", out_tok["var"])
        out_tok["std"] = S.act(std_b[:, :], std_b[:, :], AF.Sqrt)
    blk.scalar(s_sqrt)

    def v_fin2(v):
        V = VW(ec, v)
        V.wait("cACT", out_tok["std"])
        out_tok["rstd"] = V.recip(rstd_b[:, :], std_b[:, :])
        if want_aug:
            out_tok["aug"] = V.stt(tmp_b[:, :], mu_b[:, :], -1.0,
                                   rstd_b[:, :], ALU.mult, ALU.mult)
        out_tok["dn"] = ec.cnt["cDVE"]
    blk.vector(v_fin2)
    return out_tok


def build_pre():
    nc = bass.Bass("TRN2", target_bir_lowering=False)
    x_h = nc.dram_tensor("x", [D, N], F32, kind="ExternalInput").ap()
    xn_h = nc.dram_tensor("xn", [D, N], F32, kind="ExternalOutput").ap()
    with (
        nc.sbuf_tensor("xf", [128, NDK * N], F32) as xf,
        nc.sbuf_tensor("xb", [128, NDK * N], BF16) as xb,
        nc.sbuf_tensor("sq", [128, 2 * N], BF16) as sqs,
        nc.sbuf_tensor("sums_bf", [1, 2048], BF16) as sums_bf,
        nc.sbuf_tensor("mu_b", [128, N], F32) as mu_b,
        nc.sbuf_tensor("rstd_b", [128, N], F32) as rstd_b,
        nc.sbuf_tensor("tmp_b", [128, N], F32) as tmp_b,
        nc.sbuf_tensor("std_b", [128, N], F32) as std_b,
        nc.sbuf_tensor("ones_c", [128, 1], BF16) as ones_c,
        nc.sbuf_tensor("ones_r", [1, 128], BF16) as ones_r,
        nc.sbuf_tensor("xt", [128, 2 * N], F32) as xt,
        nc.psum_tensor("ps_st", [128, 2048], F32) as ps_st,
        nc.psum_tensor("ps_bc", [128, 1024], F32) as ps_bc,
        nc.Block() as blk,
    ):
        ec = EC(nc, blk)
        for s in ("ld", "ldb", "st0", "st1"):
            ec.sem(s)

        def g_load(g):
            g.memset(ones_c[:, :], 1.0)
            g.memset(ones_r[:, :], 1.0)
            for dk in range(NDK):
                ec.inc("ldb", g.dma_start(
                    out=xb[:, dk * N:(dk + 1) * N],
                    in_=x_h[dk * 128:(dk + 1) * 128, :]), 16)
        blk.gpsimd(g_load)

        def s_load(s):
            for dk in range(NDK):
                ec.inc("ld", s.dma_start(
                    out=xf[:, dk * N:(dk + 1) * N],
                    in_=x_h[dk * 128:(dk + 1) * 128, :]), 16)
        blk.sync(s_load)

        tok = emit_ln(ec,
                      xbf_sl=lambda dk: xb[:, dk * N:(dk + 1) * N],
                      sq_scr=sqs, ones_c=ones_c, ones_r=ones_r,
                      ps_stats=ps_st, ps_bc=ps_bc, sums_bf=sums_bf,
                      mu_b=mu_b, rstd_b=rstd_b, tmp_b=tmp_b, std_b=std_b,
                      dep=[("ldb", NDK * 16)])

        xn_tok = [None] * NDK

        def v_xn(v):
            V = VW(ec, v)
            V.wait("cDVE", tok["dn"])
            V.wait("ld")
            for dk in range(NDK):
                b = (dk % 2) * N
                if dk >= 2:
                    V.wait_future("st%d" % (dk % 2), (dk // 2) * 16)
                V.sub(xt[:, b:b + N], xf[:, dk * N:(dk + 1) * N], mu_b[:, :])
                xn_tok[dk] = V.tt(xt[:, b:b + N], xt[:, b:b + N],
                                  rstd_b[:, :], ALU.mult)
        blk.vector(v_xn)

        def s_store(s):
            for dk in range(NDK):
                b = (dk % 2) * N
                ec.wait(s, "cDVE", xn_tok[dk])
                ec.inc("st%d" % (dk % 2), s.dma_start(
                    out=xn_h[dk * 128:(dk + 1) * 128, :],
                    in_=xt[:, b:b + N]), 16)
            ec.wait(s, "st0")
            ec.wait(s, "st1")
        blk.sync(s_store)

        ec.close()
    return nc


def build_l0attn(nits=32, no_mask=False, no_rd=False, no_pvd=False, no_exp=False, no_rdb=False):
    nc = bass.Bass("TRN2", target_bir_lowering=False)
    xn_h = nc.dram_tensor("xn", [D, N], F32, kind="ExternalInput").ap()
    wq_h = nc.dram_tensor("wq", [D, D], F32, kind="ExternalInput").ap()
    wk_h = nc.dram_tensor("wk", [D, D], F32, kind="ExternalInput").ap()
    wv_h = nc.dram_tensor("wv", [D, D], F32, kind="ExternalInput").ap()
    wo_h = nc.dram_tensor("wo", [D, D], F32, kind="ExternalInput").ap()
    bq_h = nc.dram_tensor("bq", [128, NDK], F32, kind="ExternalInput").ap()
    bk_h = nc.dram_tensor("bk", [128, NDK], F32, kind="ExternalInput").ap()
    bv_h = nc.dram_tensor("bv", [1, D], F32, kind="ExternalInput").ap()
    bo_h = nc.dram_tensor("bo", [128, NDK], F32, kind="ExternalInput").ap()
    lam_h = nc.dram_tensor("lam", [1], F32, kind="ExternalInput").ap()
    m8_h = nc.dram_tensor("mask8", [128, 256], F32, kind="ExternalInput").ap()
    ind_h = nc.dram_tensor("ind16", [128, NDK * 16], BF16,
                           kind="ExternalInput").ap()
    idnf_h = nc.dram_tensor("idnf", [16, 16], F32, kind="ExternalInput").ap()
    ao_h = nc.dram_tensor("ao", [D, N], F32, kind="ExternalOutput").ap()
    FMAX = host_consts()["f_max"]

    st_ = ExitStack()
    with st_:
        _sb = [
            ("sc32", (128, NDK * N), BF16),
            ("q_sb", (128, NDK * N), BF16),
            ("k_sb", (128, NDK * N), BF16),
            ("v_sb", (128, NQT * 1024), BF16),
            ("war", (128, NDK * 2048), BF16),
            ("wvr", (128, NDK * 1024), BF16),
            ("q2", (128, 2 * N), BF16),
            ("eAB", (128, 4 * 512), BF16),
            ("rds", (1, 1024), F32),
            ("rds_bf", (1, 2048), BF16),
            ("hob", (1, 256), BF16),
            ("rdb", (128, 512), F32),
            ("out_t", (128, 512), F32),
            ("vb_b", (128, 1024), BF16),
            ("bq4", (128, NDK), F32),
            ("bk4", (128, NDK), F32),
            ("bo4", (128, NDK), F32),
            ("bvr", (1, D), BF16),
            ("lamc", (128, 1), F32),
            ("maskl", (128, 256), F32),
            ("nm", (16, 8), F32),
            ("cnrow", (1, 16), BF16),
            ("cneg_b", (128, 16), F32),
            ("idn16", (16, 16), F32),
            ("ind_s", (128, NDK * 16), BF16),
            ("ones_c", (128, 1), BF16),
            ("ones_r", (1, 128), BF16),
        ]
        _ps = [
            ("psA", (128, 1024)),
            ("psB", (128, 1024)),
            ("psC", (128, 1024)),
            ("psE", (128, 1024)),
        ]
        _t = _alloc(nc, st_, _sb, _ps)
        (sc32, q_sb, k_sb, v_sb, war, wvr, q2, eAB, rds, rds_bf, hob, rdb, out_t, vb_b, bq4, bk4, bo4, bvr, lamc, maskl, nm, cnrow, cneg_b, idn16, ind_s, ones_c, ones_r, psA, psB, psC, psE,) = (_t["sc32"], _t["q_sb"], _t["k_sb"], _t["v_sb"], _t["war"], _t["wvr"], _t["q2"], _t["eAB"], _t["rds"], _t["rds_bf"], _t["hob"], _t["rdb"], _t["out_t"], _t["vb_b"], _t["bq4"], _t["bk4"], _t["bo4"], _t["bvr"], _t["lamc"], _t["maskl"], _t["nm"], _t["cnrow"], _t["cneg_b"], _t["idn16"], _t["ind_s"], _t["ones_c"], _t["ones_r"], _t["psA"], _t["psB"], _t["psC"], _t["psE"],)
        blk = st_.enter_context(nc.Block())
        ec = EC(nc, blk)
        for sname in ("ld", "ldg", "ldw", "ldwo", "st0", "st1"):
            ec.sem(sname)
        tok = {}

        def g_load(g):
            G = GW(ec, g)
            G.memset(ones_c[:, :], 1.0)
            G.memset(ones_r[:, :], 1.0)
            tok["hob0"] = G.memset(hob[:, :], 0.0)
            for dk in range(NDK):
                G.dma("ldg", sc32[:, dk * N:(dk + 1) * N],
                      xn_h[dk * 128:(dk + 1) * 128, :])
            for (dst, src, off) in ((war, wq_h, 0), (war, wk_h, NDK * 1024),
                                    (wvr, wv_h, 0)):
                for dk in range(NDK):
                    G.dma("ldw", dst[:, off + dk * 1024:off + (dk + 1) * 1024],
                          src[dk * 128:(dk + 1) * 128, :])
            G.dma("ldg", bvr[:, :], bv_h[:, :])
            G.dma("ldw", ind_s[:, :], ind_h[:, :])
        blk.gpsimd(g_load)

        def s_load(s):
            for (dst, src) in ((bq4, bq_h), (bk4, bk_h), (bo4, bo_h),
                               (maskl, m8_h)):
                ec.inc("ld", s.dma_start(out=dst[:, :], in_=src[:, :]), 16)
            lsrc = bass.AP(lam_h.tensor, 0, [[0, 128], [1, 1]])
            ec.inc("ld", s.dma_start(out=lamc[:, :], in_=lsrc), 16)
            ec.inc("ld", s.dma_start(out=idn16[:, :], in_=idnf_h[:, :]), 16)
        blk.sync(s_load)

        def v_init(v):
            V = VW(ec, v)
            V.wait("cPL", tok["hob0"])
            V.memset(hob[0:1, 0:64], 1.0)
            tok["hob"] = V.memset(hob[0:1, 192:256], 1.0)
            V.wait("ld")
            tok["maskl"] = V.ts(maskl[:, :], maskl[:, :], lamc[:, 0:1],
                                op0=ALU.mult)
        blk.vector(v_init)

        def t_proj(t):
            T = TW(ec, t)
            T.wait("ld")
            T.wait("ldg")
            T.wait("ldw")
            for idx in range(64):
                ti, oc, ncc = idx // 32, (idx // NC4) % NDK, idx % NC4
                woff = ti * NDK * 1024
                pp = idx % 2
                bank = psA[:, pp * 512:pp * 512 + 512]
                if idx >= 2:
                    T.wait_future("cACT", tok[("qk_base",)] + idx - 1)
                for dk in range(NDK):
                    T.mm(bank,
                         war[:, woff + dk * 1024 + oc * 128:
                             woff + dk * 1024 + oc * 128 + 128],
                         sc32[:, dk * N + ncc * 512:dk * N + ncc * 512 + 512],
                         start=(dk == 0), stop=(dk == NDK - 1),
                         mark=(dk == NDK - 1))
        tok[("qk_base",)] = ec.cnt["cACT"]
        pe0 = ec.cnt["cPE"]
        blk.tensor(t_proj)

        def s_proj(s):
            S = SW(ec, s)
            for idx in range(64):
                ti, oc, ncc = idx // 32, (idx // NC4) % NDK, idx % NC4
                dst = q_sb if ti == 0 else k_sb
                b4 = bq4 if ti == 0 else bk4
                pp = idx % 2
                bank = psA[:, pp * 512:pp * 512 + 512]
                S.wait("cPE", pe0 + idx + 1)
                t2 = S.act(dst[:, oc * N + ncc * 512:oc * N + ncc * 512 + 512],
                           bank, AF.Identity, bias=b4[:, oc:oc + 1])
                assert t2 == tok[("qk_base",)] + idx + 1
        blk.scalar(s_proj)
        qk_done = ec.cnt["cACT"]

        def t_vb(t):
            T = TW(ec, t)
            T.wait("ldg")
            for half in range(2):
                tok[("vbb", half)] = T.mm(
                    psB[:, half * 512:half * 512 + 512], ones_r[:, :],
                    bvr[0:1, half * 512:half * 512 + 512],
                    start=True, stop=True, mark=True)
        blk.tensor(t_vb)

        def s_vb(s):
            S = SW(ec, s)
            for half in range(2):
                S.wait("cPE", tok[("vbb", half)])
                tok[("vbe", half)] = S.copy(
                    vb_b[:, half * 512:half * 512 + 512],
                    psB[:, half * 512:half * 512 + 512])
        blk.scalar(s_vb)

        vev0 = ec.cnt["cDVE"]

        def t_vproj(t):
            T = TW(ec, t)
            T.wait("cACT", tok[("vbe", 1)])
            for idx in range(32):
                nt, vc = idx // 2, idx % 2
                pp = idx % 2
                bank = psB[:, pp * 512:pp * 512 + 512]
                if idx >= 2:
                    T.wait_future("cDVE", vev0 + idx - 1)
                for dk in range(NDK):
                    T.mm(bank,
                         sc32[:, dk * N + nt * 128:dk * N + nt * 128 + 128],
                         wvr[:, dk * 1024 + vc * 512:dk * 1024 + vc * 512 + 512],
                         start=(dk == 0), stop=(dk == NDK - 1),
                         mark=(dk == NDK - 1))
        pe_v0 = ec.cnt["cPE"]
        blk.tensor(t_vproj)

        def v_vproj(v):
            V = VW(ec, v)
            for idx in range(32):
                nt, vc = idx // 2, idx % 2
                pp = idx % 2
                bank = psB[:, pp * 512:pp * 512 + 512]
                V.wait("cPE", pe_v0 + idx + 1)
                t2 = V.tt(v_sb[:, nt * 1024 + vc * 512:nt * 1024 + vc * 512 + 512],
                          bank, vb_b[:, vc * 512:vc * 512 + 512], ALU.add)
                assert t2 == vev0 + idx + 1
        blk.vector(v_vproj)
        v_done = ec.cnt["cDVE"]

        for ti in range(2):
            src = q_sb if ti == 0 else k_sb
            sq_tok = [None] * NDK
            cons_base = ec.cnt["cPE"]

            def g_sq(g, src=src, ti=ti, sq_tok=sq_tok, cons_base=cons_base):
                G = GW(ec, g)
                if ti == 0:
                    G.wait("ldw")
                    G.wait_future("cACT", qk_done)
                for dk in range(NDK):
                    if dk >= 2 or ti == 1:
                        G.wait_future("cPE", cons_base + dk - 1)
                    x = src[:, dk * N:(dk + 1) * N]
                    sq_tok[dk] = G.tt(q2[:, (dk % 2) * N:(dk % 2) * N + N],
                                      x, x, ALU.mult)
            blk.gpsimd(g_sq)

            def t_nm(t, ti=ti, sq_tok=sq_tok, cons_base=cons_base):
                T = TW(ec, t)
                if ti == 0:
                    T.wait_future("cDVE", v_done)
                else:
                    T.wait("cDVE", tok[("nm", 0)])
                for dk in range(NDK):
                    T.wait("cPL", sq_tok[dk])
                    for ncc in range(NC4):
                        ps = psA if ncc < 2 else psB
                        col = (ncc % 2) * 512
                        t2 = T.mm(ps[0:16, col:col + 512],
                                  ind_s[:, dk * 16:dk * 16 + 16],
                                  q2[:, (dk % 2) * N + ncc * 512:
                                     (dk % 2) * N + ncc * 512 + 512],
                                  start=(dk == 0), stop=(dk == NDK - 1),
                                  mark=(ncc == NC4 - 1))
                        if ncc == NC4 - 1:
                            assert t2 == cons_base + dk + 1
            blk.tensor(t_nm)

            def v_nm(v, ti=ti):
                V = VW(ec, v)
                V.wait("cPE", ec.cnt["cPE"])
                V.red(nm[0:16, 4:5], psA[0:16, 0:1024], op=ALU.max)
                V.red(nm[0:16, 5:6], psB[0:16, 0:1024], op=ALU.max)
                tok[("nm", ti)] = V.tt(nm[0:16, ti:ti + 1], nm[0:16, 4:5],
                                       nm[0:16, 5:6], ALU.max)
            blk.vector(v_nm)
        def s_nm(s):
            S = SW(ec, s)
            S.wait("cDVE", tok[("nm", 1)])
            tok["nms"] = S.act(nm[0:16, 0:2], nm[0:16, 0:2], AF.Sqrt)
        blk.scalar(s_nm)

        def v_c(v):
            V = VW(ec, v)
            V.wait("cACT", tok["nms"])
            V.tt(nm[0:16, 2:3], nm[0:16, 0:1], nm[0:16, 1:2], ALU.mult)
            V.ts(nm[0:16, 3:4], lamc[0:16, 0:1], FMAX, op0=ALU.mult)
            V.stt(nm[0:16, 2:3], nm[0:16, 2:3], 0.125, nm[0:16, 3:4],
                  ALU.mult, ALU.add)
            tok["c"] = V.ts(nm[0:16, 2:3], nm[0:16, 2:3], -1.0, op0=ALU.mult)
        blk.vector(v_c)

        def t_cn(t):
            T = TW(ec, t)
            T.wait("ld")
            T.wait("cDVE", tok["c"])
            tok["ct"] = T.trans(psE[0:1, 0:16], nm[0:16, 2:3], idn16[:, :],
                                mark=True)
        blk.tensor(t_cn)

        def s_cn(s):
            S = SW(ec, s)
            S.wait("cPE", tok["ct"])
            tok["cr"] = S.copy(cnrow[0:1, :], psE[0:1, 0:16])
        blk.scalar(s_cn)

        def t_cb(t):
            T = TW(ec, t)
            T.wait("cACT", tok["cr"])
            tok["cb"] = T.mm(psE[:, 512:528], ones_r[:, :], cnrow[0:1, :],
                             start=True, stop=True, mark=True)
        blk.tensor(t_cb)

        def s_cb(s):
            S = SW(ec, s)
            S.wait("cPE", tok["cb"])
            tok["cbe"] = S.copy(cneg_b[:, :], psE[:, 512:528])
        blk.scalar(s_cb)

        def band_ovl(kt, qc):
            c0 = max(kt * 128 - 61, qc * 512, 0)
            c1 = min(kt * 128 + 189, qc * 512 + 512, N)
            return None if c0 >= c1 else (c0, c1)

        ex_base = ec.cnt["cACT"]

        def exp_tok(it, kt, j):
            return ex_base + it * 33 + kt * 2 + j + 1

        def rdbe_tok(it):
            return ex_base + it * 33 + 33

        qk_tok = {}
        mk_tok = {}
        pv_tok = {}
        rd_tok = {}
        rdb_tok = {}
        evac_tok = {}

        for it in range(nits):
            hp, qc = it // 4, it % 4

            def t_main(t, hp=hp, qc=qc, it=it):
                T = TW(ec, t)
                if it == 0:
                    T.wait("cDVE", v_done)
                    T.wait("cACT", tok["cbe"])
                    T.wait("cDVE", tok["hob"])
                    T.wait("cDVE", tok["maskl"])
                else:
                    T.wait_future("cDVE", evac_tok[it - 1])

                def pvd(kt):
                    pp = kt % 2
                    eA = eAB[:, pp * 512:pp * 512 + 512]
                    eB = eAB[:, 1024 + pp * 512:1024 + pp * 512 + 512]
                    pv = psC[:, 0:512]
                    psD = psC[:, 512:1024]
                    T.wait_future("cACT", exp_tok(it, kt, 1))
                    if no_pvd:
                        pv_tok[(it, kt)] = ec.inc("cPE", T.r.nop())
                        return
                    st = (kt == 0)
                    sp = (kt == NQT - 1)
                    T.mm(pv[0:64, :],
                         v_sb[:, kt * 1024 + (2 * hp) * 64:
                              kt * 1024 + (2 * hp) * 64 + 64],
                         eA, start=st, stop=sp, skip=True)
                    T.mm(pv[64:128, :],
                         v_sb[:, kt * 1024 + (2 * hp + 1) * 64:
                              kt * 1024 + (2 * hp + 1) * 64 + 64],
                         eB, start=st, stop=sp, skip=True)
                    T.mm(psD[0:1, :], ones_c[:, :], eA, start=st, stop=sp,
                         skip=True)
                    pv_tok[(it, kt)] = T.mm(psE[0:1, 512:1024], ones_c[:, :],
                                            eB, start=st, stop=sp, mark=True,
                                            skip=True)

                for kt in range(NQT):
                    pp = kt % 2
                    scA = psA[:, pp * 512:pp * 512 + 512]
                    scB = psB[:, pp * 512:pp * 512 + 512]
                    if it > 0 or kt >= 2:
                        pit, pkt = (it, kt - 2) if kt >= 2 else \
                                   (it - 1, NQT - 2 + kt)
                        T.wait_future("cACT", exp_tok(pit, pkt, 1))
                    T.mm(scA, k_sb[0:64, hp * N + kt * 128:hp * N + kt * 128 + 128],
                         q_sb[0:64, hp * N + qc * 512:hp * N + qc * 512 + 512],
                         start=True, stop=True)
                    qk_tok[(it, kt)] = T.mm(
                        scB, k_sb[64:128, hp * N + kt * 128:hp * N + kt * 128 + 128],
                        q_sb[64:128, hp * N + qc * 512:hp * N + qc * 512 + 512],
                        start=True, stop=True, mark=True)
                    if kt >= 1:
                        pvd(kt - 1)
                pvd(NQT - 1)
            blk.tensor(t_main)

            def v_mask(v, hp=hp, qc=qc, it=it):
                V = VW(ec, v)
                for kt in range(NQT):
                    ov = band_ovl(kt, qc)
                    if ov is None or no_mask:
                        continue
                    pp = kt % 2
                    scA = psA[:, pp * 512:pp * 512 + 512]
                    scB = psB[:, pp * 512:pp * 512 + 512]
                    c0, c1 = ov
                    lo, hi = c0 - qc * 512, c1 - qc * 512
                    t0 = c0 - (kt * 128 - 61)
                    V.wait("cPE", qk_tok[(it, kt)])
                    V.add(scA[:, lo:hi], scA[:, lo:hi],
                          maskl[:, t0:t0 + (hi - lo)])
                    mk_tok[(it, kt)] = V.add(scB[:, lo:hi], scB[:, lo:hi],
                                             maskl[:, t0:t0 + (hi - lo)])
            blk.vector(v_mask)

            def s_exp(s, hp=hp, qc=qc, it=it):
                S = SW(ec, s)
                for kt in range(NQT):
                    pp = kt % 2
                    scA = psA[:, pp * 512:pp * 512 + 512]
                    scB = psB[:, pp * 512:pp * 512 + 512]
                    eA = eAB[:, pp * 512:pp * 512 + 512]
                    eB = eAB[:, 1024 + pp * 512:1024 + pp * 512 + 512]
                    S.wait("cPE", qk_tok[(it, kt)])
                    if (it, kt) in mk_tok:
                        S.wait("cDVE", mk_tok[(it, kt)])
                    if it > 0 or kt >= 2:
                        pit, pkt = (it, kt - 2) if kt >= 2 else \
                                   (it - 1, NQT - 2 + kt)
                        S.wait("cPE", pv_tok[(pit, pkt)])
                    if no_exp:
                        t2 = S.copy(eA, scA)
                        t2 = S.copy(eB, scB)
                    else:
                        t2 = S.act(eA, scA, AF.Exp,
                                   bias=cneg_b[:, 2 * hp:2 * hp + 1],
                                   scale=0.125)
                        assert t2 == exp_tok(it, kt, 0)
                        t2 = S.act(eB, scB, AF.Exp,
                                   bias=cneg_b[:, 2 * hp + 1:2 * hp + 2],
                                   scale=0.125)
                        assert t2 == exp_tok(it, kt, 1)
            blk.scalar(s_exp)

            def v_rd(v, it=it):
                V = VW(ec, v)
                V.wait("cPE", pv_tok[(it, NQT - 1)])
                if no_rd:
                    rd_tok[it] = V.memset(rds_bf[0:1, 0:2048], 0.001)
                    return
                V.recip(rds[0:1, 0:512], psC[0:1, 512:1024])
                V.recip(rds[0:1, 512:1024], psE[0:1, 512:1024])
                V.copy(rds_bf[0:1, 0:512], rds[0:1, 0:512])
                V.sub(rds_bf[0:1, 512:1024], rds[0:1, 0:512],
                      rds_bf[0:1, 0:512])
                V.copy(rds_bf[0:1, 1024:1536], rds[0:1, 512:1024])
                rd_tok[it] = V.sub(rds_bf[0:1, 1536:2048],
                                   rds[0:1, 512:1024],
                                   rds_bf[0:1, 1024:1536])
            blk.vector(v_rd)

            def t_rdb(t, it=it):
                T = TW(ec, t)
                T.wait("cDVE", rd_tok[it])
                if it > 0:
                    T.wait("cACT", rdbe_tok(it - 1))
                psRD = psE[:, 0:512]
                if no_rdb:
                    rdb_tok[it] = ec.inc("cPE", T.r.nop())
                    return
                T.mm(psRD, hob[0:1, 0:128], rds_bf[0:1, 0:512],
                     start=True, stop=False)
                T.mm(psRD, hob[0:1, 0:128], rds_bf[0:1, 512:1024],
                     start=False, stop=False)
                T.mm(psRD, hob[0:1, 128:256], rds_bf[0:1, 1024:1536],
                     start=False, stop=False)
                rdb_tok[it] = T.mm(psRD, hob[0:1, 128:256],
                                   rds_bf[0:1, 1536:2048],
                                   start=False, stop=True, mark=True)
            blk.tensor(t_rdb)

            def s_rdb(s, it=it):
                S = SW(ec, s)
                S.wait("cPE", rdb_tok[it])
                t2 = S.copy(rdb[:, :], psE[:, 0:512])
                assert t2 == rdbe_tok(it), (t2, rdbe_tok(it))
            blk.scalar(s_rdb)

            def v_evac(v, hp=hp, qc=qc, it=it):
                V = VW(ec, v)
                V.wait("cACT", rdbe_tok(it))
                evac_tok[it] = V.tt(
                    sc32[:, hp * N + qc * 512:hp * N + qc * 512 + 512],
                    psC[:, 0:512], rdb[:, :], ALU.mult)
            blk.vector(v_evac)

        def g_wo(g):
            G = GW(ec, g)
            for dk in range(NDK):
                G.dma("ldwo", war[:, dk * 1024:(dk + 1) * 1024],
                      wo_h[dk * 128:(dk + 1) * 128, :])
        blk.gpsimd(g_wo)

        wo_e = {}
        wo_base = ec.cnt["cACT"]
        pe_wo0 = ec.cnt["cPE"]

        def t_wo(t):
            T = TW(ec, t)
            T.wait("ldwo")
            if nits:
                T.wait("cDVE", evac_tok[nits - 1])
            for idx in range(32):
                oc, ncc = idx // NC4, idx % NC4
                pp = idx % 2
                bank = psA[:, pp * 512:pp * 512 + 512]
                if idx >= 2:
                    T.wait_future("cACT", wo_base + idx - 1)
                for dk in range(NDK):
                    T.mm(bank,
                         war[:, dk * 1024 + oc * 128:dk * 1024 + oc * 128 + 128],
                         sc32[:, dk * N + ncc * 512:dk * N + ncc * 512 + 512],
                         start=(dk == 0), stop=(dk == NDK - 1),
                         mark=(dk == NDK - 1))
        blk.tensor(t_wo)

        def s_wo(s):
            S = SW(ec, s)
            for idx in range(32):
                oc, ncc = idx // NC4, idx % NC4
                pp = idx % 2
                bank = psA[:, pp * 512:pp * 512 + 512]
                S.wait("cPE", pe_wo0 + idx + 1)
                if idx >= 1:
                    S.wait_future("st0", 16 * idx)
                t2 = S.act(out_t[:, 0:512],
                           bank, AF.Identity, bias=bo4[:, oc:oc + 1])
                wo_e[idx] = t2
        blk.scalar(s_wo)

        def sy_st(s):
            for idx in range(32):
                oc, ncc = idx // NC4, idx % NC4
                ec.wait(s, "cACT", wo_e[idx])
                ec.inc("st0", s.dma_start(
                    out=ao_h[oc * 128:(oc + 1) * 128, ncc * 512:ncc * 512 + 512],
                    in_=out_t[:, 0:512]), 16)
            ec.wait(s, "st0")
        blk.sync(sy_st)

        ec.close()
    return nc



def _run(nc, in_maps, n):
    return run_bass_kernel_spmd(nc, in_maps, core_ids=list(range(n)))


NJP = 22
JW = [128] * 21 + [42]


def inner_consts():
    hc = host_consts()
    seln = np.zeros((128, 2), np.float32)
    seln[0:64, 0] = 1.0
    seln[64:128, 1] = 1.0
    seld = np.zeros((128, 4), np.float32)
    seld[:, 0] = 1.0
    seld[:, 3] = 1.0
    sel2 = np.zeros((2, 128), np.float32)
    sel2[0, 0:64] = 1.0
    sel2[1, 64:128] = 1.0
    rot4 = np.concatenate([hc["rot_cs"], hc["rot_ss"], hc["rot_ci"],
                           hc["rot_si"]], axis=1).astype(np.float32)
    return {
        "selN": seln.astype(nbf), "selD": seld.astype(nbf),
        "sel2": sel2.astype(nbf), "rot4": rot4,
        "m01": hc["mask01"], "idnf": hc["idn_f32"],
        "onesN": np.ones((1, N), np.float32).astype(nbf),
    }


def inner_inmap(i, z_fm, inp):
    ic = inner_consts()
    b1 = np.asarray(inp["L0_ffn_b1"][i], np.float32)
    b1a = np.zeros((128, NJP), np.float32)
    b1g = np.zeros((128, NJP), np.float32)
    for jp in range(NJP):
        w = JW[jp]
        b1a[0:w, jp] = b1[jp * 128:jp * 128 + w]
        b1g[0:w, jp] = b1[DI + jp * 128:DI + jp * 128 + w]
    gcol = percol(inp["L0_ffn_ln_g"][i])
    bcol = percol(inp["L0_ffn_ln_b"][i])
    gb = np.zeros((128, 2 * NDK), np.float32)
    gb[:, 0::2] = gcol
    gb[:, 1::2] = bcol
    return {
        "z": np.ascontiguousarray(z_fm, np.float32),
        "wqkv": np.ascontiguousarray(inp["L0_ffn_Wqkv"][i], np.float32),
        "wo": np.ascontiguousarray(inp["L0_ffn_Wo"][i], np.float32),
        "qs": np.tile(np.asarray(inp["L0_ffn_q_scale"][i], np.float32),
                      2).reshape(128, 1),
        "ks": np.tile(np.asarray(inp["L0_ffn_k_scale"][i], np.float32),
                      2).reshape(128, 1),
        "g8": gcol, "gb": gb,
        "w1": np.ascontiguousarray(inp["L0_ffn_W1"][i], np.float32),
        "b1a": b1a, "b1g": b1g,
        "w2": np.ascontiguousarray(inp["L0_ffn_W2"][i], np.float32),
        "b28": percol(inp["L0_ffn_b2"][i]),
        "rot4": ic["rot4"], "m01": ic["m01"], "sel2": ic["sel2"],
        "selD": ic["selD"], "selN": ic["selN"], "idnf": ic["idnf"],
        "onesN": ic["onesN"],
    }


def build_inner(dump=None):
    nc = bass.Bass("TRN2", target_bir_lowering=False)
    if dump:
        dbg_h = nc.dram_tensor("dbg", [128, 16384], BF16,
                               kind="ExternalOutput").ap()
    z_h = nc.dram_tensor("z", [D, N], F32, kind="ExternalInput").ap()
    wqkv_h = nc.dram_tensor("wqkv", [D, 3 * D], F32, kind="ExternalInput").ap()
    wo_h = nc.dram_tensor("wo", [D, D], F32, kind="ExternalInput").ap()
    qs_h = nc.dram_tensor("qs", [128, 1], F32, kind="ExternalInput").ap()
    ks_h = nc.dram_tensor("ks", [128, 1], F32, kind="ExternalInput").ap()
    g8_h = nc.dram_tensor("g8", [128, NDK], F32, kind="ExternalInput").ap()
    gb_h = nc.dram_tensor("gb", [128, 2 * NDK], F32, kind="ExternalInput").ap()
    w1_h = nc.dram_tensor("w1", [D, W1C], F32, kind="ExternalInput").ap()
    b1a_h = nc.dram_tensor("b1a", [128, NJP], F32, kind="ExternalInput").ap()
    b1g_h = nc.dram_tensor("b1g", [128, NJP], F32, kind="ExternalInput").ap()
    w2_h = nc.dram_tensor("w2", [DI, D], F32, kind="ExternalInput").ap()
    b28_h = nc.dram_tensor("b28", [128, NDK], F32, kind="ExternalInput").ap()
    rot_h = nc.dram_tensor("rot4", [128, 4 * N], F32, kind="ExternalInput").ap()
    m01_h = nc.dram_tensor("m01", [128, 384], BF16, kind="ExternalInput").ap()
    s2_h = nc.dram_tensor("sel2", [2, 128], BF16, kind="ExternalInput").ap()
    sD_h = nc.dram_tensor("selD", [128, 4], BF16, kind="ExternalInput").ap()
    sN_h = nc.dram_tensor("selN", [128, 2], BF16, kind="ExternalInput").ap()
    idf_h = nc.dram_tensor("idnf", [128, 128], F32, kind="ExternalInput").ap()
    on_h = nc.dram_tensor("onesN", [1, N], BF16, kind="ExternalInput").ap()
    zo_h = nc.dram_tensor("zo", [D, N], F32, kind="ExternalOutput").ap()

    st_ = ExitStack()
    with st_:
        _sb = [
            ("z_s", (128, NDK * 2049), F32),
            ("arA", (128, NDK * N), BF16),
            ("arK", (128, NDK * N), BF16),
            ("arV", (128, 16384), BF16),
            ("arW", (128, 8192), BF16),
            ("zbt", (128, NDK * 513), BF16),
            ("q2", (128, N), BF16),
            ("rows3", (2, 3 * N), BF16),
            ("mcols", (128, 32), F32),
            ("msb", (32, 128), BF16),
            ("stb", (2, 256), BF16),
            ("rdb", (128, 512), BF16),
            ("eT", (128, 768), BF16),
            ("m01_s", (128, 384), BF16),
            ("sel2_s", (2, 128), BF16),
            ("selD_s", (128, 4), BF16),
            ("selN_s", (128, 2), BF16),
            ("idnf_s", (128, 128), F32),
            ("qs_c", (128, 1), F32),
            ("ks_c", (128, 1), F32),
            ("g8_s", (128, NDK), F32),
            ("gb_s", (128, 2 * NDK), BF16),
            ("b1a_s", (128, NJP), F32),
            ("b1g_s", (128, NJP), F32),
            ("b28_s", (128, NDK), F32),
            ("ones_c", (128, 1), BF16),
            ("ones_r", (1, 128), BF16),
            ("zlead", (128, NDK * 4), F32),
        ]
        _ps = [("psX", (128, 1024)), ("psCtx", (128, 2048)),
               ("psD2", (128, 1024))]
        _t = _alloc(nc, st_, _sb, _ps)
        z_s = _t["z_s"]; arA = _t["arA"]; arK = _t["arK"]
        arV = _t["arV"]; arW = _t["arW"]; zbt = _t["zbt"]; q2 = _t["q2"]
        rows3 = _t["rows3"]; mcols = _t["mcols"]; msb = _t["msb"]
        stb = _t["stb"]; rdb = _t["rdb"]; eT = _t["eT"]
        m01 = _t["m01_s"]; sel2 = _t["sel2_s"]; selD = _t["selD_s"]
        selN = _t["selN_s"]; idnf = _t["idnf_s"]; qs_c = _t["qs_c"]
        ks_c = _t["ks_c"]; g8 = _t["g8_s"]; gb = _t["gb_s"]; b1a = _t["b1a_s"]
        b1g = _t["b1g_s"]; b28 = _t["b28_s"]; ones_c = _t["ones_c"]
        ones_r = _t["ones_r"]; zlead = _t["zlead"]
        psX = _t["psX"]; psCtx = _t["psCtx"]; psD2 = _t["psD2"]
        blk = st_.enter_context(nc.Block())

        laug = rows3[0:2, 0:N]

        ec = EC(nc, blk)
        for sname in ("ldz", "ldc", "ldk", "ldw", "ldwA", "ldwB", "ldt",
                      "lds", "ldw1A", "ldw1B", "ldw2", "ldwoA", "ldwoB",
                      "st0"):
            ec.sem(sname)
        tok = {}

        def emit_dump(buf, sem_name, val):
            def sy_d(s):
                ec.wait(s, sem_name, val)
                ec.inc("st0", s.dma_start(out=dbg_h[:, :], in_=buf[:, :]), 16)
                ec.wait(s, "st0")
            blk.sync(sy_d)

        def zf(dk, c0, c1):
            return z_s[:, dk * 2049 + 1 + c0:dk * 2049 + 1 + c1]

        def zbs(dk, shifted, c0=0, c1=512):
            off = 0 if (shifted and dk >= 4) else 1
            return zbt[:, dk * 513 + off + c0:dk * 513 + off + c1]

        def build_zbt(G, ncc, key, lead=False):
            for dk in range(NDK):
                if lead and ncc > 0:
                    G.copy(zbt[:, dk * 513:dk * 513 + 1],
                           zlead[:, dk * 4 + ncc:dk * 4 + ncc + 1])
                    t2 = G.copy(zbt[:, dk * 513 + 1:dk * 513 + 513],
                                z_s[:, dk * 2049 + ncc * 512 + 1:
                                    dk * 2049 + ncc * 512 + 513])
                else:
                    t2 = G.copy(zbt[:, dk * 513:dk * 513 + 513],
                                z_s[:, dk * 2049 + ncc * 512:
                                    dk * 2049 + ncc * 512 + 513])
            tok[key] = t2

        def sy_ld(s):
            for dk in range(NDK):
                ec.inc("ldz", s.dma_start(
                    out=zf(dk, 0, N), in_=z_h[dk * 128:(dk + 1) * 128, :]), 16)
            for (dst, src) in ((qs_c, qs_h), (ks_c, ks_h), (g8, g8_h),
                               (b1a, b1a_h), (b1g, b1g_h),
                               (b28, b28_h), (idnf, idf_h)):
                ec.inc("ldc", s.dma_start(out=dst[:, :], in_=src[:, :]), 16)
        blk.sync(sy_ld)

        def g_ld(g):
            G = GW(ec, g)
            G.memset(ones_c[:, :], 1.0)
            G.memset(ones_r[:, :], 1.0)
            for dk in range(NDK):
                G.memset(z_s[:, dk * 2049:dk * 2049 + 1], 0.0)
            G.dma("ldk", m01[:, :], m01_h[:, :])
            G.dma("ldk", sel2[:, :], s2_h[:, :])
            G.dma("ldk", selD[:, :], sD_h[:, :])
            G.dma("ldk", selN[:, :], sN_h[:, :])
            G.dma("ldk", gb[:, :], gb_h[:, :])
            tok["zzero"] = ec.cnt["cPL"]
        blk.gpsimd(g_ld)

        qke0 = ec.cnt["cACT"]
        qkp0 = ec.cnt["cPE"]
        for ncc in range(NC4):
            def g_zb(g, ncc=ncc):
                G = GW(ec, g)
                if ncc == 0:
                    G.wait("ldz")
                    G.wait("cPL", tok["zzero"])
                else:
                    G.wait("cPE", qkp0 + ncc * 16)
                build_zbt(G, ncc, ("zbt1", ncc))
            blk.gpsimd(g_zb)

            for idx in range(16):
                ti, oc = idx // 8, idx % 8
                dst = arA if ti == 0 else arK
                gidx = ncc * 16 + idx
                buf = arW[:, (gidx % 2) * 1024:(gidx % 2) * 1024 + 1024]

                def g_w(g, ti=ti, oc=oc, gidx=gidx, buf=buf):
                    G = GW(ec, g)
                    if gidx >= 2:
                        G.wait_future("cPE", qkp0 + gidx - 1)
                    tok[("wld", gidx)] = G.dma(
                        "ldwA" if gidx % 2 == 0 else "ldwB", buf,
                        bass.AP(wqkv_h.tensor, ti * D + oc * 128,
                                [[3 * D, 128], [128 * 3 * D, NDK],
                                 [1, 128]]))
                blk.gpsimd(g_w)

                def t_p(t, ncc=ncc, gidx=gidx, buf=buf):
                    T = TW(ec, t)
                    T.wait("ldwA" if gidx % 2 == 0 else "ldwB",
                           tok[("wld", gidx)])
                    T.wait("cPL", tok[("zbt1", ncc)])
                    bank = psX[:, (gidx % 2) * 512:(gidx % 2) * 512 + 512]
                    if gidx >= 2:
                        T.wait_future("cACT", qke0 + gidx - 1)
                    for dk in range(NDK):
                        t2 = T.mm(bank, buf[:, dk * 128:dk * 128 + 128],
                                  zbs(dk, shifted=True),
                                  start=(dk == 0), stop=(dk == NDK - 1),
                                  mark=(dk == NDK - 1))
                    assert t2 == qkp0 + gidx + 1
                blk.tensor(t_p)

                def s_e(s, ncc=ncc, ti=ti, oc=oc, gidx=gidx, dst=dst):
                    S = SW(ec, s)
                    bank = psX[:, (gidx % 2) * 512:(gidx % 2) * 512 + 512]
                    S.wait("cPE", qkp0 + gidx + 1)
                    t2 = S.copy(
                        dst[:, oc * N + ncc * 512:oc * N + ncc * 512 + 512],
                        bank)
                    assert t2 == qke0 + gidx + 1
                blk.scalar(s_e)

        vml0 = ec.cnt["ldw"]

        def g_vw(g):
            G = GW(ec, g)
            G.wait_future("cPE", qkp0 + 64)
            for vc in range(2):
                G.dma("ldw", arW[:, vc * 4096:vc * 4096 + 4096],
                      bass.AP(wqkv_h.tensor, 2 * D + vc * 512,
                              [[3 * D, 128], [128 * 3 * D, NDK], [1, 512]]))
        blk.gpsimd(g_vw)

        vp0 = ec.cnt["cPE"]
        ve0 = ec.cnt["cACT"]
        for ncc in range(NC4):
            def g_zb2(g, ncc=ncc):
                G = GW(ec, g)
                G.wait("cPE", qkp0 + 64 if ncc == 0 else vp0 + ncc * 8)
                build_zbt(G, ncc, ("zbt2", ncc))
            blk.gpsimd(g_zb2)

            for li in range(8):
                nt = ncc * 4 + li // 2
                vc = li % 2
                vidx = ncc * 8 + li

                def t_v(t, ncc=ncc, nt=nt, vc=vc, vidx=vidx):
                    T = TW(ec, t)
                    if vidx == 0:
                        T.wait("ldw", vml0 + 32)
                    T.wait("cPL", tok[("zbt2", ncc)])
                    bank = psX[:, (vidx % 2) * 512:(vidx % 2) * 512 + 512]
                    if vidx >= 2:
                        T.wait_future("cACT", ve0 + vidx - 1)
                    else:
                        T.wait("cACT", qke0 + 63 + vidx)
                    loc = (nt % 4) * 128
                    for dk in range(NDK):
                        t2 = T.mm(bank,
                                  zbs(dk, True, loc, loc + 128),
                                  arW[:, vc * 4096 + dk * 512:
                                      vc * 4096 + dk * 512 + 512],
                                  start=(dk == 0), stop=(dk == NDK - 1),
                                  mark=(dk == NDK - 1))
                    assert t2 == vp0 + vidx + 1
                blk.tensor(t_v)

                def s_v(s, nt=nt, vc=vc, vidx=vidx):
                    S = SW(ec, s)
                    bank = psX[:, (vidx % 2) * 512:(vidx % 2) * 512 + 512]
                    S.wait("cPE", vp0 + vidx + 1)
                    t2 = S.copy(arV[:, nt * 1024 + vc * 512:
                                    nt * 1024 + vc * 512 + 512], bank)
                    assert t2 == ve0 + vidx + 1
                blk.scalar(s_v)

        def g_tab(g):
            G = GW(ec, g)
            G.wait_future("cPE", vp0 + 32)
            G.dma("ldt", arW[:, 0:N], rot_h[:, 0:N])
            G.dma("ldt", arW[:, N:2 * N], rot_h[:, N:2 * N])
        blk.gpsimd(g_tab)

        for ti in range(2):
            src = arA if ti == 0 else arK
            sc_c = qs_c if ti == 0 else ks_c
            if ti == 1:
                def g_tab2(g):
                    G = GW(ec, g)
                    G.wait("cDVE", tok[("rot", 0, NDK - 1)])
                    G.dma("ldt", arW[:, 0:N], rot_h[:, 2 * N:3 * N])
                    G.dma("ldt", arW[:, N:2 * N], rot_h[:, 3 * N:4 * N])
                blk.gpsimd(g_tab2)

            for dk in range(NDK):
                def g_s(g, ti=ti, dk=dk, src=src):
                    G = GW(ec, g)
                    if ti == 0 and dk == 0:
                        G.wait_future("cACT", qke0 + 64)
                    else:
                        G.wait("cDVE", tok[("rot", ti, dk - 1)] if dk else
                               tok[("rot", ti - 1, NDK - 1)])
                    x = src[:, dk * N:(dk + 1) * N]
                    tok[("sq", ti, dk)] = G.tt(q2[:, :], x, x, ALU.mult)
                blk.gpsimd(g_s)

                def t_n(t, ti=ti, dk=dk):
                    T = TW(ec, t)
                    T.wait("cPL", tok[("sq", ti, dk)])
                    if (ti, dk) == (0, 0):
                        T.wait("ldk")
                    else:
                        T.wait("cDVE", tok[("rr", ti, dk - 1)] if dk else
                               tok[("rr", ti - 1, NDK - 1)])
                    for ncc in range(NC4):
                        t2 = T.mm(psCtx[0:2, ncc * 512:ncc * 512 + 512],
                                  selN[:, :], q2[:, ncc * 512:ncc * 512 + 512],
                                  start=True, stop=True,
                                  mark=(ncc == NC4 - 1))
                    tok[("nmm", ti, dk)] = t2
                blk.tensor(t_n)

                def s_n(s, ti=ti, dk=dk):
                    S = SW(ec, s)
                    S.wait("cPE", tok[("nmm", ti, dk)])
                    tok[("nsq", ti, dk)] = S.act(psCtx[0:2, 0:2048],
                                                 psCtx[0:2, 0:2048], AF.Sqrt)
                blk.scalar(s_n)

                def v_n(v, ti=ti, dk=dk):
                    V = VW(ec, v)
                    V.wait("cACT", tok[("nsq", ti, dk)])
                    V.ts(psCtx[0:2, 0:2048], psCtx[0:2, 0:2048], 1e-12,
                         op0=ALU.max)
                    V.recip(psCtx[0:2, 0:2048], psCtx[0:2, 0:2048])
                    tok[("rr", ti, dk)] = V.copy(rows3[0:2, 0:N],
                                                 psCtx[0:2, 0:2048])
                blk.vector(v_n)

                ub = [None]

                def t_b(t, ti=ti, dk=dk, ub=ub):
                    T = TW(ec, t)
                    T.wait("cDVE", tok[("rr", ti, dk)])
                    ub[0] = ec.cnt["cDVE"]
                    for ncc in range(NC4):
                        if ncc >= 2:
                            T.wait_future("cDVE", ub[0] + ncc - 1)
                        tok[("rb", ti, dk, ncc)] = T.mm(
                            psD2[:, (ncc % 2) * 512:(ncc % 2) * 512 + 512],
                            sel2[:, :],
                            rows3[0:2, ncc * 512:ncc * 512 + 512],
                            start=True, stop=True, mark=True)
                blk.tensor(t_b)

                def v_u(v, ti=ti, dk=dk, src=src, sc_c=sc_c, ub=ub):
                    V = VW(ec, v)
                    if ti == 0 and dk == 0:
                        V.wait("ldc")
                    t2 = None
                    for ncc in range(NC4):
                        V.wait("cPE", tok[("rb", ti, dk, ncc)])
                        sl = src[:, dk * N + ncc * 512:dk * N + ncc * 512 + 512]
                        t2 = V.stt(sl, sl, sc_c[:, 0:1],
                                   psD2[:, (ncc % 2) * 512:(ncc % 2) * 512 + 512],
                                   ALU.mult, ALU.mult)
                        assert t2 == ub[0] + ncc + 1
                    tok[("u", ti, dk)] = t2
                blk.vector(v_u)

                def g_sg(g, ti=ti, dk=dk, src=src):
                    G = GW(ec, g)
                    G.wait("cDVE", tok[("u", ti, dk)])
                    ch0 = dk * N
                    for (a, b) in ((0, 32), (32, 0), (64, 96), (96, 64)):
                        G.dma("lds", q2[a:a + 32, :],
                              src[b:b + 32, ch0:ch0 + N])
                    tok[("sg", ti, dk)] = ec.cnt["lds"]
                blk.gpsimd(g_sg)

                def v_r1(v, ti=ti, dk=dk, src=src):
                    V = VW(ec, v)
                    if dk == 0:
                        V.wait("ldt", 32 * (ti + 1))
                    V.wait("lds", tok[("sg", ti, dk)])
                    ch = src[:, dk * N:(dk + 1) * N]
                    tok[("t1", ti, dk)] = V.tt(ch, ch, arW[:, 0:N], ALU.mult)
                blk.vector(v_r1)

                def g_r2(g, ti=ti, dk=dk):
                    G = GW(ec, g)
                    if dk == 0:
                        G.wait("ldt", 32 * (ti + 1))
                    G.wait("lds", tok[("sg", ti, dk)])
                    tok[("t2", ti, dk)] = G.tt(q2[:, :], q2[:, :],
                                               arW[:, N:2 * N], ALU.mult)
                blk.gpsimd(g_r2)

                def v_r3(v, ti=ti, dk=dk, src=src):
                    V = VW(ec, v)
                    V.wait("cPL", tok[("t2", ti, dk)])
                    ch = src[:, dk * N:(dk + 1) * N]
                    tok[("rot", ti, dk)] = V.add(ch, ch, q2[:, :])
                blk.vector(v_r3)

        if dump in ("q3", "k3", "v2"):
            src = {"q3": arA, "k3": arK, "v2": arV}[dump]
            emit_dump(src, "cDVE", tok[("rot", 1, NDK - 1)])
            ec.close()
            return nc

        def p1win(qt):
            return max(0, qt * 128 - 256), qt * 128 + 128

        def p2win(kt):
            return kt * 128, min(N, kt * 128 + 384)

        segs = {}
        seen_b = set()
        for kt in range(NQT):
            lo, hi = p2win(kt)
            brk = sorted(set([lo, hi, min(lo + 128, hi), min(lo + 256, hi)] +
                             [c for c in range(512, N, 512) if lo < c < hi]))
            ps = []
            for a, b in zip(brk[:-1], brk[1:]):
                qc = a // 512
                st = qc not in seen_b
                seen_b.add(qc)
                sp = (kt == NQT - 1) or (b <= lo + 128)
                ps.append((a, b, st, sp))
            segs[kt] = ps

        for hp in range(8):
            tok[("mx_b", hp)] = ec.cnt["cDVE"]

            def t_p1(t, hp=hp):
                T = TW(ec, t)
                if hp == 0:
                    T.wait("cDVE", tok[("rot", 1, NDK - 1)])
                    T.wait("cACT", ec.cnt["cACT"])
                else:
                    T.wait("cDVE", tok[("aoev", hp - 1, 3)])
                for qt in range(NQT):
                    lo, hi = p1win(qt)
                    w = hi - lo
                    ps = psX if qt % 2 == 0 else psD2
                    if qt >= 2:
                        T.wait_future("cDVE", tok[("mx_b", hp)] + qt - 1)
                    for j in range(2):
                        rb = j * 64
                        t2 = T.mm(ps[:, j * 512:j * 512 + w],
                                  arA[rb:rb + 64,
                                      hp * N + qt * 128:hp * N + qt * 128 + 128],
                                  arK[rb:rb + 64, hp * N + lo:hp * N + hi],
                                  start=True, stop=True, mark=(j == 1))
                    tok[("p1", hp, qt)] = t2
            blk.tensor(t_p1)

            def v_mx(v, hp=hp):
                V = VW(ec, v)
                for qt in range(NQT):
                    lo, hi = p1win(qt)
                    w = hi - lo
                    ps = psX if qt % 2 == 0 else psD2
                    V.wait("cPE", tok[("p1", hp, qt)])
                    t2 = V.red(ap3(mcols.ap(), qt, [[32, 128], [16, 2]]),
                               ap3(ps[:, :], 0,
                                   [[1024, 128], [512, 2], [1, w]]),
                               op=ALU.max, negate=True)
                    assert t2 == tok[("mx_b", hp)] + qt + 1
            blk.vector(v_mx)

            def t_mt(t, hp=hp):
                T = TW(ec, t)
                T.wait("cDVE", tok[("mx_b", hp)] + NQT)
                T.wait("ldc")
                tok[("mt", hp)] = T.trans(psD2[0:32, 0:128], mcols[:, :],
                                          idnf[:, :], mark=True)
            blk.tensor(t_mt)

            def v_mse(v, hp=hp):
                V = VW(ec, v)
                V.wait("cPE", tok[("mt", hp)])
                tok[("mse", hp)] = V.copy(msb[:, :], psD2[0:32, 0:128])
            blk.vector(v_mse)

            def g_mf(g, hp=hp):
                G = GW(ec, g)
                G.wait("cDVE", tok[("mse", hp)])
                for j in range(2):
                    G.dma("lds", rows3[0:1, (j + 1) * N:(j + 2) * N],
                          msb[j * 16:j * 16 + 16, :])
                tok[("mf", hp)] = ec.cnt["lds"]
            blk.gpsimd(g_mf)

            tok[("msk_b", hp)] = ec.cnt["cPL"]
            tok[("exp_b", hp)] = ec.cnt["cACT"]

            if dump == "e4" and hp == 0:
                def sy_e4(s):
                    ec.wait_future(s, "cPL", tok[("msk_b", 0)] + 2 * 4 + 2)
                    ec.inc("st0", s.dma_start(out=dbg_h[:, 0:768],
                                              in_=eT[:, :]), 16)
                    ec.wait(s, "st0")
                blk.sync(sy_e4)

            def t_p2(t, hp=hp):
                T = TW(ec, t)

                def emit_pvd(kt):
                    lo, hi = p2win(kt)
                    T.wait_future("cPL", tok[("msk_b", hp)] + 2 * kt + 2)
                    np_ = len(segs[kt])
                    for pi, (c0, c1, st, sp) in enumerate(segs[kt]):
                        qc = c0 // 512
                        bk, rb = qc // 2, (qc % 2) * 32
                        for j in range(2):
                            h = 2 * hp + j
                            T.mm(psCtx[j * 64:j * 64 + 64, c0:c1],
                                 arV[:, kt * 1024 + h * 64:
                                     kt * 1024 + h * 64 + 64],
                                 eT[:, j * 384 + (c0 - lo):
                                    j * 384 + (c1 - lo)],
                                 start=st, stop=sp, skip=True)
                            is_pvd = (j == 1 and sp and kt == 4 * qc + 3)
                            is_lst = (j == 1 and pi == np_ - 1)
                            t2 = T.mm(
                                psD2[rb:rb + 2,
                                     bk * 512 + (c0 - qc * 512):
                                     bk * 512 + (c1 - qc * 512)],
                                selD[:, j * 2:j * 2 + 2],
                                eT[:, j * 384 + (c0 - lo):
                                   j * 384 + (c1 - lo)],
                                start=(st and j == 0), stop=(sp and j == 1),
                                skip=True, mark=(is_pvd or is_lst))
                            if is_pvd:
                                tok[("pvd", hp, qc)] = t2
                            if is_lst:
                                tok[("pvk", hp, kt)] = t2

                T.wait("lds", tok[("mf", hp)])
                for kt in range(NQT):
                    lo, hi = p2win(kt)
                    w = hi - lo
                    for j in range(2):
                        rb = j * 64
                        mrow = rows3[0:1, (j + 1) * N + lo:(j + 1) * N + hi]
                        T.mm(psX[:, j * 512:j * 512 + w],
                             arK[rb:rb + 64,
                                 hp * N + kt * 128:hp * N + kt * 128 + 128],
                             arA[rb:rb + 64, hp * N + lo:hp * N + hi],
                             start=True, stop=False)
                        tok[("p2", hp, kt, j)] = T.mm(
                            psX[:, j * 512:j * 512 + w], ones_r[:, :], mrow,
                            start=False, stop=True, mark=True)
                    emit_pvd(kt)
            blk.tensor(t_p2)

            def s_ex(s, hp=hp):
                S = SW(ec, s)
                for kt in range(NQT):
                    lo, hi = p2win(kt)
                    w = hi - lo
                    for j in range(2):
                        S.wait("cPE", tok[("p2", hp, kt, j)])
                        if kt >= 1 and j == 0:
                            S.wait("cPE", tok[("pvk", hp, kt - 1)])
                        if dump == "e4" and hp == 0 and kt == 5 and j == 0:
                            S.wait("st0", 16)
                        t2 = S.act(eT[:, j * 384:j * 384 + w],
                                   psX[:, j * 512:j * 512 + w], AF.Exp,
                                   scale=8.0)
                        assert t2 == tok[("exp_b", hp)] + 2 * kt + j + 1
            blk.scalar(s_ex)

            def g_mk(g, hp=hp):
                G = GW(ec, g)
                for kt in range(NQT):
                    lo, hi = p2win(kt)
                    w = hi - lo
                    for j in range(2):
                        G.wait("cACT", tok[("exp_b", hp)] + 2 * kt + j + 1)
                        t2 = G.tt(eT[:, j * 384:j * 384 + w],
                                  eT[:, j * 384:j * 384 + w],
                                  m01[:, 0:w], ALU.mult)
                        assert t2 == tok[("msk_b", hp)] + 2 * kt + j + 1
            blk.gpsimd(g_mk)

            def v_rd(v, hp=hp):
                V = VW(ec, v)
                for qc in range(4):
                    bk, rb = qc // 2, (qc % 2) * 32
                    V.wait("cPE", tok[("pvd", hp, qc)])
                    V.recip(psD2[rb:rb + 2, bk * 512:bk * 512 + 512],
                            psD2[rb:rb + 2, bk * 512:bk * 512 + 512])
                    tok[("rdr", hp, qc)] = V.copy(
                        rows3[0:2, qc * 512:qc * 512 + 512],
                        psD2[rb:rb + 2, bk * 512:bk * 512 + 512])
            blk.vector(v_rd)

            def t_rb(t, hp=hp):
                T = TW(ec, t)
                T.wait("cACT", tok[("exp_b", hp)] + 2 * NQT)
                for qc in range(4):
                    T.wait("cDVE", tok[("rdr", hp, qc)])
                    if qc > 0:
                        T.wait_future("cACT", tok[("rbe_b", hp)] + qc)
                    tok[("rbm", hp, qc)] = T.mm(
                        psX[:, 0:512], sel2[:, :],
                        rows3[0:2, qc * 512:qc * 512 + 512],
                        start=True, stop=True, mark=True)
            tok[("rbe_b", hp)] = ec.cnt["cACT"]
            blk.tensor(t_rb)
            tok[("aoev_b", hp)] = ec.cnt["cDVE"]

            def s_rb(s, hp=hp):
                S = SW(ec, s)
                for qc in range(4):
                    S.wait("cPE", tok[("rbm", hp, qc)])
                    t2 = S.copy(rdb[:, :], psX[:, 0:512])
                    assert t2 == tok[("rbe_b", hp)] + qc + 1
                    S.wait_future("cDVE", tok[("aoev_b", hp)] + qc + 1)
            blk.scalar(s_rb)

            def v_ao(v, hp=hp):
                V = VW(ec, v)
                for qc in range(4):
                    V.wait("cACT", tok[("rbe_b", hp)] + qc + 1)
                    t2 = V.tt(arA[:, hp * N + qc * 512:hp * N + qc * 512 + 512],
                              psCtx[:, qc * 512:qc * 512 + 512],
                              rdb[:, :], ALU.mult)
                    tok[("aoev", hp, qc)] = t2
                    assert t2 == tok[("aoev_b", hp)] + qc + 1
            blk.vector(v_ao)

        if dump == "ao":
            emit_dump(arA, "cDVE", tok[("aoev", 7, 3)])
            ec.close()
            return nc
        if dump == "rd7":
            def sy_d7(s):
                ec.wait(s, "cDVE", tok[("aoev", 7, 3)])
                ec.inc("st0", s.dma_start(out=dbg_h[0:2, 0:N],
                                          in_=rows3[0:2, 0:N]), 16)
                ec.inc("st0", s.dma_start(out=dbg_h[0:1, N:3 * N],
                                          in_=rows3[0:1, N:3 * N]), 16)
                ec.wait(s, "st0")
            blk.sync(sy_d7)
            ec.close()
            return nc

        wo_p0 = ec.cnt["cPE"]
        wo_z0 = ec.cnt["cDVE"]
        for oc in range(NDK):
            buf = arW[:, (oc % 2) * 1024:(oc % 2) * 1024 + 1024]

            def g_wo(g, oc=oc, buf=buf):
                G = GW(ec, g)
                if oc == 0:
                    G.wait("cDVE", tok[("rot", 1, NDK - 1)])
                if oc >= 2:
                    G.wait_future("cPE", wo_p0 + (oc - 1) * NC4)
                tok[("wol", oc)] = G.dma(
                    "ldwoA" if oc % 2 == 0 else "ldwoB", buf,
                    bass.AP(wo_h.tensor, oc * 128,
                            [[D, 128], [128 * D, NDK], [1, 128]]))
            blk.gpsimd(g_wo)

            def t_wo(t, oc=oc, buf=buf):
                T = TW(ec, t)
                T.wait("ldwoA" if oc % 2 == 0 else "ldwoB", tok[("wol", oc)])
                if oc == 0:
                    T.wait("cDVE", tok[("aoev", 7, 3)])
                for ncc in range(NC4):
                    gidx = oc * NC4 + ncc
                    bank = psX[:, (gidx % 2) * 512:(gidx % 2) * 512 + 512]
                    if gidx >= 2:
                        T.wait_future("cDVE", wo_z0 + gidx - 1)
                    for dk in range(NDK):
                        t2 = T.mm(bank, buf[:, dk * 128:dk * 128 + 128],
                                  arA[:, dk * N + ncc * 512:
                                      dk * N + ncc * 512 + 512],
                                  start=(dk == 0), stop=(dk == NDK - 1),
                                  mark=(dk == NDK - 1))
                    assert t2 == wo_p0 + gidx + 1
            blk.tensor(t_wo)

            def v_zu(v, oc=oc):
                V = VW(ec, v)
                for ncc in range(NC4):
                    gidx = oc * NC4 + ncc
                    bank = psX[:, (gidx % 2) * 512:(gidx % 2) * 512 + 512]
                    V.wait("cPE", wo_p0 + gidx + 1)
                    if gidx == 0:
                        V.wait("ldz")
                    sl = zf(oc, ncc * 512, ncc * 512 + 512)
                    t2 = V.stt(sl, bank, 1.0, sl, ALU.mult, ALU.add)
                    assert t2 == wo_z0 + gidx + 1
            blk.vector(v_zu)

        if dump == "z5":
            def sy_z5(s):
                ec.wait(s, "cDVE", wo_z0 + 32)
                for oc in range(NDK):
                    ec.inc("st0", s.dma_start(
                        out=zo_h[oc * 128:(oc + 1) * 128, :],
                        in_=zf(oc, 0, N)), 16)
                ec.wait(s, "st0")
            blk.sync(sy_z5)
            ec.close()
            return nc

        mu_b = arA[:, 0:4096].bitcast(F32)
        std_b = arA[:, 4096:8192].bitcast(F32)
        rstd_b = arA[:, 8192:12288].bitcast(F32)
        tmp_b = arA[:, 12288:16384].bitcast(F32)

        def g_zcap(g):
            G = GW(ec, g)
            G.wait("cDVE", ec.cnt["cDVE"])
            for dk in range(NDK):
                G.copy(ap3(zlead.ap(), dk * 4 + 1, [[NDK * 4, 128], [1, 3]]),
                       ap3(z_s.ap(), dk * 2049 + 512,
                           [[NDK * 2049, 128], [512, 3]]))
        blk.gpsimd(g_zcap)

        ln_sq0 = ec.cnt["cPL"]
        ln_mm0 = [None]

        def ln_mm_tok(ncc, dk):
            if dk < 0:
                if ncc == 0:
                    return 0
                ncc, dk = ncc - 1, dk + NDK
            return ln_mm0[0] + ncc * NDK + dk + 1

        def g_lnsq(g):
            G = GW(ec, g)
            G.wait("cDVE", ec.cnt["cDVE"])
            G.wait("cPE", wo_p0 + 32)
            for ncc in range(NC4):
                if ncc >= 1:
                    G.wait_future("cPE", ln_mm_tok(ncc - 1, NDK - 1))
                build_zbt(G, ncc, ("zbt3", ncc))
                for dk in range(NDK):
                    G.wait_future("cPE", ln_mm_tok(ncc, dk - 2))
                    G.tt(q2[:, (dk % 2) * 512:(dk % 2) * 512 + 512],
                         zbs(dk, True), zbs(dk, True), ALU.mult)
        ln_mm0[0] = ec.cnt["cPE"]
        blk.gpsimd(g_lnsq)

        def t_lnst(t):
            T = TW(ec, t)
            for ncc in range(NC4):
                sl = slice(ncc * 512, ncc * 512 + 512)
                for dk in range(NDK):
                    T.wait("cPL", ln_sq0 + ncc * 16 + NDK + dk + 1)
                    T.mm(psCtx[0:1, sl], ones_c[:, :], zbs(dk, True),
                         start=(dk == 0), stop=(dk == NDK - 1), skip=True)
                    t2 = T.mm(psCtx[32:33, sl], ones_c[:, :],
                              q2[:, (dk % 2) * 512:(dk % 2) * 512 + 512],
                              start=(dk == 0), stop=(dk == NDK - 1),
                              skip=True, mark=True)
                    assert t2 == ln_mm_tok(ncc, dk)
        blk.tensor(t_lnst)

        sums_bf = q2[0:1, 0:2048]
        hl_tok = {}
        bc0 = ec.cnt["cPE"]
        be0 = ec.cnt["cACT"]
        lo0 = ec.cnt["cDVE"]

        def s_hilo(s):
            S = SW(ec, s)
            for ncc in range(NC4):
                sl = slice(ncc * 512, ncc * 512 + 512)
                if ncc == 0:
                    S.wait("cPE", ln_mm_tok(NC4 - 1, NDK - 1))
                S.copy(sums_bf[0:1, 0:512], psCtx[0:1, sl])
                hl_tok[ncc] = S.copy(sums_bf[0:1, 1024:1536],
                                     psCtx[32:33, sl])
                S.wait_future("cPE", bc0 + ncc + 1)
                S.copy(mu_b[:, sl], psX[:, 0:512])
                t2 = S.copy(std_b[:, sl], psX[:, 512:1024])
                assert t2 == be0 + ncc * 4 + 4
        blk.scalar(s_hilo)

        def v_lo(v):
            V = VW(ec, v)
            for ncc in range(NC4):
                sl = slice(ncc * 512, ncc * 512 + 512)
                V.wait("cACT", hl_tok[ncc])
                V.sub(sums_bf[0:1, 512:1024], psCtx[0:1, sl],
                      sums_bf[0:1, 0:512])
                t2 = V.sub(sums_bf[0:1, 1536:2048], psCtx[32:33, sl],
                           sums_bf[0:1, 1024:1536])
                assert t2 == lo0 + 2 * ncc + 2
        blk.vector(v_lo)

        def t_bc(t):
            T = TW(ec, t)
            for ncc in range(NC4):
                T.wait_future("cDVE", lo0 + 2 * ncc + 2)
                if ncc >= 1:
                    T.wait_future("cACT", be0 + ncc * 4)
                T.mm(psX[:, 0:512], ones_r[:, :], sums_bf[0:1, 0:512],
                     start=True, stop=False)
                T.mm(psX[:, 0:512], ones_r[:, :], sums_bf[0:1, 512:1024],
                     start=False, stop=True)
                T.mm(psX[:, 512:1024], ones_r[:, :], sums_bf[0:1, 1024:1536],
                     start=True, stop=False)
                t2 = T.mm(psX[:, 512:1024], ones_r[:, :],
                          sums_bf[0:1, 1536:2048],
                          start=False, stop=True, mark=True)
                assert t2 == bc0 + ncc + 1
        blk.tensor(t_bc)

        lnf = {}

        def v_fin(v):
            V = VW(ec, v)
            V.wait("cACT", be0 + 16)
            V.ts(mu_b[:, :], mu_b[:, :], 1.0 / D)
            V.ts(std_b[:, :], std_b[:, :], 1.0 / D)
            V.tt(tmp_b[:, :], mu_b[:, :], mu_b[:, :], ALU.mult)
            V.sub(std_b[:, :], std_b[:, :], tmp_b[:, :])
            lnf["var"] = V.ts(std_b[:, :], std_b[:, :], 1e-5, op0=ALU.add)
        blk.vector(v_fin)

        def s_sq2(s):
            S = SW(ec, s)
            S.wait("cDVE", lnf["var"])
            lnf["std"] = S.act(std_b[:, :], std_b[:, :], AF.Sqrt)
        blk.scalar(s_sq2)

        def v_fin2(v):
            V = VW(ec, v)
            V.wait("cACT", lnf["std"])
            V.recip(rstd_b[:, :], std_b[:, :])
            V.stt(tmp_b[:, :], mu_b[:, :], -1.0, rstd_b[:, :],
                  ALU.mult, ALU.mult)
            lnf["laug"] = V.copy(laug[0:1, :], tmp_b[0:1, :])
        blk.vector(v_fin2)

        def g_l1(g):
            G = GW(ec, g)
            G.wait("cDVE", lnf["laug"])
            G.dma("ldk", laug[1:2, :], on_h[:, :])
        blk.gpsimd(g_l1)
        ldk_l1 = ec.cnt["ldk"]

        zs3 = arK[:, 0:4096]
        gg = arV[:, 11776:12288]

        def w1b(pp, half):
            return arW[:, pp * 2048 + half * 1024:
                       pp * 2048 + half * 1024 + 1024]

        w2b = arW[:, 4096:4096 + 2816]

        for qtr in range(4):
            def g_zb3(g, qtr=qtr):
                G = GW(ec, g)
                if qtr == 0:
                    G.wait("ldk", ldk_l1)
                else:
                    G.wait("cPE", tok[("w1done", qtr - 1)])
                build_zbt(G, qtr, ("zbt4", qtr), lead=True)
            blk.gpsimd(g_zb3)

            def v_z3(v, qtr=qtr):
                V = VW(ec, v)
                V.wait("cPL", tok[("zbt4", qtr)])
                if qtr == 0:
                    V.wait("cDVE", lnf["laug"])
                t2 = None
                for dk in range(NDK):
                    sl = zs3[:, dk * 512:dk * 512 + 512]
                    V.ts(sl, zbs(dk, True), g8[:, dk:dk + 1], op0=ALU.mult)
                    t2 = V.tt(sl, sl, rstd_b[:, qtr * 512:qtr * 512 + 512],
                              ALU.mult)
                tok[("zs3", qtr)] = t2
            blk.vector(v_z3)

            for jp in range(NJP):
                w = JW[jp]
                pp = jp % 2
                idx = qtr * NJP + jp

                def g_w1(g, qtr=qtr, jp=jp, w=w, pp=pp, idx=idx):
                    G = GW(ec, g)
                    if idx == 0:
                        G.wait("cPE", wo_p0 + 32)
                    if qtr > 0 and jp == 0:
                        G.wait("cPE", tok[("w2c", qtr * NDK - 1)])
                    if idx >= 2:
                        G.wait("cPE", tok[("w1c", idx - 2)])
                    sname = "ldw1A" if pp == 0 else "ldw1B"
                    for half in range(2):
                        dst = w1b(pp, half)
                        tok[("w1l", idx)] = G.dma(
                            sname,
                            dst[:, 0:NDK * w] if w < 128 else dst,
                            bass.AP(w1_h.tensor, half * DI + jp * 128,
                                    [[W1C, 128], [128 * W1C, NDK], [1, w]]))
                blk.gpsimd(g_w1)

                def wsl(pp, half, dk, w):
                    b = w1b(pp, half)
                    return (b[:, dk * w:dk * w + w] if w < 128 else
                            b[:, dk * 128:dk * 128 + w])

                def t_w1(t, qtr=qtr, jp=jp, w=w, pp=pp, idx=idx):
                    T = TW(ec, t)
                    T.wait("ldw1A" if pp == 0 else "ldw1B", tok[("w1l", idx)])
                    T.wait("cDVE", tok[("zs3", qtr)])
                    if idx >= 1:
                        T.wait("cACT", tok[("stb", idx - 1)])
                    for dk in range(NDK):
                        T.mm(psD2[0:2, 0:w], gb[:, 2 * dk:2 * dk + 2],
                             wsl(pp, 0, dk, w),
                             start=(dk == 0), stop=(dk == NDK - 1))
                        t2 = T.mm(psD2[0:2, 512:512 + w],
                                  gb[:, 2 * dk:2 * dk + 2],
                                  wsl(pp, 1, dk, w),
                                  start=(dk == 0), stop=(dk == NDK - 1),
                                  mark=(dk == NDK - 1))
                    tok[("stm", idx)] = t2
                blk.tensor(t_w1)

                def s_st(s, idx=idx, w=w):
                    S = SW(ec, s)
                    S.wait("cPE", tok[("stm", idx)])
                    if idx >= 1:
                        S.wait("cPE", tok[("hgm", idx - 1)])
                    S.copy(stb[0:2, 0:w], psD2[0:2, 0:w])
                    tok[("stb", idx)] = S.copy(stb[0:2, 128:128 + w],
                                               psD2[0:2, 512:512 + w])
                blk.scalar(s_st)

                def t_hg(t, qtr=qtr, jp=jp, w=w, pp=pp, idx=idx):
                    T = TW(ec, t)
                    T.wait("cACT", tok[("stb", idx)])
                    if idx >= 2:
                        T.wait("cDVE", tok[("hst", idx - 2)])
                    pa = psCtx[:, (pp * 2) * 512:(pp * 2) * 512 + 512]
                    pg = psCtx[:, (pp * 2 + 1) * 512:(pp * 2 + 1) * 512 + 512]
                    for dk in range(NDK):
                        T.mm(pa[0:w, :], wsl(pp, 0, dk, w),
                             zs3[:, dk * 512:dk * 512 + 512],
                             start=(dk == 0), stop=False)
                        T.mm(pg[0:w, :], wsl(pp, 1, dk, w),
                             zs3[:, dk * 512:dk * 512 + 512],
                             start=(dk == 0), stop=False)
                    T.mm(pa[0:w, :], stb[0:2, 0:w],
                         laug[0:2, qtr * 512:qtr * 512 + 512],
                         start=False, stop=True)
                    t2 = T.mm(pg[0:w, :], stb[0:2, 128:128 + w],
                              laug[0:2, qtr * 512:qtr * 512 + 512],
                              start=False, stop=True, mark=True)
                    tok[("hgm", idx)] = t2
                    tok[("w1c", idx)] = t2
                blk.tensor(t_hg)

                def s_gg(s, jp=jp, w=w, pp=pp, idx=idx):
                    S = SW(ec, s)
                    S.wait("cPE", tok[("hgm", idx)])
                    if idx >= 1:
                        S.wait_future("cDVE", tok.get(("hst", idx - 1), 0))
                    pg = psCtx[:, (pp * 2 + 1) * 512:(pp * 2 + 1) * 512 + 512]
                    tok[("gg", idx)] = S.act(gg[0:w, :], pg[0:w, :], AF.Gelu,
                                             bias=b1g[0:w, jp:jp + 1])
                blk.scalar(s_gg)

                def v_h(v, qtr=qtr, jp=jp, w=w, pp=pp, idx=idx):
                    V = VW(ec, v)
                    V.wait("cACT", tok[("gg", idx)])
                    if qtr > 0 and jp == 0:
                        V.wait("cPE", tok[("w2c", qtr * NDK - 1)])
                    pa = psCtx[:, (pp * 2) * 512:(pp * 2) * 512 + 512]
                    tok[("hst", idx)] = V.stt(
                        arV[0:w, jp * 512:jp * 512 + 512],
                        pa[0:w, :], b1a[0:w, jp:jp + 1], gg[0:w, :],
                        ALU.add, ALU.mult)
                blk.vector(v_h)
            tok[("w1done", qtr)] = ec.cnt["cPE"]

            for oc in range(NDK):
                widx = qtr * NDK + oc

                def g_w2(g, qtr=qtr, oc=oc, widx=widx):
                    G = GW(ec, g)
                    if widx >= 1:
                        G.wait("cPE", tok[("w2c", widx - 1)])
                    else:
                        G.wait("cPE", tok[("w1c", NJP - 1)])
                    G.dma("ldw2", w2b[:, 0:21 * 128],
                          bass.AP(w2_h.tensor, oc * 128,
                                  [[D, 128], [128 * D, 21], [1, 128]]))
                    G.dma("ldw2", w2b[0:42, 21 * 128:21 * 128 + 128],
                          bass.AP(w2_h.tensor, 2688 * D + oc * 128,
                                  [[D, 42], [1, 128]]))
                blk.gpsimd(g_w2)

                def t_w2(t, qtr=qtr, oc=oc, widx=widx):
                    T = TW(ec, t)
                    T.wait("ldw2", (widx + 1) * 32)
                    T.wait("cDVE", tok[("hst", qtr * NJP + NJP - 1)])
                    if widx >= 2:
                        T.wait("cDVE", tok[("zu2", widx - 2)])
                    bank = psX[:, (widx % 2) * 512:(widx % 2) * 512 + 512]
                    for ik in range(NJP):
                        kw = JW[ik]
                        t2 = T.mm(bank,
                                  w2b[0:kw, ik * 128:ik * 128 + 128],
                                  arV[0:kw, ik * 512:ik * 512 + 512],
                                  start=(ik == 0), stop=(ik == NJP - 1),
                                  mark=(ik == NJP - 1))
                    tok[("w2c", widx)] = t2
                blk.tensor(t_w2)

                def v_z2(v, qtr=qtr, oc=oc, widx=widx):
                    V = VW(ec, v)
                    V.wait("cPE", tok[("w2c", widx)])
                    bank = psX[:, (widx % 2) * 512:(widx % 2) * 512 + 512]
                    sl = zf(oc, qtr * 512, qtr * 512 + 512)
                    tok[("zu2", widx)] = V.stt(sl, bank, b28[:, oc:oc + 1],
                                               sl, ALU.add, ALU.add)
                blk.vector(v_z2)

        def sy_st(s):
            for oc in range(NDK):
                for qtr in range(4):
                    ec.wait(s, "cDVE", tok[("zu2", qtr * NDK + oc)])
                ec.inc("st0", s.dma_start(
                    out=zo_h[oc * 128:(oc + 1) * 128, :], in_=zf(oc, 0, N)), 16)
            ec.wait(s, "st0")
        blk.sync(sy_st)

        ec.close()
    return nc


def l1_consts():
    poolind = np.zeros((128, 64), np.float32)
    for p in range(128):
        poolind[p, p % 64] = 1.0
    return {"poolind": poolind.astype(nbf)}


def l1_inmap(x_fm, ao_fm, z6_fm, inp):
    ic = l1_consts()
    qw = np.asarray(inp["L1_attn_q_w"], np.float32)
    kw = np.asarray(inp["L1_attn_k_w"], np.float32)
    vw = np.asarray(inp["L1_attn_v_w"], np.float32)
    w3 = np.stack([np.transpose(qw, (0, 2, 1)),
                   np.transpose(kw, (0, 2, 1)),
                   np.transpose(vw, (0, 2, 1))], axis=1)
    qb = np.asarray(inp["L1_attn_q_b"], np.float32).reshape(16, 64)
    kb = np.asarray(inp["L1_attn_k_b"], np.float32).reshape(16, 64)
    qkb2 = np.zeros((64, 32), np.float32)
    qkb2[:, 0::2] = qb.T
    qkb2[:, 1::2] = kb.T
    vb = np.asarray(inp["L1_attn_v_b"], np.float32).reshape(1, 16 * 64)
    return {
        "x": np.ascontiguousarray(x_fm, np.float32),
        "ao": np.ascontiguousarray(ao_fm, np.float32),
        "z6": np.ascontiguousarray(z6_fm, np.float32),
        "w3": np.ascontiguousarray(w3.reshape(16, 3 * 1024 * 64)),
        "qkb2": qkb2, "vb": vb,
        "lwT": np.ascontiguousarray(
            np.asarray(inp["L1_attn_l_w"], np.float32).T),
        "lb": percol(np.asarray(inp["L1_attn_l_b"],
                                np.float32).reshape(-1)),
        "wT": np.ascontiguousarray(
            np.asarray(inp["L1_ffn_weight"], np.float32).T),
        "disha": np.ascontiguousarray(inp["L1_ffn_disha"], np.float32),
        "poolind": ic["poolind"],
    }


def build_l1():
    nc = bass.Bass("TRN2", target_bir_lowering=False)
    x_h = nc.dram_tensor("x", [D, N], F32, kind="ExternalInput").ap()
    ao_h = nc.dram_tensor("ao", [D, N], F32, kind="ExternalInput").ap()
    z6_h = nc.dram_tensor("z6", [D, N], F32, kind="ExternalInput").ap()
    w3_h = nc.dram_tensor("w3", [16, 3 * D * 64], F32,
                          kind="ExternalInput").ap()
    qkb_h = nc.dram_tensor("qkb2", [64, 32], F32, kind="ExternalInput").ap()
    vb_h = nc.dram_tensor("vb", [1, 1024], F32, kind="ExternalInput").ap()
    lwT_h = nc.dram_tensor("lwT", [D, D], F32, kind="ExternalInput").ap()
    lb_h = nc.dram_tensor("lb", [128, NDK], F32, kind="ExternalInput").ap()
    wT_h = nc.dram_tensor("wT", [D, D], F32, kind="ExternalInput").ap()
    dsh_h = nc.dram_tensor("disha", [64, D], F32, kind="ExternalInput").ap()
    pi_h = nc.dram_tensor("poolind", [128, 64], BF16,
                          kind="ExternalInput").ap()
    out_h = nc.dram_tensor("out", [D, N], F32, kind="ExternalOutput").ap()

    st_ = ExitStack()
    with st_:
        _sb = [
            ("x1", (128, NDK * N), F32),
            ("xn2", (128, NDK * N), BF16),
            ("ao2", (128, NDK * N), BF16),
            ("qA", (64, N), BF16), ("kA", (64, N), BF16),
            ("qB", (64, N), BF16), ("kB", (64, N), BF16),
            ("vA", (128, 1024), BF16), ("vB", (128, 1024), BF16),
            ("eT", (128, N), BF16),
            ("vt", (128, 64), BF16),
            ("wqkv", (128, 2 * 1536), BF16),
            ("sqs", (128, 2 * N), BF16),
            ("sums", (1, 2048), BF16),
            ("qkb2_s", (64, 32), F32),
            ("vbr", (1, 1024), BF16),
            ("mxc", (128, 1), F32),
            ("dsc", (128, 1), F32),
            ("warA", (128, 2048), BF16),
            ("warB", (128, 2048), BF16),
            ("dishaB", (64, 1024), BF16),
            ("poolind_s", (128, 64), BF16),
            ("pooled", (64, N), BF16),
            ("lbc", (128, NDK), F32),
            ("outs", (128, 1024), F32),
            ("ones_c", (128, 1), BF16),
            ("ones_r", (1, 128), BF16),
        ]
        _ps = [("psS", (128, 2048)), ("psC", (128, 2048))]
        _t = _alloc(nc, st_, _sb, _ps)
        x1 = _t["x1"]; xn2 = _t["xn2"]; ao2 = _t["ao2"]
        qA = _t["qA"]; kA = _t["kA"]; qB = _t["qB"]; kB = _t["kB"]
        vA = _t["vA"]; vB = _t["vB"]; eT = _t["eT"]; vt = _t["vt"]
        wqkv = _t["wqkv"]; sqs = _t["sqs"]; sums = _t["sums"]
        qkb2 = _t["qkb2_s"]; vbr = _t["vbr"]; mxc = _t["mxc"]
        dsc = _t["dsc"]; warA = _t["warA"]; warB = _t["warB"]
        dishaB = _t["dishaB"]; poolind = _t["poolind_s"]
        pooled = _t["pooled"]; lbc = _t["lbc"]; outs = _t["outs"]
        ones_c = _t["ones_c"]; ones_r = _t["ones_r"]
        psS = _t["psS"]; psC = _t["psC"]
        blk = st_.enter_context(nc.Block())

        mu_b = ao2[:, 0:4096].bitcast(F32)
        std_b = ao2[:, 4096:8192].bitcast(F32)
        rstd_b = ao2[:, 8192:12288].bitcast(F32)
        tmp_b = ao2[:, 12288:16384].bitcast(F32)
        tA = xn2[:, 0:8192].bitcast(F32)
        tB = xn2[:, 8192:16384].bitcast(F32)

        ec = EC(nc, blk)
        for sname in ("ldx", "ldaA", "ldaB", "ldc", "ldk", "ldwA", "ldwB",
                      "ldfA", "ldfB", "st0", "st1"):
            ec.sem(sname)
        tok = {}
        cmb0 = [None]

        def tok_cmb(dk):
            return cmb0[0] + 2 * dk + 2

        cmb0[0] = ec.cnt["cDVE"]

        def sy_ld(s):
            for dk in range(NDK):
                ec.inc("ldx", s.dma_start(
                    out=x1[:, dk * N:(dk + 1) * N],
                    in_=x_h[dk * 128:(dk + 1) * 128, :]), 16)
            for dk in range(NDK):
                sl = tA if dk % 2 == 0 else tB
                sname = "ldaA" if dk % 2 == 0 else "ldaB"
                if dk >= 2:
                    ec.wait_future(s, "cDVE", tok_cmb(dk - 2))
                ec.inc(sname, s.dma_start(
                    out=sl[:, 0:N], in_=ao_h[dk * 128:(dk + 1) * 128, :]), 16)
                ec.inc(sname, s.dma_start(
                    out=sl[:, N:2 * N],
                    in_=z6_h[dk * 128:(dk + 1) * 128, :]), 16)
            ec.inc("ldc", s.dma_start(out=qkb2[:, :], in_=qkb_h[:, :]), 16)
            ec.inc("ldc", s.dma_start(out=lbc[:, :], in_=lb_h[:, :]), 16)
        blk.sync(sy_ld)

        def v_cmb(v):
            V = VW(ec, v)
            assert cmb0[0] == ec.cnt["cDVE"]
            V.wait("ldx")
            for dk in range(NDK):
                sl = tA if dk % 2 == 0 else tB
                V.wait("ldaA" if dk % 2 == 0 else "ldaB",
                       32 * (dk // 2 + 1))
                xs = x1[:, dk * N:(dk + 1) * N]
                V.add(xs, xs, sl[:, 0:N])
                t2 = V.add(xs, xs, sl[:, N:2 * N])
                assert t2 == tok_cmb(dk)
        blk.vector(v_cmb)

        def g_ld(g):
            G = GW(ec, g)
            G.memset(ones_c[:, :], 1.0)
            G.memset(ones_r[:, :], 1.0)
            G.dma("ldk", vbr[:, :], vb_h[:, :])
            G.dma("ldk", poolind[:, :], pi_h[:, :])
            G.dma("ldk", dishaB[:, :], dsh_h[:, :])
            G.wait("cDVE", tok_cmb(NDK - 1))
            for dk in range(NDK):
                tok[("xb", dk)] = G.copy(xn2[:, dk * N:(dk + 1) * N],
                                         x1[:, dk * N:(dk + 1) * N])
        blk.gpsimd(g_ld)

        ln = emit_ln(
            ec,
            xbf_sl=lambda dk: xn2[:, dk * N:(dk + 1) * N],
            sq_scr=sqs, ones_c=ones_c, ones_r=ones_r,
            ps_stats=psS, ps_bc=psC[:, 0:1024], sums_bf=sums,
            mu_b=mu_b, rstd_b=rstd_b, tmp_b=tmp_b, std_b=std_b,
            want_aug=True, n_tok=N,
            dep=[("cPL", tok[("xb", NDK - 1)])])

        xnt = {}

        def v_xn(v):
            V = VW(ec, v)
            V.wait("cDVE", ln["dn"])
            for dk in range(NDK):
                sl = xn2[:, dk * N:(dk + 1) * N]
                V.tt(sl, x1[:, dk * N:(dk + 1) * N], rstd_b[:, :], ALU.mult)
                xnt[dk] = V.add(sl, sl, tmp_b[:, :])
            xnt["all"] = xnt[NDK - 1]
        blk.vector(v_xn)

        def wqkv_half(h):
            return wqkv[:, (h % 2) * 1536:(h % 2) * 1536 + 1536]

        for h in range(2):
            def g_w0(g, h=h):
                G = GW(ec, g)
                tok[("w3l", h)] = G.dma(
                    "ldwA" if h % 2 == 0 else "ldwB", wqkv_half(h),
                    bass.AP(w3_h.tensor, h * 3 * D * 64,
                            [[64, 128], [D * 64, 3], [128 * 64, NDK],
                             [1, 64]]))
            blk.gpsimd(g_w0)

        for hp in range(8):
            for j in range(2):
                h = 2 * hp + j
                qh = qA if j == 0 else qB
                kh = kA if j == 0 else kB
                vh = vA if j == 0 else vB
                buf = wqkv_half(h)

                if h >= 2:
                    def g_w2(g, h=h):
                        G = GW(ec, g)
                        G.wait_future("cPE", tok[("prj", h - 2)])
                        tok[("w3l", h)] = G.dma(
                            "ldwA" if h % 2 == 0 else "ldwB", wqkv_half(h),
                            bass.AP(w3_h.tensor, h * 3 * D * 64,
                                    [[64, 128], [D * 64, 3], [128 * 64, NDK],
                                     [1, 64]]))
                    blk.gpsimd(g_w2)

                def t_prj(t, h=h, buf=buf):
                    T = TW(ec, t)
                    T.wait("ldwA" if h % 2 == 0 else "ldwB", tok[("w3l", h)])
                    if h == 0:
                        T.wait("cDVE", xnt["all"])
                        T.wait("cPL")
                    else:
                        T.wait_future("cACT", tok[("exp", h - 1, NQT - 1)])
                    for ncc in range(NC4):
                        sl = slice(ncc * 512, ncc * 512 + 512)
                        for ti in range(2):
                            for dk in range(NDK):
                                t2 = T.mm(
                                    psS[ti * 64:ti * 64 + 64, sl],
                                    buf[:, ti * 512 + dk * 64:
                                        ti * 512 + dk * 64 + 64],
                                    xn2[:, dk * N + ncc * 512:
                                        dk * N + ncc * 512 + 512],
                                    start=(dk == 0), stop=(dk == NDK - 1),
                                    skip=True,
                                    mark=(dk == NDK - 1))
                            tok[("qk", h, ncc, ti)] = t2
                    tok[("qkall", h)] = t2
                blk.tensor(t_prj)

                def s_qke(s, h=h, qh=qh, kh=kh):
                    S = SW(ec, s)
                    if h == 0:
                        S.wait("ldc")
                    for ncc in range(NC4):
                        sl = slice(ncc * 512, ncc * 512 + 512)
                        S.wait("cPE", tok[("qk", h, ncc, 1)])
                        S.act(qh[:, sl], psS[0:64, sl], AF.Identity,
                              bias=qkb2[:, 2 * h:2 * h + 1])
                        t2 = S.act(kh[:, sl], psS[64:128, sl], AF.Identity,
                                   bias=qkb2[:, 2 * h + 1:2 * h + 2])
                    tok[("qke", h)] = t2
                blk.scalar(s_qke)

                def t_vp(t, h=h, buf=buf):
                    T = TW(ec, t)
                    T.wait("cACT", tok[("qke", h)])
                    if h == 0:
                        T.wait("ldk")
                    for kt in range(NQT):
                        bank = kt // 8
                        c0 = bank * 512 + (kt % 8) * 64
                        for dk in range(NDK):
                            T.mm(psS[:, c0:c0 + 64],
                                 xn2[:, dk * N + kt * 128:
                                     dk * N + kt * 128 + 128],
                                 buf[:, 1024 + dk * 64:1024 + dk * 64 + 64],
                                 start=(dk == 0 and kt % 8 == 0),
                                 stop=False, skip=True)
                        t2 = T.mm(psS[:, c0:c0 + 64], ones_r[:, :],
                                  vbr[0:1, h * 64:h * 64 + 64],
                                  start=False, stop=(kt % 8 == 7),
                                  skip=True, mark=(kt % 8 == 7))
                    tok[("prj", h)] = t2
                blk.tensor(t_vp)

                def s_ve(s, h=h, vh=vh):
                    S = SW(ec, s)
                    S.wait("cPE", tok[("prj", h)])
                    S.copy(vh[:, 0:512], psS[:, 0:512])
                    tok[("pev", h)] = S.copy(vh[:, 512:1024],
                                             psS[:, 512:1024])
                blk.scalar(s_ve)

                def t_sc(t, hp=hp, j=j, h=h, qh=qh, kh=kh):
                    T = TW(ec, t)
                    for kt in range(NQT):
                        if kt == 0:
                            T.wait("cACT", tok[("pev", h)])
                        else:
                            T.wait_future("cACT", tok[("exp", h, kt - 1)])
                        for b in range(4):
                            t2 = T.mm(psS[:, b * 512:b * 512 + 512],
                                      kh[:, kt * 128:kt * 128 + 128],
                                      qh[:, b * 512:b * 512 + 512],
                                      start=True, stop=True, skip=True,
                                      mark=(b == 3))
                        tok[("qkm", h, kt)] = t2
                        T.wait_future("cDVE", tok[("vt", h, kt)])
                        for b in range(4):
                            t2 = T.mm(psC[j * 64:j * 64 + 64,
                                          b * 512:b * 512 + 512],
                                      vt[:, :],
                                      eT[:, b * 512:b * 512 + 512],
                                      start=(kt == 0), stop=(kt == NQT - 1),
                                      skip=True, mark=(b == 3))
                        tok[("pvm", h, kt)] = t2
                vt0 = ec.cnt["cDVE"]
                ex0 = ec.cnt["cACT"]
                for kt in range(NQT):
                    tok[("vt", h, kt)] = vt0 + 4 * kt + 4
                    tok[("expf", h, kt)] = ex0 + kt + 1
                    tok[("exp", h, kt)] = ex0 + kt + 1
                blk.tensor(t_sc)

                def v_sc(v, h=h, vh=vh, vt0=vt0):
                    V = VW(ec, v)
                    for kt in range(NQT):
                        V.wait("cPE", tok[("qkm", h, kt)])
                        V.red(mxc[:, 0:1],
                              ap3(psS[:, :], 0, [[2048, 128], [1, 2048]]),
                              op=ALU.max, negate=True)
                        V.ts(mxc[:, 0:1], mxc[:, 0:1], 0.125)
                        V.wait_future("cACT", tok[("expf", h, kt)])
                        V.recip(dsc[:, 0:1], dsc[:, 0:1])
                        if kt >= 1:
                            V.wait("cPE", tok[("pvm", h, kt - 1)])
                        t2 = V.ts(vt[:, :], vh[:, kt * 64:kt * 64 + 64],
                                  dsc[:, 0:1], op0=ALU.mult)
                        assert t2 == tok[("vt", h, kt)]
                blk.vector(v_sc)

                def s_sc(s, h=h, ex0=ex0):
                    S = SW(ec, s)
                    for kt in range(NQT):
                        S.wait("cDVE", vt0 + 4 * kt + 2)
                        if kt >= 1:
                            S.wait("cPE", tok[("pvm", h, kt - 1)])
                        t2 = S.act(eT[:, :], psS[:, :], AF.Exp,
                                   bias=mxc[:, 0:1], scale=0.125,
                                   accum_out=dsc[:, 0:1])
                        assert t2 == ex0 + kt + 1
                blk.scalar(s_sc)

            def s_ce(s, hp=hp):
                S = SW(ec, s)
                S.wait("cPE", tok[("pvm", 2 * hp + 1, NQT - 1)])
                for b in range(4):
                    t2 = S.copy(ao2[:, hp * N + b * 512:hp * N + b * 512 + 512],
                                psC[:, b * 512:b * 512 + 512])
                tok[("ctxev", hp)] = t2
            blk.scalar(s_ce)

        def t_pool(t):
            T = TW(ec, t)
            T.wait("cACT", tok[("exp", 15, NQT - 1)])
            for ncc in range(NC4):
                sl = slice(ncc * 512, ncc * 512 + 512)
                for dk in range(NDK):
                    t2 = T.mm(psS[0:64, sl], poolind[:, :],
                              xn2[:, dk * N + ncc * 512:
                                  dk * N + ncc * 512 + 512],
                              start=(dk == 0), stop=(dk == NDK - 1),
                              skip=True, mark=(dk == NDK - 1))
                tok[("pool", ncc)] = t2
        blk.tensor(t_pool)

        def s_pe(s):
            S = SW(ec, s)
            for ncc in range(NC4):
                sl = slice(ncc * 512, ncc * 512 + 512)
                S.wait("cPE", tok[("pool", ncc)])
                t2 = S.copy(pooled[:, sl], psS[0:64, sl])
            tok["poole"] = t2
        blk.scalar(s_pe)

        for oc in range(NDK):
            def g_f(g, oc=oc):
                G = GW(ec, g)
                if oc >= 2:
                    G.wait_future("cPE", tok[("fin", (oc - 1) * NC4 - 1)])
                h0 = (oc % 2) * 1024
                sname = "ldfA" if oc % 2 == 0 else "ldfB"
                G.dma(sname, warA[:, h0:h0 + 1024],
                      bass.AP(lwT_h.tensor, oc * 128,
                              [[D, 128], [128 * D, NDK], [1, 128]]))
                tok[("fl", oc)] = G.dma(
                    sname, warB[:, h0:h0 + 1024],
                    bass.AP(wT_h.tensor, oc * 128,
                            [[D, 128], [128 * D, NDK], [1, 128]]))
            blk.gpsimd(g_f)

            for ncc in range(NC4):
                idx = oc * NC4 + ncc
                bank = idx % 4
                half = idx % 2

                def t_fin(t, oc=oc, ncc=ncc, idx=idx, bank=bank):
                    T = TW(ec, t)
                    h0 = (oc % 2) * 1024
                    if ncc == 0:
                        T.wait("ldfA" if oc % 2 == 0 else "ldfB",
                               tok[("fl", oc)])
                    if idx == 0:
                        T.wait("cACT", tok["poole"])
                        T.wait("cACT", tok[("ctxev", 7)])
                    if idx >= 4:
                        T.wait_future("cDVE", tok[("fst", idx - 4)])
                    ps = psC[:, bank * 512:bank * 512 + 512]
                    for dk in range(NDK):
                        T.mm(ps, warA[:, h0 + dk * 128:h0 + dk * 128 + 128],
                             ao2[:, dk * N + ncc * 512:
                                 dk * N + ncc * 512 + 512],
                             start=(dk == 0), stop=False, skip=True)
                    for dk in range(NDK):
                        T.mm(ps, warB[:, h0 + dk * 128:h0 + dk * 128 + 128],
                             xn2[:, dk * N + ncc * 512:
                                 dk * N + ncc * 512 + 512],
                             start=False, stop=False, skip=True)
                    t2 = T.mm(ps, dishaB[:, oc * 128:oc * 128 + 128],
                              pooled[:, ncc * 512:ncc * 512 + 512],
                              start=False, stop=True, skip=True, mark=True)
                    tok[("fin", idx)] = t2
                blk.tensor(t_fin)

                def v_fin(v, oc=oc, ncc=ncc, idx=idx, bank=bank, half=half):
                    V = VW(ec, v)
                    V.wait("cPE", tok[("fin", idx)])
                    if idx >= 2:
                        sname = "st0" if half == 0 else "st1"
                        V.wait_future(sname, 16 * (idx // 2))
                    ps = psC[:, bank * 512:bank * 512 + 512]
                    t2 = V.stt(outs[:, half * 512:half * 512 + 512],
                               ps, lbc[:, oc:oc + 1],
                               x1[:, oc * N + ncc * 512:
                                  oc * N + ncc * 512 + 512],
                               ALU.add, ALU.add)
                    tok[("fst", idx)] = t2
                blk.vector(v_fin)

        def sy_st(s):
            for oc in range(NDK):
                for ncc in range(NC4):
                    idx = oc * NC4 + ncc
                    half = idx % 2
                    ec.wait(s, "cDVE", tok[("fst", idx)])
                    ec.inc("st0" if half == 0 else "st1", s.dma_start(
                        out=bass.AP(out_h.tensor, oc * 128 * N + ncc * 512,
                                    [[N, 128], [1, 512]]),
                        in_=outs[:, half * 512:half * 512 + 512]), 16)
            ec.wait(s, "st0")
            ec.wait(s, "st1")
        blk.sync(sy_st)

        ec.close()
    return nc


def kernel(**inputs):
    inp = {k: np.asarray(v) for k, v in inputs.items()}
    x = np.asarray(inp["L0_attn_x"], np.float32)
    B = x.shape[0]
    hc = host_consts()
    T_ = lambda a: np.ascontiguousarray(np.asarray(a, np.float32).T)
    cores = list(range(B))
    maps = [{"x": np.ascontiguousarray(x[b].T)} for b in range(B)]
    res = run_bass_kernel_spmd(build_pre(), maps, core_ids=cores)
    xn = [np.asarray(res.results[b]["xn"]) for b in range(B)]
    com = {
        "wq": T_(inp["L0_attn_Wq"]), "wk": T_(inp["L0_attn_Wk"]),
        "wv": T_(inp["L0_attn_Wv"]), "wo": T_(inp["L0_attn_Wo"]),
        "bq": percol(inp["L0_attn_bq"]), "bk": percol(inp["L0_attn_bk"]),
        "bo": percol(inp["L0_attn_bo"]),
        "bv": np.asarray(inp["L0_attn_bv"], np.float32).reshape(1, -1),
        "lam": np.asarray(inp["L0_attn_prior_lambda"],
                          np.float32).reshape(1),
        "mask8": hc["mask8"], "ind16": hc["ind16"],
        "idnf": np.eye(16, dtype=np.float32),
    }
    maps = [dict(com, xn=xn[b]) for b in range(B)]
    res = run_bass_kernel_spmd(build_l0attn(), maps, core_ids=cores)
    ao = [np.asarray(res.results[b]["ao"]) for b in range(B)]
    nc_i = build_inner()
    z = xn
    for i in range(IDEPTH):
        maps = [inner_inmap(i, z[b], inp) for b in range(B)]
        res = run_bass_kernel_spmd(nc_i, maps, core_ids=cores)
        z = [np.asarray(res.results[b]["zo"]) for b in range(B)]
    maps = [l1_inmap(np.ascontiguousarray(x[b].T), ao[b], z[b], inp)
            for b in range(B)]
    res = run_bass_kernel_spmd(build_l1(), maps, core_ids=cores)
    out = np.stack([np.asarray(res.results[b]["out"]).T for b in range(B)])
    return np.ascontiguousarray(out, np.float32)


# version 02
# speedup vs baseline: 1.0077x; 1.0077x over previous
import math
import numpy as np
import ml_dtypes
import concourse.bass as bass
import concourse.mybir as mybir
from concourse.bass_utils import run_bass_kernel_spmd
from contextlib import ExitStack

F32 = mybir.dt.float32
BF16 = mybir.dt.bfloat16
AF = mybir.ActivationFunctionType
ALU = mybir.AluOpType
AX = mybir.AxisListType
nbf = ml_dtypes.bfloat16

D = 1024
N = 2048
H = 16
HD = 64
NDK = D // 128
NQT = N // 128
NC4 = N // 512
DI = 2730
W1C = 2 * DI
IDEPTH = 6
WIN = 256

BIGNEG = -1.0e30


def _f_band(delta):
    d = np.abs(np.asarray(delta, np.float64))
    m = 0.8 * np.exp(-((d - 1.0) ** 2) / (2.0 * 0.5 ** 2)) \
        + 0.4 * np.exp(-((d - 30.0) ** 2) / (2.0 * 2.0 ** 2))
    return m.astype(np.float32)


_HOST_CONSTS = None


def host_consts():
    global _HOST_CONSTS
    if _HOST_CONSTS is not None:
        return _HOST_CONSTS
    c = {}
    c["idn_bf"] = np.eye(128, dtype=np.float32).astype(nbf)
    c["idn_f32"] = np.eye(128, dtype=np.float32)
    p = np.arange(128)[:, None]
    j = np.arange(256)[None, :]
    c["mask8"] = (8.0 * _f_band(j - 61 - p)).astype(np.float32)
    c["f_max"] = float(_f_band(np.arange(-100, 100)).max())
    jj = np.arange(384)[None, :]
    c["mask01"] = (((jj >= p) & (jj <= p + WIN)).astype(np.float32)).astype(nbf)
    ind = np.zeros((128, NDK, 16), np.float32)
    for dk in range(NDK):
        ind[0:64, dk, 2 * dk] = 1.0
        ind[64:128, dk, 2 * dk + 1] = 1.0
    c["ind16"] = ind.reshape(128, NDK * 16).astype(nbf)
    selb = np.zeros((16, 128), np.float32)
    selb[0, 0:64] = 1.0
    selb[8, 64:128] = 1.0
    c["selb"] = selb.astype(nbf)
    sel = np.zeros((128, 16, 16), np.float32)
    for h in range(16):
        sel[:, h, h] = 1.0
    c["sel16"] = sel.reshape(128, 256).astype(nbf)
    pr = np.zeros((128, 64), np.float32)
    pr[np.arange(128), np.arange(128) % 64] = 1.0
    c["poolind"] = pr.astype(nbf)
    half = np.arange(0, HD, 2, dtype=np.float64)
    inv_freq = 1.0 / 10000.0 ** (half / HD)
    t = np.arange(N, dtype=np.float64)
    freqs = t[:, None] * inv_freq[None, :]
    freqs = np.concatenate([freqs, freqs], -1)
    base_scale = (half + 0.4 * HD) / (1.4 * HD)
    power = (t - N // 2) / (WIN // 2)
    scale = base_scale[None, :] ** power[:, None]
    scale = np.concatenate([scale, scale], -1)
    cos = np.cos(freqs)
    sin = np.sin(freqs)
    sign = np.where(np.arange(HD) < HD // 2, -1.0, 1.0)
    cs = (cos * scale).astype(np.float32)
    ss = (sin * scale * sign[None, :]).astype(np.float32)
    ci = (cos / scale).astype(np.float32)
    si = (sin / scale * sign[None, :]).astype(np.float32)

    def fm(tab):
        return np.concatenate([tab.T, tab.T], axis=0).astype(np.float32)
    c["rot_cs"] = fm(cs)
    c["rot_ss"] = fm(ss)
    c["rot_ci"] = fm(ci)
    c["rot_si"] = fm(si)
    _HOST_CONSTS = c
    return c


def percol(v):
    return np.ascontiguousarray(np.asarray(v, np.float32).reshape(NDK, 128).T)


_CHAIN_SEMS = ("cPE", "cDVE", "cACT", "cPL")


class EC:

    def __init__(self, nc, blk):
        self.nc = nc
        self.blk = blk
        self.px = ""
        self.sems = {}
        self.cnt = {}
        self._exit = []
        self._fut = {}
        for c in _CHAIN_SEMS:
            self.sem(c)

    def _n(self, name):
        return name if name in _CHAIN_SEMS else self.px + name

    def sem(self, name):
        name = self._n(name)
        if name not in self.sems:
            cm = self.nc.semaphore(name)
            h = cm.__enter__()
            self._exit.append(cm)
            self.sems[name] = h
            self.cnt[name] = 0
        return self.sems[name]

    def inc(self, name, inst, n=1):
        name = self._n(name)
        inst.then_inc(self.sem(name), n)
        self.cnt[name] += n
        return self.cnt[name]

    def wait(self, eng, name, val=None):
        name = self._n(name)
        if val is None:
            val = self.cnt[name]
        if val <= 0:
            return
        assert val <= self.cnt.get(name, 0), (name, val, self.cnt.get(name))
        eng.wait_ge(self.sem(name), val)

    def wait_future(self, eng, name, val):
        name = self._n(name)
        if val <= 0:
            return
        self._fut[name] = max(self._fut.get(name, 0), val)
        eng.wait_ge(self.sem(name), val)

    def barrier(self):
        snap = [(n, v) for n, v in self.cnt.items() if v > 0]

        def mk(eng):
            for n, v in snap:
                eng.wait_ge(self.sems[n], v)
        for attr in ("tensor", "vector", "scalar", "gpsimd", "sync"):
            getattr(self.blk, attr)(mk)

    def close(self):
        for name, val in self._fut.items():
            assert val <= self.cnt.get(name, 0), \
                (f"unsatisfied future wait {name}", val, self.cnt.get(name))
        for cm in reversed(self._exit):
            cm.__exit__(None, None, None)


class _EngW:
    CHAIN = None

    def __init__(self, ec, raw):
        self.ec = ec
        self.r = raw
        ec.sem(self.CHAIN)

    def _do(self, f, *a, **kw):
        ec = self.ec
        c = self.CHAIN
        ec.wait(self.r, c)
        inst = f(*a, **kw)
        return ec.inc(c, inst)

    def wait(self, name, val=None):
        self.ec.wait(self.r, name, val)

    def wait_future(self, name, val):
        self.ec.wait_future(self.r, name, val)


class VW(_EngW):
    CHAIN = "cDVE"

    def tt(self, out, a, b, op=ALU.mult):
        return self._do(self.r.tensor_tensor, out, a, b, op)

    def add(self, out, a, b):
        return self._do(self.r.tensor_add, out, a, b)

    def sub(self, out, a, b):
        return self._do(self.r.tensor_sub, out, a, b)

    def ts(self, out, a, s1, s2=None, op0=ALU.mult, op1=ALU.bypass,
           accum_out=None):
        return self._do(self.r.tensor_scalar, out, a, s1, s2, op0, op1,
                        accum_out=accum_out)

    def stt(self, out, in0, scalar, in1, op0=ALU.mult, op1=ALU.mult,
            accum_out=None):
        return self._do(self.r.scalar_tensor_tensor, out, in0, scalar, in1,
                        op0, op1, accum_out=accum_out)

    def red(self, out, in_, op=ALU.max, negate=False):
        return self._do(self.r.tensor_reduce, out, in_, AX.X, op, negate=negate)

    def recip(self, out, in_):
        return self._do(self.r.reciprocal, out, in_)

    def copy(self, out, in_):
        return self._do(self.r.tensor_copy, out, in_)

    def memset(self, ap, v):
        return self._do(self.r.memset, ap, v)


class SW(_EngW):
    CHAIN = "cACT"

    def act(self, out, in_, func, bias=0.0, scale=1.0, accum_out=None):
        return self._do(self.r.activation, out, in_, func, bias=bias,
                        scale=scale, accum_out=accum_out)

    def copy(self, out, in_):
        return self._do(self.r.copy, out, in_)

    def mul(self, out, in_, c):
        return self._do(self.r.mul, out, in_, c)


class GW(_EngW):
    CHAIN = "cPL"

    def tt(self, out, a, b, op=ALU.mult):
        return self._do(self.r.tensor_tensor, out, a, b, op)

    def copy(self, out, in_):
        return self._do(self.r.tensor_copy, out, in_)

    def memset(self, ap, v):
        return self._do(self.r.memset, ap, v)

    def dma(self, sem, out, in_):
        return self.ec.inc(sem, self.r.dma_start(out=out, in_=in_), 16)


class TW:
    CHAIN = "cPE"

    def __init__(self, ec, raw):
        self.ec = ec
        self.r = raw
        ec.sem(self.CHAIN)

    def mm(self, out, lhsT, rhs, start, stop, mark=False, skip=False):
        inst = self.r.matmul(out, lhsT, rhs, start=start, stop=stop,
                             skip_group_check=skip)
        if mark:
            return self.ec.inc(self.CHAIN, inst)
        return None

    def trans(self, out, in_, idn, mark=False):
        inst = self.r.transpose(out, in_, idn)
        if mark:
            return self.ec.inc(self.CHAIN, inst)
        return None

    def wait(self, name, val=None):
        self.ec.wait(self.r, name, val)

    def wait_future(self, name, val):
        self.ec.wait_future(self.r, name, val)




def _alloc(nc, st, sbufs, psums, prefix=""):
    d = {}
    for name, shape, dt in sbufs:
        d[name] = st.enter_context(
            nc.sbuf_tensor(prefix + name, list(shape), dt))
    for name, shape in psums:
        d[name] = st.enter_context(
            nc.psum_tensor(prefix + name, list(shape), F32))
    return d


def ap3(t, off, dims):
    return bass.AP(t.tensor, t.offset + off, dims)


def emit_ln(ec, xbf_sl, sq_scr, ones_c, ones_r, ps_stats, ps_bc, sums_bf,
            mu_b, rstd_b, tmp_b, std_b, want_aug=False, n_tok=N,
            dep=None):
    blk = ec.blk
    nch = n_tok // 512
    sq_tok = [None] * NDK
    st_tok = [None]
    cons_tok = [None] * NDK

    def pool_fn(g):
        G = GW(ec, g)
        for d_ in (dep or []):
            G.wait_future(*d_) if d_[1] > ec.cnt.get(d_[0], 0) else G.wait(*d_)
        for dk in range(NDK):
            if dk >= 2:
                G.wait_future("cPE", cons_tok[dk - 2])
            x = xbf_sl(dk)
            sq_tok[dk] = G.tt(
                sq_scr[:, (dk % 2) * n_tok:(dk % 2) * n_tok + n_tok],
                x, x, ALU.mult)
    pe0 = ec.cnt["cPE"] if "cPE" in ec.cnt else 0
    ec.sem("cPE")
    pe0 = ec.cnt["cPE"]
    for dk in range(NDK):
        cons_tok[dk] = pe0 + dk + 1
    blk.gpsimd(pool_fn)

    def t_fn(t):
        T = TW(ec, t)
        for dk in range(NDK):
            T.wait("cPL", sq_tok[dk])
            for ncc in range(nch):
                sl = slice(ncc * 512, ncc * 512 + 512)
                t.matmul(ps_stats[0:1, sl], ones_c[:, :], xbf_sl(dk)[:, sl],
                         start=(dk == 0), stop=(dk == NDK - 1))
                mark = (ncc == nch - 1)
                tok = T.mm(ps_stats[32:33, sl], ones_c[:, :],
                           sq_scr[:, (dk % 2) * n_tok + ncc * 512:
                                  (dk % 2) * n_tok + ncc * 512 + 512],
                           start=(dk == 0), stop=(dk == NDK - 1), mark=mark)
                if mark:
                    assert tok == cons_tok[dk], (tok, cons_tok[dk])
        st_tok[0] = cons_tok[NDK - 1]
    blk.tensor(t_fn)

    hi_tok = [None] * nch
    lo_tok = [None] * nch
    bc_tok = [None] * nch
    be_tok = [None] * nch
    pe_now = ec.cnt["cPE"]
    for i in range(nch):
        bc_tok[i] = pe_now + i + 1

    def s_hilo(s):
        S = SW(ec, s)
        for ncc in range(nch):
            sl = slice(ncc * 512, ncc * 512 + 512)
            if ncc == 0:
                S.wait("cPE", st_tok[0])
            S.copy(sums_bf[0:1, 0:512], ps_stats[0:1, sl])
            hi_tok[ncc] = S.copy(sums_bf[0:1, 1024:1536], ps_stats[32:33, sl])
            S.wait_future("cPE", bc_tok[ncc])
            S.copy(mu_b[:, sl], ps_bc[:, 0:512])
            be_tok[ncc] = S.copy(std_b[:, sl], ps_bc[:, 512:1024])
    blk.scalar(s_hilo)

    def v_lo(v):
        V = VW(ec, v)
        for ncc in range(nch):
            sl = slice(ncc * 512, ncc * 512 + 512)
            V.wait_future("cACT", hi_tok[ncc])
            V.sub(sums_bf[0:1, 512:1024], ps_stats[0:1, sl],
                  sums_bf[0:1, 0:512])
            lo_tok[ncc] = V.sub(sums_bf[0:1, 1536:2048], ps_stats[32:33, sl],
                                sums_bf[0:1, 1024:1536])
    blk.vector(v_lo)

    def t_bc(t):
        T = TW(ec, t)
        for ncc in range(nch):
            T.wait_future("cDVE", lo_tok[ncc])
            if ncc >= 1:
                T.wait_future("cACT", be_tok[ncc - 1])
            t.matmul(ps_bc[:, 0:512], ones_r[:, :], sums_bf[0:1, 0:512],
                     start=True, stop=False)
            t.matmul(ps_bc[:, 0:512], ones_r[:, :], sums_bf[0:1, 512:1024],
                     start=False, stop=True)
            t.matmul(ps_bc[:, 512:1024], ones_r[:, :], sums_bf[0:1, 1024:1536],
                     start=True, stop=False)
            tok = T.mm(ps_bc[:, 512:1024], ones_r[:, :],
                       sums_bf[0:1, 1536:2048], start=False, stop=True,
                       mark=True)
            assert tok == bc_tok[ncc], (tok, bc_tok[ncc])
    blk.tensor(t_bc)

    out_tok = {}

    def v_fin(v):
        V = VW(ec, v)
        V.wait_future("cACT", be_tok[nch - 1])
        V.ts(mu_b[:, :], mu_b[:, :], 1.0 / D)
        V.ts(std_b[:, :], std_b[:, :], 1.0 / D)
        V.tt(tmp_b[:, :], mu_b[:, :], mu_b[:, :], ALU.mult)
        V.sub(std_b[:, :], std_b[:, :], tmp_b[:, :])
        out_tok["var"] = V.ts(std_b[:, :], std_b[:, :], 1e-5, op0=ALU.add)
    blk.vector(v_fin)

    def s_sqrt(s):
        S = SW(ec, s)
        S.wait("cDVE", out_tok["var"])
        out_tok["std"] = S.act(std_b[:, :], std_b[:, :], AF.Sqrt)
    blk.scalar(s_sqrt)

    def v_fin2(v):
        V = VW(ec, v)
        V.wait("cACT", out_tok["std"])
        out_tok["rstd"] = V.recip(rstd_b[:, :], std_b[:, :])
        if want_aug:
            out_tok["aug"] = V.stt(tmp_b[:, :], mu_b[:, :], -1.0,
                                   rstd_b[:, :], ALU.mult, ALU.mult)
        out_tok["dn"] = ec.cnt["cDVE"]
    blk.vector(v_fin2)
    return out_tok


def emit_pre(nc, blk, ec, io, prefix="p_"):
    x_h = io["x"]
    xn_h = io["xn"]
    st_ = ExitStack()
    with st_:
        _sb = [
            ("xf", (128, NDK * N), F32),
            ("xb", (128, NDK * N), BF16),
            ("sq", (128, 2 * N), BF16),
            ("sums_bf", (1, 2048), BF16),
            ("mu_b", (128, N), F32),
            ("rstd_b", (128, N), F32),
            ("tmp_b", (128, N), F32),
            ("std_b", (128, N), F32),
            ("ones_c", (128, 1), BF16),
            ("ones_r", (1, 128), BF16),
            ("xt", (128, 2 * N), F32),
        ]
        _ps = [("ps_st", (128, 2048)), ("ps_bc", (128, 1024))]
        _t = _alloc(nc, st_, _sb, _ps, prefix=prefix)
        xf = _t["xf"]; xb = _t["xb"]; sqs = _t["sq"]
        sums_bf = _t["sums_bf"]; mu_b = _t["mu_b"]; rstd_b = _t["rstd_b"]
        tmp_b = _t["tmp_b"]; std_b = _t["std_b"]; ones_c = _t["ones_c"]
        ones_r = _t["ones_r"]; xt = _t["xt"]
        ps_st = _t["ps_st"]; ps_bc = _t["ps_bc"]
        for s in ("ld", "ldb", "st0", "st1"):
            ec.sem(s)

        def g_load(g):
            g.memset(ones_c[:, :], 1.0)
            g.memset(ones_r[:, :], 1.0)
            for dk in range(NDK):
                ec.inc("ldb", g.dma_start(
                    out=xb[:, dk * N:(dk + 1) * N],
                    in_=x_h[dk * 128:(dk + 1) * 128, :]), 16)
        blk.gpsimd(g_load)

        def s_load(s):
            for dk in range(NDK):
                ec.inc("ld", s.dma_start(
                    out=xf[:, dk * N:(dk + 1) * N],
                    in_=x_h[dk * 128:(dk + 1) * 128, :]), 16)
        blk.sync(s_load)

        tok = emit_ln(ec,
                      xbf_sl=lambda dk: xb[:, dk * N:(dk + 1) * N],
                      sq_scr=sqs, ones_c=ones_c, ones_r=ones_r,
                      ps_stats=ps_st, ps_bc=ps_bc, sums_bf=sums_bf,
                      mu_b=mu_b, rstd_b=rstd_b, tmp_b=tmp_b, std_b=std_b,
                      dep=[("ldb", NDK * 16)])

        xn_tok = [None] * NDK

        def v_xn(v):
            V = VW(ec, v)
            V.wait("cDVE", tok["dn"])
            V.wait("ld")
            for dk in range(NDK):
                b = (dk % 2) * N
                if dk >= 2:
                    V.wait_future("st%d" % (dk % 2), (dk // 2) * 16)
                V.sub(xt[:, b:b + N], xf[:, dk * N:(dk + 1) * N], mu_b[:, :])
                xn_tok[dk] = V.tt(xt[:, b:b + N], xt[:, b:b + N],
                                  rstd_b[:, :], ALU.mult)
        blk.vector(v_xn)

        def s_store(s):
            for dk in range(NDK):
                b = (dk % 2) * N
                ec.wait(s, "cDVE", xn_tok[dk])
                ec.inc("st%d" % (dk % 2), s.dma_start(
                    out=xn_h[dk * 128:(dk + 1) * 128, :],
                    in_=xt[:, b:b + N]), 16)
            ec.wait(s, "st0")
            ec.wait(s, "st1")
        blk.sync(s_store)


def build_pre():
    nc = bass.Bass("TRN2", target_bir_lowering=False)
    io = {"x": nc.dram_tensor("x", [D, N], F32, kind="ExternalInput").ap(),
          "xn": nc.dram_tensor("xn", [D, N], F32,
                               kind="ExternalOutput").ap()}
    with nc.Block() as blk:
        ec = EC(nc, blk)
        emit_pre(nc, blk, ec, io)
        ec.close()
    return nc


_L0A_IO = (("xn", [D, N], F32), ("wq", [D, D], F32), ("wk", [D, D], F32),
           ("wv", [D, D], F32), ("wo", [D, D], F32),
           ("bq", [128, NDK], F32), ("bk", [128, NDK], F32),
           ("bv", [1, D], F32), ("bo", [128, NDK], F32),
           ("lam", [1], F32), ("mask8", [128, 256], F32),
           ("ind16", [128, NDK * 16], BF16), ("idnf", [16, 16], F32))


def emit_l0attn(nc, blk, ec, io, prefix="a_", nits=32, no_mask=False,
                no_rd=False, no_pvd=False, no_exp=False, no_rdb=False):
    xn_h = io["xn"]; wq_h = io["wq"]; wk_h = io["wk"]; wv_h = io["wv"]
    wo_h = io["wo"]; bq_h = io["bq"]; bk_h = io["bk"]; bv_h = io["bv"]
    bo_h = io["bo"]; lam_h = io["lam"]; m8_h = io["mask8"]
    ind_h = io["ind16"]; idnf_h = io["idnf"]; ao_h = io["ao"]
    FMAX = host_consts()["f_max"]

    st_ = ExitStack()
    with st_:
        _sb = [
            ("sc32", (128, NDK * N), BF16),
            ("q_sb", (128, NDK * N), BF16),
            ("k_sb", (128, NDK * N), BF16),
            ("v_sb", (128, NQT * 1024), BF16),
            ("war", (128, NDK * 2048), BF16),
            ("wvr", (128, NDK * 1024), BF16),
            ("q2", (128, 2 * N), BF16),
            ("eAB", (128, 4 * 512), BF16),
            ("rds", (1, 1024), F32),
            ("rds_bf", (1, 2048), BF16),
            ("hob", (1, 256), BF16),
            ("rdb", (128, 512), F32),
            ("out_t", (128, 512), F32),
            ("vb_b", (128, 1024), BF16),
            ("bq4", (128, NDK), F32),
            ("bk4", (128, NDK), F32),
            ("bo4", (128, NDK), F32),
            ("bvr", (1, D), BF16),
            ("lamc", (128, 1), F32),
            ("maskl", (128, 256), F32),
            ("nm", (16, 8), F32),
            ("cnrow", (1, 16), BF16),
            ("cneg_b", (128, 16), F32),
            ("idn16", (16, 16), F32),
            ("ind_s", (128, NDK * 16), BF16),
            ("ones_c", (128, 1), BF16),
            ("ones_r", (1, 128), BF16),
        ]
        _ps = [
            ("psA", (128, 1024)),
            ("psB", (128, 1024)),
            ("psC", (128, 1024)),
            ("psE", (128, 1024)),
        ]
        _t = _alloc(nc, st_, _sb, _ps, prefix=prefix)
        (sc32, q_sb, k_sb, v_sb, war, wvr, q2, eAB, rds, rds_bf, hob, rdb, out_t, vb_b, bq4, bk4, bo4, bvr, lamc, maskl, nm, cnrow, cneg_b, idn16, ind_s, ones_c, ones_r, psA, psB, psC, psE,) = (_t["sc32"], _t["q_sb"], _t["k_sb"], _t["v_sb"], _t["war"], _t["wvr"], _t["q2"], _t["eAB"], _t["rds"], _t["rds_bf"], _t["hob"], _t["rdb"], _t["out_t"], _t["vb_b"], _t["bq4"], _t["bk4"], _t["bo4"], _t["bvr"], _t["lamc"], _t["maskl"], _t["nm"], _t["cnrow"], _t["cneg_b"], _t["idn16"], _t["ind_s"], _t["ones_c"], _t["ones_r"], _t["psA"], _t["psB"], _t["psC"], _t["psE"],)
        for sname in ("ld", "ldg", "ldw", "ldwo", "st0", "st1"):
            ec.sem(sname)
        tok = {}

        def g_load(g):
            G = GW(ec, g)
            G.memset(ones_c[:, :], 1.0)
            G.memset(ones_r[:, :], 1.0)
            tok["hob0"] = G.memset(hob[:, :], 0.0)
            for dk in range(NDK):
                G.dma("ldg", sc32[:, dk * N:(dk + 1) * N],
                      xn_h[dk * 128:(dk + 1) * 128, :])
            for (dst, src, off) in ((war, wq_h, 0), (war, wk_h, NDK * 1024),
                                    (wvr, wv_h, 0)):
                for dk in range(NDK):
                    G.dma("ldw", dst[:, off + dk * 1024:off + (dk + 1) * 1024],
                          src[dk * 128:(dk + 1) * 128, :])
            G.dma("ldg", bvr[:, :], bv_h[:, :])
            G.dma("ldw", ind_s[:, :], ind_h[:, :])
        blk.gpsimd(g_load)

        def s_load(s):
            for (dst, src) in ((bq4, bq_h), (bk4, bk_h), (bo4, bo_h),
                               (maskl, m8_h)):
                ec.inc("ld", s.dma_start(out=dst[:, :], in_=src[:, :]), 16)
            lsrc = bass.AP(lam_h.tensor, 0, [[0, 128], [1, 1]])
            ec.inc("ld", s.dma_start(out=lamc[:, :], in_=lsrc), 16)
            ec.inc("ld", s.dma_start(out=idn16[:, :], in_=idnf_h[:, :]), 16)
        blk.sync(s_load)

        def v_init(v):
            V = VW(ec, v)
            V.wait("cPL", tok["hob0"])
            V.memset(hob[0:1, 0:64], 1.0)
            tok["hob"] = V.memset(hob[0:1, 192:256], 1.0)
            V.wait("ld")
            tok["maskl"] = V.ts(maskl[:, :], maskl[:, :], lamc[:, 0:1],
                                op0=ALU.mult)
        blk.vector(v_init)

        def t_proj(t):
            T = TW(ec, t)
            T.wait("ld")
            T.wait("ldg")
            T.wait("ldw")
            for idx in range(64):
                ti, oc, ncc = idx // 32, (idx // NC4) % NDK, idx % NC4
                woff = ti * NDK * 1024
                pp = idx % 2
                bank = psA[:, pp * 512:pp * 512 + 512]
                if idx >= 2:
                    T.wait_future("cACT", tok[("qk_base",)] + idx - 1)
                for dk in range(NDK):
                    T.mm(bank,
                         war[:, woff + dk * 1024 + oc * 128:
                             woff + dk * 1024 + oc * 128 + 128],
                         sc32[:, dk * N + ncc * 512:dk * N + ncc * 512 + 512],
                         start=(dk == 0), stop=(dk == NDK - 1),
                         mark=(dk == NDK - 1))
        tok[("qk_base",)] = ec.cnt["cACT"]
        pe0 = ec.cnt["cPE"]
        blk.tensor(t_proj)

        def s_proj(s):
            S = SW(ec, s)
            for idx in range(64):
                ti, oc, ncc = idx // 32, (idx // NC4) % NDK, idx % NC4
                dst = q_sb if ti == 0 else k_sb
                b4 = bq4 if ti == 0 else bk4
                pp = idx % 2
                bank = psA[:, pp * 512:pp * 512 + 512]
                S.wait("cPE", pe0 + idx + 1)
                t2 = S.act(dst[:, oc * N + ncc * 512:oc * N + ncc * 512 + 512],
                           bank, AF.Identity, bias=b4[:, oc:oc + 1])
                assert t2 == tok[("qk_base",)] + idx + 1
        blk.scalar(s_proj)
        qk_done = ec.cnt["cACT"]

        def t_vb(t):
            T = TW(ec, t)
            T.wait("ldg")
            for half in range(2):
                tok[("vbb", half)] = T.mm(
                    psB[:, half * 512:half * 512 + 512], ones_r[:, :],
                    bvr[0:1, half * 512:half * 512 + 512],
                    start=True, stop=True, mark=True)
        blk.tensor(t_vb)

        def s_vb(s):
            S = SW(ec, s)
            for half in range(2):
                S.wait("cPE", tok[("vbb", half)])
                tok[("vbe", half)] = S.copy(
                    vb_b[:, half * 512:half * 512 + 512],
                    psB[:, half * 512:half * 512 + 512])
        blk.scalar(s_vb)

        vev0 = ec.cnt["cDVE"]

        def t_vproj(t):
            T = TW(ec, t)
            T.wait("cACT", tok[("vbe", 1)])
            for idx in range(32):
                nt, vc = idx // 2, idx % 2
                pp = idx % 2
                bank = psB[:, pp * 512:pp * 512 + 512]
                if idx >= 2:
                    T.wait_future("cDVE", vev0 + idx - 1)
                for dk in range(NDK):
                    T.mm(bank,
                         sc32[:, dk * N + nt * 128:dk * N + nt * 128 + 128],
                         wvr[:, dk * 1024 + vc * 512:dk * 1024 + vc * 512 + 512],
                         start=(dk == 0), stop=(dk == NDK - 1),
                         mark=(dk == NDK - 1))
        pe_v0 = ec.cnt["cPE"]
        blk.tensor(t_vproj)

        def v_vproj(v):
            V = VW(ec, v)
            for idx in range(32):
                nt, vc = idx // 2, idx % 2
                pp = idx % 2
                bank = psB[:, pp * 512:pp * 512 + 512]
                V.wait("cPE", pe_v0 + idx + 1)
                t2 = V.tt(v_sb[:, nt * 1024 + vc * 512:nt * 1024 + vc * 512 + 512],
                          bank, vb_b[:, vc * 512:vc * 512 + 512], ALU.add)
                assert t2 == vev0 + idx + 1
        blk.vector(v_vproj)
        v_done = ec.cnt["cDVE"]

        for ti in range(2):
            src = q_sb if ti == 0 else k_sb
            sq_tok = [None] * NDK
            cons_base = ec.cnt["cPE"]

            def g_sq(g, src=src, ti=ti, sq_tok=sq_tok, cons_base=cons_base):
                G = GW(ec, g)
                if ti == 0:
                    G.wait("ldw")
                    G.wait_future("cACT", qk_done)
                for dk in range(NDK):
                    if dk >= 2 or ti == 1:
                        G.wait_future("cPE", cons_base + dk - 1)
                    x = src[:, dk * N:(dk + 1) * N]
                    sq_tok[dk] = G.tt(q2[:, (dk % 2) * N:(dk % 2) * N + N],
                                      x, x, ALU.mult)
            blk.gpsimd(g_sq)

            def t_nm(t, ti=ti, sq_tok=sq_tok, cons_base=cons_base):
                T = TW(ec, t)
                if ti == 0:
                    T.wait_future("cDVE", v_done)
                else:
                    T.wait("cDVE", tok[("nm", 0)])
                for dk in range(NDK):
                    T.wait("cPL", sq_tok[dk])
                    for ncc in range(NC4):
                        ps = psA if ncc < 2 else psB
                        col = (ncc % 2) * 512
                        t2 = T.mm(ps[0:16, col:col + 512],
                                  ind_s[:, dk * 16:dk * 16 + 16],
                                  q2[:, (dk % 2) * N + ncc * 512:
                                     (dk % 2) * N + ncc * 512 + 512],
                                  start=(dk == 0), stop=(dk == NDK - 1),
                                  mark=(ncc == NC4 - 1))
                        if ncc == NC4 - 1:
                            assert t2 == cons_base + dk + 1
            blk.tensor(t_nm)

            def v_nm(v, ti=ti):
                V = VW(ec, v)
                V.wait("cPE", ec.cnt["cPE"])
                V.red(nm[0:16, 4:5], psA[0:16, 0:1024], op=ALU.max)
                V.red(nm[0:16, 5:6], psB[0:16, 0:1024], op=ALU.max)
                tok[("nm", ti)] = V.tt(nm[0:16, ti:ti + 1], nm[0:16, 4:5],
                                       nm[0:16, 5:6], ALU.max)
            blk.vector(v_nm)
        def s_nm(s):
            S = SW(ec, s)
            S.wait("cDVE", tok[("nm", 1)])
            tok["nms"] = S.act(nm[0:16, 0:2], nm[0:16, 0:2], AF.Sqrt)
        blk.scalar(s_nm)

        def v_c(v):
            V = VW(ec, v)
            V.wait("cACT", tok["nms"])
            V.tt(nm[0:16, 2:3], nm[0:16, 0:1], nm[0:16, 1:2], ALU.mult)
            V.ts(nm[0:16, 3:4], lamc[0:16, 0:1], FMAX, op0=ALU.mult)
            V.stt(nm[0:16, 2:3], nm[0:16, 2:3], 0.125, nm[0:16, 3:4],
                  ALU.mult, ALU.add)
            tok["c"] = V.ts(nm[0:16, 2:3], nm[0:16, 2:3], -1.0, op0=ALU.mult)
        blk.vector(v_c)

        def t_cn(t):
            T = TW(ec, t)
            T.wait("ld")
            T.wait("cDVE", tok["c"])
            tok["ct"] = T.trans(psE[0:1, 0:16], nm[0:16, 2:3], idn16[:, :],
                                mark=True)
        blk.tensor(t_cn)

        def s_cn(s):
            S = SW(ec, s)
            S.wait("cPE", tok["ct"])
            tok["cr"] = S.copy(cnrow[0:1, :], psE[0:1, 0:16])
        blk.scalar(s_cn)

        def t_cb(t):
            T = TW(ec, t)
            T.wait("cACT", tok["cr"])
            tok["cb"] = T.mm(psE[:, 512:528], ones_r[:, :], cnrow[0:1, :],
                             start=True, stop=True, mark=True)
        blk.tensor(t_cb)

        def s_cb(s):
            S = SW(ec, s)
            S.wait("cPE", tok["cb"])
            tok["cbe"] = S.copy(cneg_b[:, :], psE[:, 512:528])
        blk.scalar(s_cb)

        def band_ovl(kt, qc):
            c0 = max(kt * 128 - 61, qc * 512, 0)
            c1 = min(kt * 128 + 189, qc * 512 + 512, N)
            return None if c0 >= c1 else (c0, c1)

        ex_base = ec.cnt["cACT"]

        def exp_tok(it, kt, j):
            return ex_base + it * 33 + kt * 2 + j + 1

        def rdbe_tok(it):
            return ex_base + it * 33 + 33

        qk_tok = {}
        mk_tok = {}
        pv_tok = {}
        rd_tok = {}
        rdb_tok = {}
        evac_tok = {}

        for it in range(nits):
            hp, qc = it // 4, it % 4

            def t_main(t, hp=hp, qc=qc, it=it):
                T = TW(ec, t)
                if it == 0:
                    T.wait("cDVE", v_done)
                    T.wait("cACT", tok["cbe"])
                    T.wait("cDVE", tok["hob"])
                    T.wait("cDVE", tok["maskl"])
                else:
                    T.wait_future("cDVE", evac_tok[it - 1])

                def pvd(kt):
                    pp = kt % 2
                    eA = eAB[:, pp * 512:pp * 512 + 512]
                    eB = eAB[:, 1024 + pp * 512:1024 + pp * 512 + 512]
                    pv = psC[:, 0:512]
                    psD = psC[:, 512:1024]
                    T.wait_future("cACT", exp_tok(it, kt, 1))
                    if no_pvd:
                        pv_tok[(it, kt)] = ec.inc("cPE", T.r.nop())
                        return
                    st = (kt == 0)
                    sp = (kt == NQT - 1)
                    T.mm(pv[0:64, :],
                         v_sb[:, kt * 1024 + (2 * hp) * 64:
                              kt * 1024 + (2 * hp) * 64 + 64],
                         eA, start=st, stop=sp, skip=True)
                    T.mm(pv[64:128, :],
                         v_sb[:, kt * 1024 + (2 * hp + 1) * 64:
                              kt * 1024 + (2 * hp + 1) * 64 + 64],
                         eB, start=st, stop=sp, skip=True)
                    T.mm(psD[0:1, :], ones_c[:, :], eA, start=st, stop=sp,
                         skip=True)
                    pv_tok[(it, kt)] = T.mm(psE[0:1, 512:1024], ones_c[:, :],
                                            eB, start=st, stop=sp, mark=True,
                                            skip=True)

                for kt in range(NQT):
                    pp = kt % 2
                    scA = psA[:, pp * 512:pp * 512 + 512]
                    scB = psB[:, pp * 512:pp * 512 + 512]
                    if it > 0 or kt >= 2:
                        pit, pkt = (it, kt - 2) if kt >= 2 else \
                                   (it - 1, NQT - 2 + kt)
                        T.wait_future("cACT", exp_tok(pit, pkt, 1))
                    T.mm(scA, k_sb[0:64, hp * N + kt * 128:hp * N + kt * 128 + 128],
                         q_sb[0:64, hp * N + qc * 512:hp * N + qc * 512 + 512],
                         start=True, stop=True)
                    qk_tok[(it, kt)] = T.mm(
                        scB, k_sb[64:128, hp * N + kt * 128:hp * N + kt * 128 + 128],
                        q_sb[64:128, hp * N + qc * 512:hp * N + qc * 512 + 512],
                        start=True, stop=True, mark=True)
                    if kt >= 1:
                        pvd(kt - 1)
                pvd(NQT - 1)
            blk.tensor(t_main)

            def v_mask(v, hp=hp, qc=qc, it=it):
                V = VW(ec, v)
                for kt in range(NQT):
                    ov = band_ovl(kt, qc)
                    if ov is None or no_mask:
                        continue
                    pp = kt % 2
                    scA = psA[:, pp * 512:pp * 512 + 512]
                    scB = psB[:, pp * 512:pp * 512 + 512]
                    c0, c1 = ov
                    lo, hi = c0 - qc * 512, c1 - qc * 512
                    t0 = c0 - (kt * 128 - 61)
                    V.wait("cPE", qk_tok[(it, kt)])
                    V.add(scA[:, lo:hi], scA[:, lo:hi],
                          maskl[:, t0:t0 + (hi - lo)])
                    mk_tok[(it, kt)] = V.add(scB[:, lo:hi], scB[:, lo:hi],
                                             maskl[:, t0:t0 + (hi - lo)])
            blk.vector(v_mask)

            def s_exp(s, hp=hp, qc=qc, it=it):
                S = SW(ec, s)
                for kt in range(NQT):
                    pp = kt % 2
                    scA = psA[:, pp * 512:pp * 512 + 512]
                    scB = psB[:, pp * 512:pp * 512 + 512]
                    eA = eAB[:, pp * 512:pp * 512 + 512]
                    eB = eAB[:, 1024 + pp * 512:1024 + pp * 512 + 512]
                    S.wait("cPE", qk_tok[(it, kt)])
                    if (it, kt) in mk_tok:
                        S.wait("cDVE", mk_tok[(it, kt)])
                    if it > 0 or kt >= 2:
                        pit, pkt = (it, kt - 2) if kt >= 2 else \
                                   (it - 1, NQT - 2 + kt)
                        S.wait("cPE", pv_tok[(pit, pkt)])
                    if no_exp:
                        t2 = S.copy(eA, scA)
                        t2 = S.copy(eB, scB)
                    else:
                        t2 = S.act(eA, scA, AF.Exp,
                                   bias=cneg_b[:, 2 * hp:2 * hp + 1],
                                   scale=0.125)
                        assert t2 == exp_tok(it, kt, 0)
                        t2 = S.act(eB, scB, AF.Exp,
                                   bias=cneg_b[:, 2 * hp + 1:2 * hp + 2],
                                   scale=0.125)
                        assert t2 == exp_tok(it, kt, 1)
            blk.scalar(s_exp)

            def v_rd(v, it=it):
                V = VW(ec, v)
                V.wait("cPE", pv_tok[(it, NQT - 1)])
                if no_rd:
                    rd_tok[it] = V.memset(rds_bf[0:1, 0:2048], 0.001)
                    return
                V.recip(rds[0:1, 0:512], psC[0:1, 512:1024])
                V.recip(rds[0:1, 512:1024], psE[0:1, 512:1024])
                V.copy(rds_bf[0:1, 0:512], rds[0:1, 0:512])
                V.sub(rds_bf[0:1, 512:1024], rds[0:1, 0:512],
                      rds_bf[0:1, 0:512])
                V.copy(rds_bf[0:1, 1024:1536], rds[0:1, 512:1024])
                rd_tok[it] = V.sub(rds_bf[0:1, 1536:2048],
                                   rds[0:1, 512:1024],
                                   rds_bf[0:1, 1024:1536])
            blk.vector(v_rd)

            def t_rdb(t, it=it):
                T = TW(ec, t)
                T.wait("cDVE", rd_tok[it])
                if it > 0:
                    T.wait("cACT", rdbe_tok(it - 1))
                psRD = psE[:, 0:512]
                if no_rdb:
                    rdb_tok[it] = ec.inc("cPE", T.r.nop())
                    return
                T.mm(psRD, hob[0:1, 0:128], rds_bf[0:1, 0:512],
                     start=True, stop=False)
                T.mm(psRD, hob[0:1, 0:128], rds_bf[0:1, 512:1024],
                     start=False, stop=False)
                T.mm(psRD, hob[0:1, 128:256], rds_bf[0:1, 1024:1536],
                     start=False, stop=False)
                rdb_tok[it] = T.mm(psRD, hob[0:1, 128:256],
                                   rds_bf[0:1, 1536:2048],
                                   start=False, stop=True, mark=True)
            blk.tensor(t_rdb)

            def s_rdb(s, it=it):
                S = SW(ec, s)
                S.wait("cPE", rdb_tok[it])
                t2 = S.copy(rdb[:, :], psE[:, 0:512])
                assert t2 == rdbe_tok(it), (t2, rdbe_tok(it))
            blk.scalar(s_rdb)

            def v_evac(v, hp=hp, qc=qc, it=it):
                V = VW(ec, v)
                V.wait("cACT", rdbe_tok(it))
                evac_tok[it] = V.tt(
                    sc32[:, hp * N + qc * 512:hp * N + qc * 512 + 512],
                    psC[:, 0:512], rdb[:, :], ALU.mult)
            blk.vector(v_evac)

        def g_wo(g):
            G = GW(ec, g)
            for dk in range(NDK):
                G.dma("ldwo", war[:, dk * 1024:(dk + 1) * 1024],
                      wo_h[dk * 128:(dk + 1) * 128, :])
        blk.gpsimd(g_wo)

        wo_e = {}
        wo_base = ec.cnt["cACT"]
        pe_wo0 = ec.cnt["cPE"]

        def t_wo(t):
            T = TW(ec, t)
            T.wait("ldwo")
            if nits:
                T.wait("cDVE", evac_tok[nits - 1])
            for idx in range(32):
                oc, ncc = idx // NC4, idx % NC4
                pp = idx % 2
                bank = psA[:, pp * 512:pp * 512 + 512]
                if idx >= 2:
                    T.wait_future("cACT", wo_base + idx - 1)
                for dk in range(NDK):
                    T.mm(bank,
                         war[:, dk * 1024 + oc * 128:dk * 1024 + oc * 128 + 128],
                         sc32[:, dk * N + ncc * 512:dk * N + ncc * 512 + 512],
                         start=(dk == 0), stop=(dk == NDK - 1),
                         mark=(dk == NDK - 1))
        blk.tensor(t_wo)

        def s_wo(s):
            S = SW(ec, s)
            for idx in range(32):
                oc, ncc = idx // NC4, idx % NC4
                pp = idx % 2
                bank = psA[:, pp * 512:pp * 512 + 512]
                S.wait("cPE", pe_wo0 + idx + 1)
                if idx >= 1:
                    S.wait_future("st0", 16 * idx)
                t2 = S.act(out_t[:, 0:512],
                           bank, AF.Identity, bias=bo4[:, oc:oc + 1])
                wo_e[idx] = t2
        blk.scalar(s_wo)

        def sy_st(s):
            for idx in range(32):
                oc, ncc = idx // NC4, idx % NC4
                ec.wait(s, "cACT", wo_e[idx])
                ec.inc("st0", s.dma_start(
                    out=ao_h[oc * 128:(oc + 1) * 128, ncc * 512:ncc * 512 + 512],
                    in_=out_t[:, 0:512]), 16)
            ec.wait(s, "st0")
        blk.sync(sy_st)


def build_l0attn(**kw):
    nc = bass.Bass("TRN2", target_bir_lowering=False)
    io = {n: nc.dram_tensor(n, sh, dt, kind="ExternalInput").ap()
          for (n, sh, dt) in _L0A_IO}
    io["ao"] = nc.dram_tensor("ao", [D, N], F32, kind="ExternalOutput").ap()
    with nc.Block() as blk:
        ec = EC(nc, blk)
        emit_l0attn(nc, blk, ec, io, **kw)
        ec.close()
    return nc


def _run(nc, in_maps, n):
    return run_bass_kernel_spmd(nc, in_maps, core_ids=list(range(n)))


NJP = 22
JW = [128] * 21 + [42]


def inner_consts():
    hc = host_consts()
    seln = np.zeros((128, 2), np.float32)
    seln[0:64, 0] = 1.0
    seln[64:128, 1] = 1.0
    seld = np.zeros((128, 4), np.float32)
    seld[:, 0] = 1.0
    seld[:, 3] = 1.0
    sel2 = np.zeros((2, 128), np.float32)
    sel2[0, 0:64] = 1.0
    sel2[1, 64:128] = 1.0
    rot4 = np.concatenate([hc["rot_cs"], hc["rot_ss"], hc["rot_ci"],
                           hc["rot_si"]], axis=1).astype(np.float32)
    return {
        "selN": seln.astype(nbf), "selD": seld.astype(nbf),
        "sel2": sel2.astype(nbf), "rot4": rot4,
        "m01": hc["mask01"], "idnf": hc["idn_f32"],
        "onesN": np.ones((1, N), np.float32).astype(nbf),
    }


def inner_inmap(i, z_fm, inp):
    ic = inner_consts()
    b1 = np.asarray(inp["L0_ffn_b1"][i], np.float32)
    b1a = np.zeros((128, NJP), np.float32)
    b1g = np.zeros((128, NJP), np.float32)
    for jp in range(NJP):
        w = JW[jp]
        b1a[0:w, jp] = b1[jp * 128:jp * 128 + w]
        b1g[0:w, jp] = b1[DI + jp * 128:DI + jp * 128 + w]
    gcol = percol(inp["L0_ffn_ln_g"][i])
    bcol = percol(inp["L0_ffn_ln_b"][i])
    gb = np.zeros((128, 2 * NDK), np.float32)
    gb[:, 0::2] = gcol
    gb[:, 1::2] = bcol
    return {
        "z": np.ascontiguousarray(z_fm, np.float32),
        "wqkv": np.ascontiguousarray(inp["L0_ffn_Wqkv"][i], np.float32),
        "wo": np.ascontiguousarray(inp["L0_ffn_Wo"][i], np.float32),
        "qs": np.tile(np.asarray(inp["L0_ffn_q_scale"][i], np.float32),
                      2).reshape(128, 1),
        "ks": np.tile(np.asarray(inp["L0_ffn_k_scale"][i], np.float32),
                      2).reshape(128, 1),
        "g8": gcol, "gb": gb,
        "w1": np.ascontiguousarray(inp["L0_ffn_W1"][i], np.float32),
        "b1a": b1a, "b1g": b1g,
        "w2": np.ascontiguousarray(inp["L0_ffn_W2"][i], np.float32),
        "b28": percol(inp["L0_ffn_b2"][i]),
        "rot4": ic["rot4"], "m01": ic["m01"], "sel2": ic["sel2"],
        "selD": ic["selD"], "selN": ic["selN"], "idnf": ic["idnf"],
        "onesN": ic["onesN"],
    }


def _inner_io(nc, nlayers):
    specs = {"wqkv": ([D, 3 * D], F32), "wo": ([D, D], F32),
             "qs": ([128, 1], F32), "ks": ([128, 1], F32),
             "g8": ([128, NDK], F32), "gb": ([128, 2 * NDK], F32),
             "w1": ([D, W1C], F32), "b1a": ([128, NJP], F32),
             "b1g": ([128, NJP], F32), "w2": ([DI, D], F32),
             "b28": ([128, NDK], F32)}
    io = {}
    for n, (sh, dt) in specs.items():
        t = nc.dram_tensor("in_" + n, [nlayers * sh[0], sh[1]], dt,
                           kind="ExternalInput").ap()
        io[n] = [t[i * sh[0]:(i + 1) * sh[0], :] for i in range(nlayers)]
    for n, sh, dt in (("rot4", [128, 4 * N], F32), ("m01", [128, 384], BF16),
                      ("sel2", [2, 128], BF16), ("selD", [128, 4], BF16),
                      ("selN", [128, 2], BF16), ("idnf", [128, 128], F32),
                      ("onesN", [1, N], BF16)):
        io[n] = nc.dram_tensor("in_" + n, sh, dt, kind="ExternalInput").ap()
    return io


def inner_chain_inmap(z_fm, inp, nlayers=IDEPTH):
    ms = [inner_inmap(i, z_fm if i == 0 else np.zeros((D, N), np.float32),
                      inp) for i in range(nlayers)]
    out = {"in_" + n: np.concatenate([m[n] for m in ms], axis=0)
           for n in ("wqkv", "wo", "qs", "ks", "g8", "gb", "w1", "b1a",
                     "b1g", "w2", "b28")}
    for n in ("rot4", "m01", "sel2", "selD", "selN", "idnf", "onesN"):
        out["in_" + n] = ms[0][n]
    out["z"] = np.ascontiguousarray(z_fm, np.float32)
    return out


def emit_inner_chain(nc, blk, ec, io, nlayers=IDEPTH, dump=None,
                     prefix="i_"):
    dbg_h = io.get("dbg")
    z_h = io["z"]
    rot_h = io["rot4"]; m01_h = io["m01"]; s2_h = io["sel2"]
    sD_h = io["selD"]; sN_h = io["selN"]; idf_h = io["idnf"]
    on_h = io["onesN"]
    zo_h = io["zo"]

    st_ = ExitStack()
    with st_:
        _sb = [
            ("z_s", (128, NDK * 2049), F32),
            ("arA", (128, NDK * N), BF16),
            ("arK", (128, NDK * N), BF16),
            ("arV", (128, 16384), BF16),
            ("arW", (128, 8192), BF16),
            ("zbt", (128, NDK * 513), BF16),
            ("q2", (128, N), BF16),
            ("rows3", (2, 3 * N), BF16),
            ("mcols", (128, 32), F32),
            ("msb", (32, 128), BF16),
            ("stb", (2, 256), BF16),
            ("rdb", (128, 512), BF16),
            ("eT", (128, 768), BF16),
            ("m01_s", (128, 384), BF16),
            ("sel2_s", (2, 128), BF16),
            ("selD_s", (128, 4), BF16),
            ("selN_s", (128, 2), BF16),
            ("idnf_s", (128, 128), F32),
            ("qs_c", (128, 1), F32),
            ("ks_c", (128, 1), F32),
            ("g8_s", (128, NDK), F32),
            ("gb_s", (128, 2 * NDK), BF16),
            ("b1a_s", (128, NJP), F32),
            ("b1g_s", (128, NJP), F32),
            ("b28_s", (128, NDK), F32),
            ("ones_c", (128, 1), BF16),
            ("ones_r", (1, 128), BF16),
            ("zlead", (128, NDK * 4), F32),
        ]
        _ps = [("psX", (128, 1024)), ("psCtx", (128, 2048)),
               ("psD2", (128, 1024))]
        _t = _alloc(nc, st_, _sb, _ps, prefix=prefix)
        z_s = _t["z_s"]; arA = _t["arA"]; arK = _t["arK"]
        arV = _t["arV"]; arW = _t["arW"]; zbt = _t["zbt"]; q2 = _t["q2"]
        rows3 = _t["rows3"]; mcols = _t["mcols"]; msb = _t["msb"]
        stb = _t["stb"]; rdb = _t["rdb"]; eT = _t["eT"]
        m01 = _t["m01_s"]; sel2 = _t["sel2_s"]; selD = _t["selD_s"]
        selN = _t["selN_s"]; idnf = _t["idnf_s"]; qs_c = _t["qs_c"]
        ks_c = _t["ks_c"]; g8 = _t["g8_s"]; gb = _t["gb_s"]; b1a = _t["b1a_s"]
        b1g = _t["b1g_s"]; b28 = _t["b28_s"]; ones_c = _t["ones_c"]
        ones_r = _t["ones_r"]; zlead = _t["zlead"]
        psX = _t["psX"]; psCtx = _t["psCtx"]; psD2 = _t["psD2"]

        laug = rows3[0:2, 0:N]

        for sname in ("ldz", "ldc", "ldk", "ldw", "ldwA", "ldwB", "ldt",
                      "lds", "ldw1A", "ldw1B", "ldw2", "ldwoA", "ldwoB",
                      "st0"):
            ec.sem(sname)
        tok = {}
        for _li in range(nlayers):
            _emit_inner_layer(
                nc, blk, ec, io, _li, _li == 0, _li == nlayers - 1, dump,
                tok, _t, dbg_h, z_h, rot_h, m01_h, s2_h, sD_h, sN_h,
                idf_h, on_h, zo_h)


def _emit_inner_layer(nc, blk, ec, io, li, first, last, dump, tok, _t,
                      dbg_h, z_h, rot_h, m01_h, s2_h, sD_h, sN_h, idf_h,
                      on_h, zo_h):
    if True:
        wqkv_h = io["wqkv"][li]; wo_h = io["wo"][li]; qs_h = io["qs"][li]
        ks_h = io["ks"][li]; g8_h = io["g8"][li]; gb_h = io["gb"][li]
        w1_h = io["w1"][li]; b1a_h = io["b1a"][li]; b1g_h = io["b1g"][li]
        w2_h = io["w2"][li]; b28_h = io["b28"][li]
        z_s = _t["z_s"]; arA = _t["arA"]; arK = _t["arK"]
        arV = _t["arV"]; arW = _t["arW"]; zbt = _t["zbt"]; q2 = _t["q2"]
        rows3 = _t["rows3"]; mcols = _t["mcols"]; msb = _t["msb"]
        stb = _t["stb"]; rdb = _t["rdb"]; eT = _t["eT"]
        m01 = _t["m01_s"]; sel2 = _t["sel2_s"]; selD = _t["selD_s"]
        selN = _t["selN_s"]; idnf = _t["idnf_s"]; qs_c = _t["qs_c"]
        ks_c = _t["ks_c"]; g8 = _t["g8_s"]; gb = _t["gb_s"]
        b1a = _t["b1a_s"]; b1g = _t["b1g_s"]; b28 = _t["b28_s"]
        ones_c = _t["ones_c"]; ones_r = _t["ones_r"]; zlead = _t["zlead"]
        psX = _t["psX"]; psCtx = _t["psCtx"]; psD2 = _t["psD2"]
        laug = rows3[0:2, 0:N]
        pe_prev = ec.cnt["cPE"]
        dve_prev = ec.cnt["cDVE"]
        tok.clear()

        def emit_dump(buf, sem_name, val):
            def sy_d(s):
                ec.wait(s, sem_name, val)
                ec.inc("st0", s.dma_start(out=dbg_h[:, :], in_=buf[:, :]), 16)
                ec.wait(s, "st0")
            blk.sync(sy_d)

        def zf(dk, c0, c1):
            return z_s[:, dk * 2049 + 1 + c0:dk * 2049 + 1 + c1]

        def zbs(dk, shifted, c0=0, c1=512):
            off = 0 if (shifted and dk >= 4) else 1
            return zbt[:, dk * 513 + off + c0:dk * 513 + off + c1]

        def build_zbt(G, ncc, key, lead=False):
            for dk in range(NDK):
                if lead and ncc > 0:
                    G.copy(zbt[:, dk * 513:dk * 513 + 1],
                           zlead[:, dk * 4 + ncc:dk * 4 + ncc + 1])
                    t2 = G.copy(zbt[:, dk * 513 + 1:dk * 513 + 513],
                                z_s[:, dk * 2049 + ncc * 512 + 1:
                                    dk * 2049 + ncc * 512 + 513])
                else:
                    t2 = G.copy(zbt[:, dk * 513:dk * 513 + 513],
                                z_s[:, dk * 2049 + ncc * 512:
                                    dk * 2049 + ncc * 512 + 513])
            tok[key] = t2

        def sy_ld(s):
            if first:
                for dk in range(NDK):
                    ec.inc("ldz", s.dma_start(
                        out=zf(dk, 0, N),
                        in_=z_h[dk * 128:(dk + 1) * 128, :]), 16)
            else:
                ec.wait(s, "cPE", pe_prev)
                ec.wait(s, "cDVE", dve_prev)
            pairs = [(qs_c, qs_h), (ks_c, ks_h), (g8, g8_h),
                     (b1a, b1a_h), (b1g, b1g_h), (b28, b28_h)]
            if first:
                pairs.append((idnf, idf_h))
            for (dst, src) in pairs:
                ec.inc("ldc", s.dma_start(out=dst[:, :], in_=src), 16)
        blk.sync(sy_ld)

        def g_ld(g):
            G = GW(ec, g)
            if first:
                G.memset(ones_c[:, :], 1.0)
                G.memset(ones_r[:, :], 1.0)
                for dk in range(NDK):
                    G.memset(z_s[:, dk * 2049:dk * 2049 + 1], 0.0)
                G.dma("ldk", m01[:, :], m01_h[:, :])
                G.dma("ldk", sel2[:, :], s2_h[:, :])
                G.dma("ldk", selD[:, :], sD_h[:, :])
                G.dma("ldk", selN[:, :], sN_h[:, :])
            else:
                G.wait("cPE", pe_prev)
            G.dma("ldk", gb[:, :], gb_h)
            tok["zzero"] = ec.cnt["cPL"]
        blk.gpsimd(g_ld)

        qke0 = ec.cnt["cACT"]
        qkp0 = ec.cnt["cPE"]
        for ncc in range(NC4):
            def g_zb(g, ncc=ncc):
                G = GW(ec, g)
                if ncc == 0:
                    if first:
                        G.wait("ldz")
                        G.wait("cPL", tok["zzero"])
                    else:
                        G.wait("cPE", pe_prev)
                        G.wait("cDVE", dve_prev)
                else:
                    G.wait("cPE", qkp0 + ncc * 16)
                build_zbt(G, ncc, ("zbt1", ncc))
            blk.gpsimd(g_zb)

            for idx in range(16):
                ti, oc = idx // 8, idx % 8
                dst = arA if ti == 0 else arK
                gidx = ncc * 16 + idx
                buf = arW[:, (gidx % 2) * 1024:(gidx % 2) * 1024 + 1024]

                def g_w(g, ti=ti, oc=oc, gidx=gidx, buf=buf):
                    G = GW(ec, g)
                    if gidx >= 2:
                        G.wait_future("cPE", qkp0 + gidx - 1)
                    tok[("wld", gidx)] = G.dma(
                        "ldwA" if gidx % 2 == 0 else "ldwB", buf,
                        bass.AP(wqkv_h.tensor, wqkv_h.offset + ti * D + oc * 128,
                                [[3 * D, 128], [128 * 3 * D, NDK],
                                 [1, 128]]))
                blk.gpsimd(g_w)

                def t_p(t, ncc=ncc, gidx=gidx, buf=buf):
                    T = TW(ec, t)
                    T.wait("ldwA" if gidx % 2 == 0 else "ldwB",
                           tok[("wld", gidx)])
                    T.wait("cPL", tok[("zbt1", ncc)])
                    bank = psX[:, (gidx % 2) * 512:(gidx % 2) * 512 + 512]
                    if gidx >= 2:
                        T.wait_future("cACT", qke0 + gidx - 1)
                    for dk in range(NDK):
                        t2 = T.mm(bank, buf[:, dk * 128:dk * 128 + 128],
                                  zbs(dk, shifted=True),
                                  start=(dk == 0), stop=(dk == NDK - 1),
                                  mark=(dk == NDK - 1))
                    assert t2 == qkp0 + gidx + 1
                blk.tensor(t_p)

                def s_e(s, ncc=ncc, ti=ti, oc=oc, gidx=gidx, dst=dst):
                    S = SW(ec, s)
                    bank = psX[:, (gidx % 2) * 512:(gidx % 2) * 512 + 512]
                    S.wait("cPE", qkp0 + gidx + 1)
                    t2 = S.copy(
                        dst[:, oc * N + ncc * 512:oc * N + ncc * 512 + 512],
                        bank)
                    assert t2 == qke0 + gidx + 1
                blk.scalar(s_e)

        vml0 = ec.cnt["ldw"]

        def g_vw(g):
            G = GW(ec, g)
            G.wait_future("cPE", qkp0 + 64)
            for vc in range(2):
                G.dma("ldw", arW[:, vc * 4096:vc * 4096 + 4096],
                      bass.AP(wqkv_h.tensor, wqkv_h.offset + 2 * D + vc * 512,
                              [[3 * D, 128], [128 * 3 * D, NDK], [1, 512]]))
        blk.gpsimd(g_vw)

        vp0 = ec.cnt["cPE"]
        ve0 = ec.cnt["cACT"]
        for ncc in range(NC4):
            def g_zb2(g, ncc=ncc):
                G = GW(ec, g)
                G.wait("cPE", qkp0 + 64 if ncc == 0 else vp0 + ncc * 8)
                build_zbt(G, ncc, ("zbt2", ncc))
            blk.gpsimd(g_zb2)

            for li in range(8):
                nt = ncc * 4 + li // 2
                vc = li % 2
                vidx = ncc * 8 + li

                def t_v(t, ncc=ncc, nt=nt, vc=vc, vidx=vidx):
                    T = TW(ec, t)
                    if vidx == 0:
                        T.wait("ldw", vml0 + 32)
                    T.wait("cPL", tok[("zbt2", ncc)])
                    bank = psX[:, (vidx % 2) * 512:(vidx % 2) * 512 + 512]
                    if vidx >= 2:
                        T.wait_future("cACT", ve0 + vidx - 1)
                    else:
                        T.wait("cACT", qke0 + 63 + vidx)
                    loc = (nt % 4) * 128
                    for dk in range(NDK):
                        t2 = T.mm(bank,
                                  zbs(dk, True, loc, loc + 128),
                                  arW[:, vc * 4096 + dk * 512:
                                      vc * 4096 + dk * 512 + 512],
                                  start=(dk == 0), stop=(dk == NDK - 1),
                                  mark=(dk == NDK - 1))
                    assert t2 == vp0 + vidx + 1
                blk.tensor(t_v)

                def s_v(s, nt=nt, vc=vc, vidx=vidx):
                    S = SW(ec, s)
                    bank = psX[:, (vidx % 2) * 512:(vidx % 2) * 512 + 512]
                    S.wait("cPE", vp0 + vidx + 1)
                    t2 = S.copy(arV[:, nt * 1024 + vc * 512:
                                    nt * 1024 + vc * 512 + 512], bank)
                    assert t2 == ve0 + vidx + 1
                blk.scalar(s_v)

        ldt0 = ec.cnt["ldt"]

        def g_tab(g):
            G = GW(ec, g)
            G.wait_future("cPE", vp0 + 32)
            G.dma("ldt", arW[:, 0:N], rot_h[:, 0:N])
            G.dma("ldt", arW[:, N:2 * N], rot_h[:, N:2 * N])
        blk.gpsimd(g_tab)

        for ti in range(2):
            src = arA if ti == 0 else arK
            sc_c = qs_c if ti == 0 else ks_c
            if ti == 1:
                def g_tab2(g):
                    G = GW(ec, g)
                    G.wait("cDVE", tok[("rot", 0, NDK - 1)])
                    G.dma("ldt", arW[:, 0:N], rot_h[:, 2 * N:3 * N])
                    G.dma("ldt", arW[:, N:2 * N], rot_h[:, 3 * N:4 * N])
                blk.gpsimd(g_tab2)

            for dk in range(NDK):
                def g_s(g, ti=ti, dk=dk, src=src):
                    G = GW(ec, g)
                    if ti == 0 and dk == 0:
                        G.wait_future("cACT", qke0 + 64)
                    else:
                        G.wait("cDVE", tok[("rot", ti, dk - 1)] if dk else
                               tok[("rot", ti - 1, NDK - 1)])
                    x = src[:, dk * N:(dk + 1) * N]
                    tok[("sq", ti, dk)] = G.tt(q2[:, :], x, x, ALU.mult)
                blk.gpsimd(g_s)

                def t_n(t, ti=ti, dk=dk):
                    T = TW(ec, t)
                    T.wait("cPL", tok[("sq", ti, dk)])
                    if (ti, dk) == (0, 0):
                        T.wait("ldk")
                    else:
                        T.wait("cDVE", tok[("rr", ti, dk - 1)] if dk else
                               tok[("rr", ti - 1, NDK - 1)])
                    for ncc in range(NC4):
                        t2 = T.mm(psCtx[0:2, ncc * 512:ncc * 512 + 512],
                                  selN[:, :], q2[:, ncc * 512:ncc * 512 + 512],
                                  start=True, stop=True,
                                  mark=(ncc == NC4 - 1))
                    tok[("nmm", ti, dk)] = t2
                blk.tensor(t_n)

                def s_n(s, ti=ti, dk=dk):
                    S = SW(ec, s)
                    S.wait("cPE", tok[("nmm", ti, dk)])
                    tok[("nsq", ti, dk)] = S.act(psCtx[0:2, 0:2048],
                                                 psCtx[0:2, 0:2048], AF.Sqrt)
                blk.scalar(s_n)

                def v_n(v, ti=ti, dk=dk):
                    V = VW(ec, v)
                    V.wait("cACT", tok[("nsq", ti, dk)])
                    V.ts(psCtx[0:2, 0:2048], psCtx[0:2, 0:2048], 1e-12,
                         op0=ALU.max)
                    V.recip(psCtx[0:2, 0:2048], psCtx[0:2, 0:2048])
                    tok[("rr", ti, dk)] = V.copy(rows3[0:2, 0:N],
                                                 psCtx[0:2, 0:2048])
                blk.vector(v_n)

                ub = [None]

                def t_b(t, ti=ti, dk=dk, ub=ub):
                    T = TW(ec, t)
                    T.wait("cDVE", tok[("rr", ti, dk)])
                    ub[0] = ec.cnt["cDVE"]
                    for ncc in range(NC4):
                        if ncc >= 2:
                            T.wait_future("cDVE", ub[0] + ncc - 1)
                        tok[("rb", ti, dk, ncc)] = T.mm(
                            psD2[:, (ncc % 2) * 512:(ncc % 2) * 512 + 512],
                            sel2[:, :],
                            rows3[0:2, ncc * 512:ncc * 512 + 512],
                            start=True, stop=True, mark=True)
                blk.tensor(t_b)

                def v_u(v, ti=ti, dk=dk, src=src, sc_c=sc_c, ub=ub):
                    V = VW(ec, v)
                    if ti == 0 and dk == 0:
                        V.wait("ldc")
                    t2 = None
                    for ncc in range(NC4):
                        V.wait("cPE", tok[("rb", ti, dk, ncc)])
                        sl = src[:, dk * N + ncc * 512:dk * N + ncc * 512 + 512]
                        t2 = V.stt(sl, sl, sc_c[:, 0:1],
                                   psD2[:, (ncc % 2) * 512:(ncc % 2) * 512 + 512],
                                   ALU.mult, ALU.mult)
                        assert t2 == ub[0] + ncc + 1
                    tok[("u", ti, dk)] = t2
                blk.vector(v_u)

                def g_sg(g, ti=ti, dk=dk, src=src):
                    G = GW(ec, g)
                    G.wait("cDVE", tok[("u", ti, dk)])
                    ch0 = dk * N
                    for (a, b) in ((0, 32), (32, 0), (64, 96), (96, 64)):
                        G.dma("lds", q2[a:a + 32, :],
                              src[b:b + 32, ch0:ch0 + N])
                    tok[("sg", ti, dk)] = ec.cnt["lds"]
                blk.gpsimd(g_sg)

                def v_r1(v, ti=ti, dk=dk, src=src):
                    V = VW(ec, v)
                    if dk == 0:
                        V.wait("ldt", ldt0 + 32 * (ti + 1))
                    V.wait("lds", tok[("sg", ti, dk)])
                    ch = src[:, dk * N:(dk + 1) * N]
                    tok[("t1", ti, dk)] = V.tt(ch, ch, arW[:, 0:N], ALU.mult)
                blk.vector(v_r1)

                def g_r2(g, ti=ti, dk=dk):
                    G = GW(ec, g)
                    if dk == 0:
                        G.wait("ldt", ldt0 + 32 * (ti + 1))
                    G.wait("lds", tok[("sg", ti, dk)])
                    tok[("t2", ti, dk)] = G.tt(q2[:, :], q2[:, :],
                                               arW[:, N:2 * N], ALU.mult)
                blk.gpsimd(g_r2)

                def v_r3(v, ti=ti, dk=dk, src=src):
                    V = VW(ec, v)
                    V.wait("cPL", tok[("t2", ti, dk)])
                    ch = src[:, dk * N:(dk + 1) * N]
                    tok[("rot", ti, dk)] = V.add(ch, ch, q2[:, :])
                blk.vector(v_r3)

        if dump in ("q3", "k3", "v2"):
            src = {"q3": arA, "k3": arK, "v2": arV}[dump]
            emit_dump(src, "cDVE", tok[("rot", 1, NDK - 1)])
            return

        def p1win(qt):
            return max(0, qt * 128 - 256), qt * 128 + 128

        def p2win(kt):
            return kt * 128, min(N, kt * 128 + 384)

        segs = {}
        seen_b = set()
        for kt in range(NQT):
            lo, hi = p2win(kt)
            brk = sorted(set([lo, hi, min(lo + 128, hi), min(lo + 256, hi)] +
                             [c for c in range(512, N, 512) if lo < c < hi]))
            ps = []
            for a, b in zip(brk[:-1], brk[1:]):
                qc = a // 512
                st = qc not in seen_b
                seen_b.add(qc)
                sp = (kt == NQT - 1) or (b <= lo + 128)
                ps.append((a, b, st, sp))
            segs[kt] = ps

        for hp in range(8):
            tok[("mx_b", hp)] = ec.cnt["cDVE"]

            def t_p1(t, hp=hp):
                T = TW(ec, t)
                if hp == 0:
                    T.wait("cDVE", tok[("rot", 1, NDK - 1)])
                    T.wait("cACT", ec.cnt["cACT"])
                else:
                    T.wait("cDVE", tok[("aoev", hp - 1, 3)])
                for qt in range(NQT):
                    lo, hi = p1win(qt)
                    w = hi - lo
                    ps = psX if qt % 2 == 0 else psD2
                    if qt >= 2:
                        T.wait_future("cDVE", tok[("mx_b", hp)] + qt - 1)
                    for j in range(2):
                        rb = j * 64
                        t2 = T.mm(ps[:, j * 512:j * 512 + w],
                                  arA[rb:rb + 64,
                                      hp * N + qt * 128:hp * N + qt * 128 + 128],
                                  arK[rb:rb + 64, hp * N + lo:hp * N + hi],
                                  start=True, stop=True, mark=(j == 1))
                    tok[("p1", hp, qt)] = t2
            blk.tensor(t_p1)

            def v_mx(v, hp=hp):
                V = VW(ec, v)
                for qt in range(NQT):
                    lo, hi = p1win(qt)
                    w = hi - lo
                    ps = psX if qt % 2 == 0 else psD2
                    V.wait("cPE", tok[("p1", hp, qt)])
                    t2 = V.red(ap3(mcols.ap(), qt, [[32, 128], [16, 2]]),
                               ap3(ps[:, :], 0,
                                   [[1024, 128], [512, 2], [1, w]]),
                               op=ALU.max, negate=True)
                    assert t2 == tok[("mx_b", hp)] + qt + 1
            blk.vector(v_mx)

            def t_mt(t, hp=hp):
                T = TW(ec, t)
                T.wait("cDVE", tok[("mx_b", hp)] + NQT)
                T.wait("ldc")
                tok[("mt", hp)] = T.trans(psD2[0:32, 0:128], mcols[:, :],
                                          idnf[:, :], mark=True)
            blk.tensor(t_mt)

            def v_mse(v, hp=hp):
                V = VW(ec, v)
                V.wait("cPE", tok[("mt", hp)])
                tok[("mse", hp)] = V.copy(msb[:, :], psD2[0:32, 0:128])
            blk.vector(v_mse)

            def g_mf(g, hp=hp):
                G = GW(ec, g)
                G.wait("cDVE", tok[("mse", hp)])
                for j in range(2):
                    G.dma("lds", rows3[0:1, (j + 1) * N:(j + 2) * N],
                          msb[j * 16:j * 16 + 16, :])
                tok[("mf", hp)] = ec.cnt["lds"]
            blk.gpsimd(g_mf)

            tok[("msk_b", hp)] = ec.cnt["cPL"]
            tok[("exp_b", hp)] = ec.cnt["cACT"]

            if dump == "e4" and hp == 0:
                def sy_e4(s):
                    ec.wait_future(s, "cPL", tok[("msk_b", 0)] + 2 * 4 + 2)
                    ec.inc("st0", s.dma_start(out=dbg_h[:, 0:768],
                                              in_=eT[:, :]), 16)
                    ec.wait(s, "st0")
                blk.sync(sy_e4)

            def t_p2(t, hp=hp):
                T = TW(ec, t)

                def emit_pvd(kt):
                    lo, hi = p2win(kt)
                    T.wait_future("cPL", tok[("msk_b", hp)] + 2 * kt + 2)
                    np_ = len(segs[kt])
                    for pi, (c0, c1, st, sp) in enumerate(segs[kt]):
                        qc = c0 // 512
                        bk, rb = qc // 2, (qc % 2) * 32
                        for j in range(2):
                            h = 2 * hp + j
                            T.mm(psCtx[j * 64:j * 64 + 64, c0:c1],
                                 arV[:, kt * 1024 + h * 64:
                                     kt * 1024 + h * 64 + 64],
                                 eT[:, j * 384 + (c0 - lo):
                                    j * 384 + (c1 - lo)],
                                 start=st, stop=sp, skip=True)
                            is_pvd = (j == 1 and sp and kt == 4 * qc + 3)
                            is_lst = (j == 1 and pi == np_ - 1)
                            t2 = T.mm(
                                psD2[rb:rb + 2,
                                     bk * 512 + (c0 - qc * 512):
                                     bk * 512 + (c1 - qc * 512)],
                                selD[:, j * 2:j * 2 + 2],
                                eT[:, j * 384 + (c0 - lo):
                                   j * 384 + (c1 - lo)],
                                start=(st and j == 0), stop=(sp and j == 1),
                                skip=True, mark=(is_pvd or is_lst))
                            if is_pvd:
                                tok[("pvd", hp, qc)] = t2
                            if is_lst:
                                tok[("pvk", hp, kt)] = t2

                T.wait("lds", tok[("mf", hp)])
                for kt in range(NQT):
                    lo, hi = p2win(kt)
                    w = hi - lo
                    for j in range(2):
                        rb = j * 64
                        mrow = rows3[0:1, (j + 1) * N + lo:(j + 1) * N + hi]
                        T.mm(psX[:, j * 512:j * 512 + w],
                             arK[rb:rb + 64,
                                 hp * N + kt * 128:hp * N + kt * 128 + 128],
                             arA[rb:rb + 64, hp * N + lo:hp * N + hi],
                             start=True, stop=False)
                        tok[("p2", hp, kt, j)] = T.mm(
                            psX[:, j * 512:j * 512 + w], ones_r[:, :], mrow,
                            start=False, stop=True, mark=True)
                    emit_pvd(kt)
            blk.tensor(t_p2)

            def s_ex(s, hp=hp):
                S = SW(ec, s)
                for kt in range(NQT):
                    lo, hi = p2win(kt)
                    w = hi - lo
                    for j in range(2):
                        S.wait("cPE", tok[("p2", hp, kt, j)])
                        if kt >= 1 and j == 0:
                            S.wait("cPE", tok[("pvk", hp, kt - 1)])
                        if dump == "e4" and hp == 0 and kt == 5 and j == 0:
                            S.wait("st0", 16)
                        t2 = S.act(eT[:, j * 384:j * 384 + w],
                                   psX[:, j * 512:j * 512 + w], AF.Exp,
                                   scale=8.0)
                        assert t2 == tok[("exp_b", hp)] + 2 * kt + j + 1
            blk.scalar(s_ex)

            def g_mk(g, hp=hp):
                G = GW(ec, g)
                for kt in range(NQT):
                    lo, hi = p2win(kt)
                    w = hi - lo
                    for j in range(2):
                        G.wait("cACT", tok[("exp_b", hp)] + 2 * kt + j + 1)
                        t2 = G.tt(eT[:, j * 384:j * 384 + w],
                                  eT[:, j * 384:j * 384 + w],
                                  m01[:, 0:w], ALU.mult)
                        assert t2 == tok[("msk_b", hp)] + 2 * kt + j + 1
            blk.gpsimd(g_mk)

            def v_rd(v, hp=hp):
                V = VW(ec, v)
                for qc in range(4):
                    bk, rb = qc // 2, (qc % 2) * 32
                    V.wait("cPE", tok[("pvd", hp, qc)])
                    V.recip(psD2[rb:rb + 2, bk * 512:bk * 512 + 512],
                            psD2[rb:rb + 2, bk * 512:bk * 512 + 512])
                    tok[("rdr", hp, qc)] = V.copy(
                        rows3[0:2, qc * 512:qc * 512 + 512],
                        psD2[rb:rb + 2, bk * 512:bk * 512 + 512])
            blk.vector(v_rd)

            def t_rb(t, hp=hp):
                T = TW(ec, t)
                T.wait("cACT", tok[("exp_b", hp)] + 2 * NQT)
                for qc in range(4):
                    T.wait("cDVE", tok[("rdr", hp, qc)])
                    if qc > 0:
                        T.wait_future("cACT", tok[("rbe_b", hp)] + qc)
                    tok[("rbm", hp, qc)] = T.mm(
                        psX[:, 0:512], sel2[:, :],
                        rows3[0:2, qc * 512:qc * 512 + 512],
                        start=True, stop=True, mark=True)
            tok[("rbe_b", hp)] = ec.cnt["cACT"]
            blk.tensor(t_rb)
            tok[("aoev_b", hp)] = ec.cnt["cDVE"]

            def s_rb(s, hp=hp):
                S = SW(ec, s)
                for qc in range(4):
                    S.wait("cPE", tok[("rbm", hp, qc)])
                    t2 = S.copy(rdb[:, :], psX[:, 0:512])
                    assert t2 == tok[("rbe_b", hp)] + qc + 1
                    S.wait_future("cDVE", tok[("aoev_b", hp)] + qc + 1)
            blk.scalar(s_rb)

            def v_ao(v, hp=hp):
                V = VW(ec, v)
                for qc in range(4):
                    V.wait("cACT", tok[("rbe_b", hp)] + qc + 1)
                    t2 = V.tt(arA[:, hp * N + qc * 512:hp * N + qc * 512 + 512],
                              psCtx[:, qc * 512:qc * 512 + 512],
                              rdb[:, :], ALU.mult)
                    tok[("aoev", hp, qc)] = t2
                    assert t2 == tok[("aoev_b", hp)] + qc + 1
            blk.vector(v_ao)

        if dump == "ao":
            emit_dump(arA, "cDVE", tok[("aoev", 7, 3)])
            return
        if dump == "rd7":
            def sy_d7(s):
                ec.wait(s, "cDVE", tok[("aoev", 7, 3)])
                ec.inc("st0", s.dma_start(out=dbg_h[0:2, 0:N],
                                          in_=rows3[0:2, 0:N]), 16)
                ec.inc("st0", s.dma_start(out=dbg_h[0:1, N:3 * N],
                                          in_=rows3[0:1, N:3 * N]), 16)
                ec.wait(s, "st0")
            blk.sync(sy_d7)
            return

        wo_p0 = ec.cnt["cPE"]
        wo_z0 = ec.cnt["cDVE"]
        for oc in range(NDK):
            buf = arW[:, (oc % 2) * 1024:(oc % 2) * 1024 + 1024]

            def g_wo(g, oc=oc, buf=buf):
                G = GW(ec, g)
                if oc == 0:
                    G.wait("cDVE", tok[("rot", 1, NDK - 1)])
                if oc >= 2:
                    G.wait_future("cPE", wo_p0 + (oc - 1) * NC4)
                tok[("wol", oc)] = G.dma(
                    "ldwoA" if oc % 2 == 0 else "ldwoB", buf,
                    bass.AP(wo_h.tensor, wo_h.offset + oc * 128,
                            [[D, 128], [128 * D, NDK], [1, 128]]))
            blk.gpsimd(g_wo)

            def t_wo(t, oc=oc, buf=buf):
                T = TW(ec, t)
                T.wait("ldwoA" if oc % 2 == 0 else "ldwoB", tok[("wol", oc)])
                if oc == 0:
                    T.wait("cDVE", tok[("aoev", 7, 3)])
                for ncc in range(NC4):
                    gidx = oc * NC4 + ncc
                    bank = psX[:, (gidx % 2) * 512:(gidx % 2) * 512 + 512]
                    if gidx >= 2:
                        T.wait_future("cDVE", wo_z0 + gidx - 1)
                    for dk in range(NDK):
                        t2 = T.mm(bank, buf[:, dk * 128:dk * 128 + 128],
                                  arA[:, dk * N + ncc * 512:
                                      dk * N + ncc * 512 + 512],
                                  start=(dk == 0), stop=(dk == NDK - 1),
                                  mark=(dk == NDK - 1))
                    assert t2 == wo_p0 + gidx + 1
            blk.tensor(t_wo)

            def v_zu(v, oc=oc):
                V = VW(ec, v)
                for ncc in range(NC4):
                    gidx = oc * NC4 + ncc
                    bank = psX[:, (gidx % 2) * 512:(gidx % 2) * 512 + 512]
                    V.wait("cPE", wo_p0 + gidx + 1)
                    if gidx == 0:
                        V.wait("ldz")
                    sl = zf(oc, ncc * 512, ncc * 512 + 512)
                    t2 = V.stt(sl, bank, 1.0, sl, ALU.mult, ALU.add)
                    assert t2 == wo_z0 + gidx + 1
            blk.vector(v_zu)

        if dump == "z5":
            def sy_z5(s):
                ec.wait(s, "cDVE", wo_z0 + 32)
                for oc in range(NDK):
                    ec.inc("st0", s.dma_start(
                        out=zo_h[oc * 128:(oc + 1) * 128, :],
                        in_=zf(oc, 0, N)), 16)
                ec.wait(s, "st0")
            blk.sync(sy_z5)
            return

        mu_b = arA[:, 0:4096].bitcast(F32)
        std_b = arA[:, 4096:8192].bitcast(F32)
        rstd_b = arA[:, 8192:12288].bitcast(F32)
        tmp_b = arA[:, 12288:16384].bitcast(F32)

        def g_zcap(g):
            G = GW(ec, g)
            G.wait("cDVE", ec.cnt["cDVE"])
            for dk in range(NDK):
                G.copy(ap3(zlead.ap(), dk * 4 + 1, [[NDK * 4, 128], [1, 3]]),
                       ap3(z_s.ap(), dk * 2049 + 512,
                           [[NDK * 2049, 128], [512, 3]]))
        blk.gpsimd(g_zcap)

        ln_sq0 = ec.cnt["cPL"]
        ln_mm0 = [None]

        def ln_mm_tok(ncc, dk):
            if dk < 0:
                if ncc == 0:
                    return 0
                ncc, dk = ncc - 1, dk + NDK
            return ln_mm0[0] + ncc * NDK + dk + 1

        def g_lnsq(g):
            G = GW(ec, g)
            G.wait("cDVE", ec.cnt["cDVE"])
            G.wait("cPE", wo_p0 + 32)
            for ncc in range(NC4):
                if ncc >= 1:
                    G.wait_future("cPE", ln_mm_tok(ncc - 1, NDK - 1))
                build_zbt(G, ncc, ("zbt3", ncc))
                for dk in range(NDK):
                    G.wait_future("cPE", ln_mm_tok(ncc, dk - 2))
                    G.tt(q2[:, (dk % 2) * 512:(dk % 2) * 512 + 512],
                         zbs(dk, True), zbs(dk, True), ALU.mult)
        ln_mm0[0] = ec.cnt["cPE"]
        blk.gpsimd(g_lnsq)

        def t_lnst(t):
            T = TW(ec, t)
            for ncc in range(NC4):
                sl = slice(ncc * 512, ncc * 512 + 512)
                for dk in range(NDK):
                    T.wait("cPL", ln_sq0 + ncc * 16 + NDK + dk + 1)
                    T.mm(psCtx[0:1, sl], ones_c[:, :], zbs(dk, True),
                         start=(dk == 0), stop=(dk == NDK - 1), skip=True)
                    t2 = T.mm(psCtx[32:33, sl], ones_c[:, :],
                              q2[:, (dk % 2) * 512:(dk % 2) * 512 + 512],
                              start=(dk == 0), stop=(dk == NDK - 1),
                              skip=True, mark=True)
                    assert t2 == ln_mm_tok(ncc, dk)
        blk.tensor(t_lnst)

        sums_bf = q2[0:1, 0:2048]
        hl_tok = {}
        bc0 = ec.cnt["cPE"]
        be0 = ec.cnt["cACT"]
        lo0 = ec.cnt["cDVE"]

        def s_hilo(s):
            S = SW(ec, s)
            for ncc in range(NC4):
                sl = slice(ncc * 512, ncc * 512 + 512)
                if ncc == 0:
                    S.wait("cPE", ln_mm_tok(NC4 - 1, NDK - 1))
                S.copy(sums_bf[0:1, 0:512], psCtx[0:1, sl])
                hl_tok[ncc] = S.copy(sums_bf[0:1, 1024:1536],
                                     psCtx[32:33, sl])
                S.wait_future("cPE", bc0 + ncc + 1)
                S.copy(mu_b[:, sl], psX[:, 0:512])
                t2 = S.copy(std_b[:, sl], psX[:, 512:1024])
                assert t2 == be0 + ncc * 4 + 4
        blk.scalar(s_hilo)

        def v_lo(v):
            V = VW(ec, v)
            for ncc in range(NC4):
                sl = slice(ncc * 512, ncc * 512 + 512)
                V.wait("cACT", hl_tok[ncc])
                V.sub(sums_bf[0:1, 512:1024], psCtx[0:1, sl],
                      sums_bf[0:1, 0:512])
                t2 = V.sub(sums_bf[0:1, 1536:2048], psCtx[32:33, sl],
                           sums_bf[0:1, 1024:1536])
                assert t2 == lo0 + 2 * ncc + 2
        blk.vector(v_lo)

        def t_bc(t):
            T = TW(ec, t)
            for ncc in range(NC4):
                T.wait_future("cDVE", lo0 + 2 * ncc + 2)
                if ncc >= 1:
                    T.wait_future("cACT", be0 + ncc * 4)
                T.mm(psX[:, 0:512], ones_r[:, :], sums_bf[0:1, 0:512],
                     start=True, stop=False)
                T.mm(psX[:, 0:512], ones_r[:, :], sums_bf[0:1, 512:1024],
                     start=False, stop=True)
                T.mm(psX[:, 512:1024], ones_r[:, :], sums_bf[0:1, 1024:1536],
                     start=True, stop=False)
                t2 = T.mm(psX[:, 512:1024], ones_r[:, :],
                          sums_bf[0:1, 1536:2048],
                          start=False, stop=True, mark=True)
                assert t2 == bc0 + ncc + 1
        blk.tensor(t_bc)

        lnf = {}

        def v_fin(v):
            V = VW(ec, v)
            V.wait("cACT", be0 + 16)
            V.ts(mu_b[:, :], mu_b[:, :], 1.0 / D)
            V.ts(std_b[:, :], std_b[:, :], 1.0 / D)
            V.tt(tmp_b[:, :], mu_b[:, :], mu_b[:, :], ALU.mult)
            V.sub(std_b[:, :], std_b[:, :], tmp_b[:, :])
            lnf["var"] = V.ts(std_b[:, :], std_b[:, :], 1e-5, op0=ALU.add)
        blk.vector(v_fin)

        def s_sq2(s):
            S = SW(ec, s)
            S.wait("cDVE", lnf["var"])
            lnf["std"] = S.act(std_b[:, :], std_b[:, :], AF.Sqrt)
        blk.scalar(s_sq2)

        def v_fin2(v):
            V = VW(ec, v)
            V.wait("cACT", lnf["std"])
            V.recip(rstd_b[:, :], std_b[:, :])
            V.stt(tmp_b[:, :], mu_b[:, :], -1.0, rstd_b[:, :],
                  ALU.mult, ALU.mult)
            lnf["laug"] = V.copy(laug[0:1, :], tmp_b[0:1, :])
        blk.vector(v_fin2)

        def g_l1(g):
            G = GW(ec, g)
            G.wait("cDVE", lnf["laug"])
            G.dma("ldk", laug[1:2, :], on_h[:, :])
        blk.gpsimd(g_l1)
        ldk_l1 = ec.cnt["ldk"]

        zs3 = arK[:, 0:4096]
        gg = arV[:, 11776:12288]

        def w1b(pp, half):
            return arW[:, pp * 2048 + half * 1024:
                       pp * 2048 + half * 1024 + 1024]

        w2b = arW[:, 4096:4096 + 2816]

        for qtr in range(4):
            def g_zb3(g, qtr=qtr):
                G = GW(ec, g)
                if qtr == 0:
                    G.wait("ldk", ldk_l1)
                else:
                    G.wait("cPE", tok[("w1done", qtr - 1)])
                build_zbt(G, qtr, ("zbt4", qtr), lead=True)
            blk.gpsimd(g_zb3)

            def v_z3(v, qtr=qtr):
                V = VW(ec, v)
                V.wait("cPL", tok[("zbt4", qtr)])
                if qtr == 0:
                    V.wait("cDVE", lnf["laug"])
                t2 = None
                for dk in range(NDK):
                    sl = zs3[:, dk * 512:dk * 512 + 512]
                    V.ts(sl, zbs(dk, True), g8[:, dk:dk + 1], op0=ALU.mult)
                    t2 = V.tt(sl, sl, rstd_b[:, qtr * 512:qtr * 512 + 512],
                              ALU.mult)
                tok[("zs3", qtr)] = t2
            blk.vector(v_z3)

            for jp in range(NJP):
                w = JW[jp]
                pp = jp % 2
                idx = qtr * NJP + jp

                def g_w1(g, qtr=qtr, jp=jp, w=w, pp=pp, idx=idx):
                    G = GW(ec, g)
                    if idx == 0:
                        G.wait("cPE", wo_p0 + 32)
                    if qtr > 0 and jp == 0:
                        G.wait("cPE", tok[("w2c", qtr * NDK - 1)])
                    if idx >= 2:
                        G.wait("cPE", tok[("w1c", idx - 2)])
                    sname = "ldw1A" if pp == 0 else "ldw1B"
                    for half in range(2):
                        dst = w1b(pp, half)
                        tok[("w1l", idx)] = G.dma(
                            sname,
                            dst[:, 0:NDK * w] if w < 128 else dst,
                            bass.AP(w1_h.tensor, w1_h.offset + half * DI + jp * 128,
                                    [[W1C, 128], [128 * W1C, NDK], [1, w]]))
                blk.gpsimd(g_w1)

                def wsl(pp, half, dk, w):
                    b = w1b(pp, half)
                    return (b[:, dk * w:dk * w + w] if w < 128 else
                            b[:, dk * 128:dk * 128 + w])

                def t_w1(t, qtr=qtr, jp=jp, w=w, pp=pp, idx=idx):
                    T = TW(ec, t)
                    T.wait("ldw1A" if pp == 0 else "ldw1B", tok[("w1l", idx)])
                    T.wait("cDVE", tok[("zs3", qtr)])
                    if idx >= 1:
                        T.wait("cACT", tok[("stb", idx - 1)])
                    for dk in range(NDK):
                        T.mm(psD2[0:2, 0:w], gb[:, 2 * dk:2 * dk + 2],
                             wsl(pp, 0, dk, w),
                             start=(dk == 0), stop=(dk == NDK - 1))
                        t2 = T.mm(psD2[0:2, 512:512 + w],
                                  gb[:, 2 * dk:2 * dk + 2],
                                  wsl(pp, 1, dk, w),
                                  start=(dk == 0), stop=(dk == NDK - 1),
                                  mark=(dk == NDK - 1))
                    tok[("stm", idx)] = t2
                blk.tensor(t_w1)

                def s_st(s, idx=idx, w=w):
                    S = SW(ec, s)
                    S.wait("cPE", tok[("stm", idx)])
                    if idx >= 1:
                        S.wait("cPE", tok[("hgm", idx - 1)])
                    S.copy(stb[0:2, 0:w], psD2[0:2, 0:w])
                    tok[("stb", idx)] = S.copy(stb[0:2, 128:128 + w],
                                               psD2[0:2, 512:512 + w])
                blk.scalar(s_st)

                def t_hg(t, qtr=qtr, jp=jp, w=w, pp=pp, idx=idx):
                    T = TW(ec, t)
                    T.wait("cACT", tok[("stb", idx)])
                    if idx >= 2:
                        T.wait("cDVE", tok[("hst", idx - 2)])
                    pa = psCtx[:, (pp * 2) * 512:(pp * 2) * 512 + 512]
                    pg = psCtx[:, (pp * 2 + 1) * 512:(pp * 2 + 1) * 512 + 512]
                    for dk in range(NDK):
                        T.mm(pa[0:w, :], wsl(pp, 0, dk, w),
                             zs3[:, dk * 512:dk * 512 + 512],
                             start=(dk == 0), stop=False)
                        T.mm(pg[0:w, :], wsl(pp, 1, dk, w),
                             zs3[:, dk * 512:dk * 512 + 512],
                             start=(dk == 0), stop=False)
                    T.mm(pa[0:w, :], stb[0:2, 0:w],
                         laug[0:2, qtr * 512:qtr * 512 + 512],
                         start=False, stop=True)
                    t2 = T.mm(pg[0:w, :], stb[0:2, 128:128 + w],
                              laug[0:2, qtr * 512:qtr * 512 + 512],
                              start=False, stop=True, mark=True)
                    tok[("hgm", idx)] = t2
                    tok[("w1c", idx)] = t2
                blk.tensor(t_hg)

                def s_gg(s, jp=jp, w=w, pp=pp, idx=idx):
                    S = SW(ec, s)
                    S.wait("cPE", tok[("hgm", idx)])
                    if idx >= 1:
                        S.wait_future("cDVE", tok.get(("hst", idx - 1), 0))
                    pg = psCtx[:, (pp * 2 + 1) * 512:(pp * 2 + 1) * 512 + 512]
                    tok[("gg", idx)] = S.act(gg[0:w, :], pg[0:w, :], AF.Gelu,
                                             bias=b1g[0:w, jp:jp + 1])
                blk.scalar(s_gg)

                def v_h(v, qtr=qtr, jp=jp, w=w, pp=pp, idx=idx):
                    V = VW(ec, v)
                    V.wait("cACT", tok[("gg", idx)])
                    if qtr > 0 and jp == 0:
                        V.wait("cPE", tok[("w2c", qtr * NDK - 1)])
                    pa = psCtx[:, (pp * 2) * 512:(pp * 2) * 512 + 512]
                    tok[("hst", idx)] = V.stt(
                        arV[0:w, jp * 512:jp * 512 + 512],
                        pa[0:w, :], b1a[0:w, jp:jp + 1], gg[0:w, :],
                        ALU.add, ALU.mult)
                blk.vector(v_h)
            tok[("w1done", qtr)] = ec.cnt["cPE"]

            for oc in range(NDK):
                widx = qtr * NDK + oc

                def g_w2(g, qtr=qtr, oc=oc, widx=widx):
                    G = GW(ec, g)
                    if widx >= 1:
                        G.wait("cPE", tok[("w2c", widx - 1)])
                    else:
                        G.wait("cPE", tok[("w1c", NJP - 1)])
                    G.dma("ldw2", w2b[:, 0:21 * 128],
                          bass.AP(w2_h.tensor, w2_h.offset + oc * 128,
                                  [[D, 128], [128 * D, 21], [1, 128]]))
                    tok[("w2l", widx)] = G.dma(
                        "ldw2", w2b[0:42, 21 * 128:21 * 128 + 128],
                        bass.AP(w2_h.tensor,
                                w2_h.offset + 2688 * D + oc * 128,
                                [[D, 42], [1, 128]]))
                blk.gpsimd(g_w2)

                def t_w2(t, qtr=qtr, oc=oc, widx=widx):
                    T = TW(ec, t)
                    T.wait("ldw2", tok[("w2l", widx)])
                    T.wait("cDVE", tok[("hst", qtr * NJP + NJP - 1)])
                    if widx >= 2:
                        T.wait("cDVE", tok[("zu2", widx - 2)])
                    bank = psX[:, (widx % 2) * 512:(widx % 2) * 512 + 512]
                    for ik in range(NJP):
                        kw = JW[ik]
                        t2 = T.mm(bank,
                                  w2b[0:kw, ik * 128:ik * 128 + 128],
                                  arV[0:kw, ik * 512:ik * 512 + 512],
                                  start=(ik == 0), stop=(ik == NJP - 1),
                                  mark=(ik == NJP - 1))
                    tok[("w2c", widx)] = t2
                blk.tensor(t_w2)

                def v_z2(v, qtr=qtr, oc=oc, widx=widx):
                    V = VW(ec, v)
                    V.wait("cPE", tok[("w2c", widx)])
                    bank = psX[:, (widx % 2) * 512:(widx % 2) * 512 + 512]
                    sl = zf(oc, qtr * 512, qtr * 512 + 512)
                    tok[("zu2", widx)] = V.stt(sl, bank, b28[:, oc:oc + 1],
                                               sl, ALU.add, ALU.add)
                blk.vector(v_z2)

        if last:
            def sy_st(s):
                for oc in range(NDK):
                    for qtr in range(4):
                        ec.wait(s, "cDVE", tok[("zu2", qtr * NDK + oc)])
                    ec.inc("st0", s.dma_start(
                        out=zo_h[oc * 128:(oc + 1) * 128, :],
                        in_=zf(oc, 0, N)), 16)
                ec.wait(s, "st0")
            blk.sync(sy_st)


def build_inner(dump=None, nlayers=1):
    nc = bass.Bass("TRN2", target_bir_lowering=False)
    io = _inner_io(nc, nlayers)
    io["z"] = nc.dram_tensor("z", [D, N], F32, kind="ExternalInput").ap()
    io["zo"] = nc.dram_tensor("zo", [D, N], F32, kind="ExternalOutput").ap()
    if dump:
        io["dbg"] = nc.dram_tensor("dbg", [128, 16384], BF16,
                                   kind="ExternalOutput").ap()
    with nc.Block() as blk:
        ec = EC(nc, blk)
        emit_inner_chain(nc, blk, ec, io, nlayers=nlayers, dump=dump)
        ec.close()
    return nc


def l1_consts():
    poolind = np.zeros((128, 64), np.float32)
    for p in range(128):
        poolind[p, p % 64] = 1.0
    return {"poolind": poolind.astype(nbf)}


def l1_inmap(x_fm, ao_fm, z6_fm, inp):
    ic = l1_consts()
    qw = np.asarray(inp["L1_attn_q_w"], np.float32)
    kw = np.asarray(inp["L1_attn_k_w"], np.float32)
    vw = np.asarray(inp["L1_attn_v_w"], np.float32)
    w3 = np.stack([np.transpose(qw, (0, 2, 1)),
                   np.transpose(kw, (0, 2, 1)),
                   np.transpose(vw, (0, 2, 1))], axis=1)
    qb = np.asarray(inp["L1_attn_q_b"], np.float32).reshape(16, 64)
    kb = np.asarray(inp["L1_attn_k_b"], np.float32).reshape(16, 64)
    qkb2 = np.zeros((64, 32), np.float32)
    qkb2[:, 0::2] = qb.T
    qkb2[:, 1::2] = kb.T
    vb = np.asarray(inp["L1_attn_v_b"], np.float32).reshape(1, 16 * 64)
    return {
        "x": np.ascontiguousarray(x_fm, np.float32),
        "ao": np.ascontiguousarray(ao_fm, np.float32),
        "z6": np.ascontiguousarray(z6_fm, np.float32),
        "w3": np.ascontiguousarray(w3.reshape(16, 3 * 1024 * 64)),
        "qkb2": qkb2, "vb": vb,
        "lwT": np.ascontiguousarray(
            np.asarray(inp["L1_attn_l_w"], np.float32).T),
        "lb": percol(np.asarray(inp["L1_attn_l_b"],
                                np.float32).reshape(-1)),
        "wT": np.ascontiguousarray(
            np.asarray(inp["L1_ffn_weight"], np.float32).T),
        "disha": np.ascontiguousarray(inp["L1_ffn_disha"], np.float32),
        "poolind": ic["poolind"],
    }


_L1_IO = (("w3", [16, 3 * D * 64], F32), ("qkb2", [64, 32], F32),
          ("vb", [1, 1024], F32), ("lwT", [D, D], F32),
          ("lb", [128, NDK], F32), ("wT", [D, D], F32),
          ("disha", [64, D], F32), ("poolind", [128, 64], BF16))


def emit_l1(nc, blk, ec, io, prefix="f_"):
    x_h = io["x"]; ao_h = io["ao"]; z6_h = io["z6"]
    w3_h = io["w3"]; qkb_h = io["qkb2"]; vb_h = io["vb"]
    lwT_h = io["lwT"]; lb_h = io["lb"]; wT_h = io["wT"]
    dsh_h = io["disha"]; pi_h = io["poolind"]; out_h = io["out"]

    st_ = ExitStack()
    with st_:
        _sb = [
            ("x1", (128, NDK * N), F32),
            ("xn2", (128, NDK * N), BF16),
            ("ao2", (128, NDK * N), BF16),
            ("qA", (64, N), BF16), ("kA", (64, N), BF16),
            ("qB", (64, N), BF16), ("kB", (64, N), BF16),
            ("vA", (128, 1024), BF16), ("vB", (128, 1024), BF16),
            ("eT", (128, N), BF16),
            ("vt", (128, 64), BF16),
            ("wqkv", (128, 2 * 1536), BF16),
            ("sqs", (128, 2 * N), BF16),
            ("sums", (1, 2048), BF16),
            ("qkb2_s", (64, 32), F32),
            ("vbr", (1, 1024), BF16),
            ("mxc", (128, 1), F32),
            ("dsc", (128, 1), F32),
            ("warA", (128, 2048), BF16),
            ("warB", (128, 2048), BF16),
            ("dishaB", (64, 1024), BF16),
            ("poolind_s", (128, 64), BF16),
            ("pooled", (64, N), BF16),
            ("lbc", (128, NDK), F32),
            ("outs", (128, 1024), F32),
            ("ones_c", (128, 1), BF16),
            ("ones_r", (1, 128), BF16),
        ]
        _ps = [("psS", (128, 2048)), ("psC", (128, 2048))]
        _t = _alloc(nc, st_, _sb, _ps, prefix=prefix)
        x1 = _t["x1"]; xn2 = _t["xn2"]; ao2 = _t["ao2"]
        qA = _t["qA"]; kA = _t["kA"]; qB = _t["qB"]; kB = _t["kB"]
        vA = _t["vA"]; vB = _t["vB"]; eT = _t["eT"]; vt = _t["vt"]
        wqkv = _t["wqkv"]; sqs = _t["sqs"]; sums = _t["sums"]
        qkb2 = _t["qkb2_s"]; vbr = _t["vbr"]; mxc = _t["mxc"]
        dsc = _t["dsc"]; warA = _t["warA"]; warB = _t["warB"]
        dishaB = _t["dishaB"]; poolind = _t["poolind_s"]
        pooled = _t["pooled"]; lbc = _t["lbc"]; outs = _t["outs"]
        ones_c = _t["ones_c"]; ones_r = _t["ones_r"]
        psS = _t["psS"]; psC = _t["psC"]

        mu_b = ao2[:, 0:4096].bitcast(F32)
        std_b = ao2[:, 4096:8192].bitcast(F32)
        rstd_b = ao2[:, 8192:12288].bitcast(F32)
        tmp_b = ao2[:, 12288:16384].bitcast(F32)
        tA = xn2[:, 0:8192].bitcast(F32)
        tB = xn2[:, 8192:16384].bitcast(F32)

        for sname in ("ldx", "ldaA", "ldaB", "ldc", "ldk", "ldwA", "ldwB",
                      "ldfA", "ldfB", "st0", "st1"):
            ec.sem(sname)
        tok = {}
        cmb0 = [None]

        def tok_cmb(dk):
            return cmb0[0] + 2 * dk + 2

        cmb0[0] = ec.cnt["cDVE"]

        def sy_ld(s):
            for dk in range(NDK):
                ec.inc("ldx", s.dma_start(
                    out=x1[:, dk * N:(dk + 1) * N],
                    in_=x_h[dk * 128:(dk + 1) * 128, :]), 16)
            for dk in range(NDK):
                sl = tA if dk % 2 == 0 else tB
                sname = "ldaA" if dk % 2 == 0 else "ldaB"
                if dk >= 2:
                    ec.wait_future(s, "cDVE", tok_cmb(dk - 2))
                ec.inc(sname, s.dma_start(
                    out=sl[:, 0:N], in_=ao_h[dk * 128:(dk + 1) * 128, :]), 16)
                ec.inc(sname, s.dma_start(
                    out=sl[:, N:2 * N],
                    in_=z6_h[dk * 128:(dk + 1) * 128, :]), 16)
            ec.inc("ldc", s.dma_start(out=qkb2[:, :], in_=qkb_h[:, :]), 16)
            ec.inc("ldc", s.dma_start(out=lbc[:, :], in_=lb_h[:, :]), 16)
        blk.sync(sy_ld)

        def v_cmb(v):
            V = VW(ec, v)
            assert cmb0[0] == ec.cnt["cDVE"]
            V.wait("ldx")
            for dk in range(NDK):
                sl = tA if dk % 2 == 0 else tB
                V.wait("ldaA" if dk % 2 == 0 else "ldaB",
                       32 * (dk // 2 + 1))
                xs = x1[:, dk * N:(dk + 1) * N]
                V.add(xs, xs, sl[:, 0:N])
                t2 = V.add(xs, xs, sl[:, N:2 * N])
                assert t2 == tok_cmb(dk)
        blk.vector(v_cmb)

        def g_ld(g):
            G = GW(ec, g)
            G.memset(ones_c[:, :], 1.0)
            G.memset(ones_r[:, :], 1.0)
            G.dma("ldk", vbr[:, :], vb_h[:, :])
            G.dma("ldk", poolind[:, :], pi_h[:, :])
            G.dma("ldk", dishaB[:, :], dsh_h[:, :])
            G.wait("cDVE", tok_cmb(NDK - 1))
            for dk in range(NDK):
                tok[("xb", dk)] = G.copy(xn2[:, dk * N:(dk + 1) * N],
                                         x1[:, dk * N:(dk + 1) * N])
        blk.gpsimd(g_ld)

        ln = emit_ln(
            ec,
            xbf_sl=lambda dk: xn2[:, dk * N:(dk + 1) * N],
            sq_scr=sqs, ones_c=ones_c, ones_r=ones_r,
            ps_stats=psS, ps_bc=psC[:, 0:1024], sums_bf=sums,
            mu_b=mu_b, rstd_b=rstd_b, tmp_b=tmp_b, std_b=std_b,
            want_aug=True, n_tok=N,
            dep=[("cPL", tok[("xb", NDK - 1)])])

        xnt = {}

        def v_xn(v):
            V = VW(ec, v)
            V.wait("cDVE", ln["dn"])
            for dk in range(NDK):
                sl = xn2[:, dk * N:(dk + 1) * N]
                V.tt(sl, x1[:, dk * N:(dk + 1) * N], rstd_b[:, :], ALU.mult)
                xnt[dk] = V.add(sl, sl, tmp_b[:, :])
            xnt["all"] = xnt[NDK - 1]
        blk.vector(v_xn)

        def wqkv_half(h):
            return wqkv[:, (h % 2) * 1536:(h % 2) * 1536 + 1536]

        for h in range(2):
            def g_w0(g, h=h):
                G = GW(ec, g)
                tok[("w3l", h)] = G.dma(
                    "ldwA" if h % 2 == 0 else "ldwB", wqkv_half(h),
                    bass.AP(w3_h.tensor, h * 3 * D * 64,
                            [[64, 128], [D * 64, 3], [128 * 64, NDK],
                             [1, 64]]))
            blk.gpsimd(g_w0)

        for hp in range(8):
            for j in range(2):
                h = 2 * hp + j
                qh = qA if j == 0 else qB
                kh = kA if j == 0 else kB
                vh = vA if j == 0 else vB
                buf = wqkv_half(h)

                if h >= 2:
                    def g_w2(g, h=h):
                        G = GW(ec, g)
                        G.wait_future("cPE", tok[("prj", h - 2)])
                        tok[("w3l", h)] = G.dma(
                            "ldwA" if h % 2 == 0 else "ldwB", wqkv_half(h),
                            bass.AP(w3_h.tensor, h * 3 * D * 64,
                                    [[64, 128], [D * 64, 3], [128 * 64, NDK],
                                     [1, 64]]))
                    blk.gpsimd(g_w2)

                def t_prj(t, h=h, buf=buf):
                    T = TW(ec, t)
                    T.wait("ldwA" if h % 2 == 0 else "ldwB", tok[("w3l", h)])
                    if h == 0:
                        T.wait("cDVE", xnt["all"])
                        T.wait("cPL")
                    else:
                        T.wait_future("cACT", tok[("exp", h - 1, NQT - 1)])
                    for ncc in range(NC4):
                        sl = slice(ncc * 512, ncc * 512 + 512)
                        for ti in range(2):
                            for dk in range(NDK):
                                t2 = T.mm(
                                    psS[ti * 64:ti * 64 + 64, sl],
                                    buf[:, ti * 512 + dk * 64:
                                        ti * 512 + dk * 64 + 64],
                                    xn2[:, dk * N + ncc * 512:
                                        dk * N + ncc * 512 + 512],
                                    start=(dk == 0), stop=(dk == NDK - 1),
                                    skip=True,
                                    mark=(dk == NDK - 1))
                            tok[("qk", h, ncc, ti)] = t2
                    tok[("qkall", h)] = t2
                blk.tensor(t_prj)

                def s_qke(s, h=h, qh=qh, kh=kh):
                    S = SW(ec, s)
                    if h == 0:
                        S.wait("ldc")
                    for ncc in range(NC4):
                        sl = slice(ncc * 512, ncc * 512 + 512)
                        S.wait("cPE", tok[("qk", h, ncc, 1)])
                        S.act(qh[:, sl], psS[0:64, sl], AF.Identity,
                              bias=qkb2[:, 2 * h:2 * h + 1])
                        t2 = S.act(kh[:, sl], psS[64:128, sl], AF.Identity,
                                   bias=qkb2[:, 2 * h + 1:2 * h + 2])
                    tok[("qke", h)] = t2
                blk.scalar(s_qke)

                def t_vp(t, h=h, buf=buf):
                    T = TW(ec, t)
                    T.wait("cACT", tok[("qke", h)])
                    if h == 0:
                        T.wait("ldk")
                    for kt in range(NQT):
                        bank = kt // 8
                        c0 = bank * 512 + (kt % 8) * 64
                        for dk in range(NDK):
                            T.mm(psS[:, c0:c0 + 64],
                                 xn2[:, dk * N + kt * 128:
                                     dk * N + kt * 128 + 128],
                                 buf[:, 1024 + dk * 64:1024 + dk * 64 + 64],
                                 start=(dk == 0 and kt % 8 == 0),
                                 stop=False, skip=True)
                        t2 = T.mm(psS[:, c0:c0 + 64], ones_r[:, :],
                                  vbr[0:1, h * 64:h * 64 + 64],
                                  start=False, stop=(kt % 8 == 7),
                                  skip=True, mark=(kt % 8 == 7))
                    tok[("prj", h)] = t2
                blk.tensor(t_vp)

                def s_ve(s, h=h, vh=vh):
                    S = SW(ec, s)
                    S.wait("cPE", tok[("prj", h)])
                    S.copy(vh[:, 0:512], psS[:, 0:512])
                    tok[("pev", h)] = S.copy(vh[:, 512:1024],
                                             psS[:, 512:1024])
                blk.scalar(s_ve)

                def t_sc(t, hp=hp, j=j, h=h, qh=qh, kh=kh):
                    T = TW(ec, t)
                    for kt in range(NQT):
                        if kt == 0:
                            T.wait("cACT", tok[("pev", h)])
                        else:
                            T.wait_future("cACT", tok[("exp", h, kt - 1)])
                        for b in range(4):
                            t2 = T.mm(psS[:, b * 512:b * 512 + 512],
                                      kh[:, kt * 128:kt * 128 + 128],
                                      qh[:, b * 512:b * 512 + 512],
                                      start=True, stop=True, skip=True,
                                      mark=(b == 3))
                        tok[("qkm", h, kt)] = t2
                        T.wait_future("cDVE", tok[("vt", h, kt)])
                        for b in range(4):
                            t2 = T.mm(psC[j * 64:j * 64 + 64,
                                          b * 512:b * 512 + 512],
                                      vt[:, :],
                                      eT[:, b * 512:b * 512 + 512],
                                      start=(kt == 0), stop=(kt == NQT - 1),
                                      skip=True, mark=(b == 3))
                        tok[("pvm", h, kt)] = t2
                vt0 = ec.cnt["cDVE"]
                ex0 = ec.cnt["cACT"]
                for kt in range(NQT):
                    tok[("vt", h, kt)] = vt0 + 4 * kt + 4
                    tok[("expf", h, kt)] = ex0 + kt + 1
                    tok[("exp", h, kt)] = ex0 + kt + 1
                blk.tensor(t_sc)

                def v_sc(v, h=h, vh=vh, vt0=vt0):
                    V = VW(ec, v)
                    for kt in range(NQT):
                        V.wait("cPE", tok[("qkm", h, kt)])
                        V.red(mxc[:, 0:1],
                              ap3(psS[:, :], 0, [[2048, 128], [1, 2048]]),
                              op=ALU.max, negate=True)
                        V.ts(mxc[:, 0:1], mxc[:, 0:1], 0.125)
                        V.wait_future("cACT", tok[("expf", h, kt)])
                        V.recip(dsc[:, 0:1], dsc[:, 0:1])
                        if kt >= 1:
                            V.wait("cPE", tok[("pvm", h, kt - 1)])
                        t2 = V.ts(vt[:, :], vh[:, kt * 64:kt * 64 + 64],
                                  dsc[:, 0:1], op0=ALU.mult)
                        assert t2 == tok[("vt", h, kt)]
                blk.vector(v_sc)

                def s_sc(s, h=h, ex0=ex0):
                    S = SW(ec, s)
                    for kt in range(NQT):
                        S.wait("cDVE", vt0 + 4 * kt + 2)
                        if kt >= 1:
                            S.wait("cPE", tok[("pvm", h, kt - 1)])
                        t2 = S.act(eT[:, :], psS[:, :], AF.Exp,
                                   bias=mxc[:, 0:1], scale=0.125,
                                   accum_out=dsc[:, 0:1])
                        assert t2 == ex0 + kt + 1
                blk.scalar(s_sc)

            def s_ce(s, hp=hp):
                S = SW(ec, s)
                S.wait("cPE", tok[("pvm", 2 * hp + 1, NQT - 1)])
                for b in range(4):
                    t2 = S.copy(ao2[:, hp * N + b * 512:hp * N + b * 512 + 512],
                                psC[:, b * 512:b * 512 + 512])
                tok[("ctxev", hp)] = t2
            blk.scalar(s_ce)

        def t_pool(t):
            T = TW(ec, t)
            T.wait("cACT", tok[("exp", 15, NQT - 1)])
            for ncc in range(NC4):
                sl = slice(ncc * 512, ncc * 512 + 512)
                for dk in range(NDK):
                    t2 = T.mm(psS[0:64, sl], poolind[:, :],
                              xn2[:, dk * N + ncc * 512:
                                  dk * N + ncc * 512 + 512],
                              start=(dk == 0), stop=(dk == NDK - 1),
                              skip=True, mark=(dk == NDK - 1))
                tok[("pool", ncc)] = t2
        blk.tensor(t_pool)

        def s_pe(s):
            S = SW(ec, s)
            for ncc in range(NC4):
                sl = slice(ncc * 512, ncc * 512 + 512)
                S.wait("cPE", tok[("pool", ncc)])
                t2 = S.copy(pooled[:, sl], psS[0:64, sl])
            tok["poole"] = t2
        blk.scalar(s_pe)

        for oc in range(NDK):
            def g_f(g, oc=oc):
                G = GW(ec, g)
                if oc >= 2:
                    G.wait_future("cPE", tok[("fin", (oc - 1) * NC4 - 1)])
                h0 = (oc % 2) * 1024
                sname = "ldfA" if oc % 2 == 0 else "ldfB"
                G.dma(sname, warA[:, h0:h0 + 1024],
                      bass.AP(lwT_h.tensor, oc * 128,
                              [[D, 128], [128 * D, NDK], [1, 128]]))
                tok[("fl", oc)] = G.dma(
                    sname, warB[:, h0:h0 + 1024],
                    bass.AP(wT_h.tensor, oc * 128,
                            [[D, 128], [128 * D, NDK], [1, 128]]))
            blk.gpsimd(g_f)

            for ncc in range(NC4):
                idx = oc * NC4 + ncc
                bank = idx % 4
                half = idx % 2

                def t_fin(t, oc=oc, ncc=ncc, idx=idx, bank=bank):
                    T = TW(ec, t)
                    h0 = (oc % 2) * 1024
                    if ncc == 0:
                        T.wait("ldfA" if oc % 2 == 0 else "ldfB",
                               tok[("fl", oc)])
                    if idx == 0:
                        T.wait("cACT", tok["poole"])
                        T.wait("cACT", tok[("ctxev", 7)])
                    if idx >= 4:
                        T.wait_future("cDVE", tok[("fst", idx - 4)])
                    ps = psC[:, bank * 512:bank * 512 + 512]
                    for dk in range(NDK):
                        T.mm(ps, warA[:, h0 + dk * 128:h0 + dk * 128 + 128],
                             ao2[:, dk * N + ncc * 512:
                                 dk * N + ncc * 512 + 512],
                             start=(dk == 0), stop=False, skip=True)
                    for dk in range(NDK):
                        T.mm(ps, warB[:, h0 + dk * 128:h0 + dk * 128 + 128],
                             xn2[:, dk * N + ncc * 512:
                                 dk * N + ncc * 512 + 512],
                             start=False, stop=False, skip=True)
                    t2 = T.mm(ps, dishaB[:, oc * 128:oc * 128 + 128],
                              pooled[:, ncc * 512:ncc * 512 + 512],
                              start=False, stop=True, skip=True, mark=True)
                    tok[("fin", idx)] = t2
                blk.tensor(t_fin)

                def v_fin(v, oc=oc, ncc=ncc, idx=idx, bank=bank, half=half):
                    V = VW(ec, v)
                    V.wait("cPE", tok[("fin", idx)])
                    if idx >= 2:
                        sname = "st0" if half == 0 else "st1"
                        V.wait_future(sname, 16 * (idx // 2))
                    ps = psC[:, bank * 512:bank * 512 + 512]
                    t2 = V.stt(outs[:, half * 512:half * 512 + 512],
                               ps, lbc[:, oc:oc + 1],
                               x1[:, oc * N + ncc * 512:
                                  oc * N + ncc * 512 + 512],
                               ALU.add, ALU.add)
                    tok[("fst", idx)] = t2
                blk.vector(v_fin)

        def sy_st(s):
            for oc in range(NDK):
                for ncc in range(NC4):
                    idx = oc * NC4 + ncc
                    half = idx % 2
                    ec.wait(s, "cDVE", tok[("fst", idx)])
                    ec.inc("st0" if half == 0 else "st1", s.dma_start(
                        out=bass.AP(out_h.tensor, oc * 128 * N + ncc * 512,
                                    [[N, 128], [1, 512]]),
                        in_=outs[:, half * 512:half * 512 + 512]), 16)
            ec.wait(s, "st0")
            ec.wait(s, "st1")
        blk.sync(sy_st)


def build_l1():
    nc = bass.Bass("TRN2", target_bir_lowering=False)
    io = {n: nc.dram_tensor(n, sh, dt, kind="ExternalInput").ap()
          for (n, sh, dt) in _L1_IO}
    for n in ("x", "ao", "z6"):
        io[n] = nc.dram_tensor(n, [D, N], F32, kind="ExternalInput").ap()
    io["out"] = nc.dram_tensor("out", [D, N], F32,
                               kind="ExternalOutput").ap()
    with nc.Block() as blk:
        ec = EC(nc, blk)
        emit_l1(nc, blk, ec, io)
        ec.close()
    return nc


def kernel(**inputs):
    inp = {k: np.asarray(v) for k, v in inputs.items()}
    x = np.asarray(inp["L0_attn_x"], np.float32)
    B = x.shape[0]
    hc = host_consts()
    T_ = lambda a: np.ascontiguousarray(np.asarray(a, np.float32).T)
    cores = list(range(B))
    maps = [{"x": np.ascontiguousarray(x[b].T)} for b in range(B)]
    res = run_bass_kernel_spmd(build_pre(), maps, core_ids=cores)
    xn = [np.asarray(res.results[b]["xn"]) for b in range(B)]
    com = {
        "wq": T_(inp["L0_attn_Wq"]), "wk": T_(inp["L0_attn_Wk"]),
        "wv": T_(inp["L0_attn_Wv"]), "wo": T_(inp["L0_attn_Wo"]),
        "bq": percol(inp["L0_attn_bq"]), "bk": percol(inp["L0_attn_bk"]),
        "bo": percol(inp["L0_attn_bo"]),
        "bv": np.asarray(inp["L0_attn_bv"], np.float32).reshape(1, -1),
        "lam": np.asarray(inp["L0_attn_prior_lambda"],
                          np.float32).reshape(1),
        "mask8": hc["mask8"], "ind16": hc["ind16"],
        "idnf": np.eye(16, dtype=np.float32),
    }
    maps = [dict(com, xn=xn[b]) for b in range(B)]
    res = run_bass_kernel_spmd(build_l0attn(), maps, core_ids=cores)
    ao = [np.asarray(res.results[b]["ao"]) for b in range(B)]
    nc_i = build_inner(nlayers=IDEPTH)
    maps = [inner_chain_inmap(xn[b], inp) for b in range(B)]
    res = run_bass_kernel_spmd(nc_i, maps, core_ids=cores)
    z = [np.asarray(res.results[b]["zo"]) for b in range(B)]
    maps = [l1_inmap(np.ascontiguousarray(x[b].T), ao[b], z[b], inp)
            for b in range(B)]
    res = run_bass_kernel_spmd(build_l1(), maps, core_ids=cores)
    out = np.stack([np.asarray(res.results[b]["out"]).T for b in range(B)])
    return np.ascontiguousarray(out, np.float32)


# version 03
# speedup vs baseline: 1.0090x; 1.0014x over previous
import math
import numpy as np
import ml_dtypes
import concourse.bass as bass
import concourse.mybir as mybir
from concourse.bass_utils import run_bass_kernel_spmd
from contextlib import ExitStack

F32 = mybir.dt.float32
BF16 = mybir.dt.bfloat16
AF = mybir.ActivationFunctionType
ALU = mybir.AluOpType
AX = mybir.AxisListType
nbf = ml_dtypes.bfloat16

D = 1024
N = 2048
H = 16
HD = 64
NDK = D // 128
NQT = N // 128
NC4 = N // 512
DI = 2730
W1C = 2 * DI
IDEPTH = 6
WIN = 256

BIGNEG = -1.0e30


def _f_band(delta):
    d = np.abs(np.asarray(delta, np.float64))
    m = 0.8 * np.exp(-((d - 1.0) ** 2) / (2.0 * 0.5 ** 2)) \
        + 0.4 * np.exp(-((d - 30.0) ** 2) / (2.0 * 2.0 ** 2))
    return m.astype(np.float32)


_HOST_CONSTS = None


def host_consts():
    global _HOST_CONSTS
    if _HOST_CONSTS is not None:
        return _HOST_CONSTS
    c = {}
    c["idn_bf"] = np.eye(128, dtype=np.float32).astype(nbf)
    c["idn_f32"] = np.eye(128, dtype=np.float32)
    p = np.arange(128)[:, None]
    j = np.arange(256)[None, :]
    c["mask8"] = (8.0 * _f_band(j - 61 - p)).astype(np.float32)
    c["f_max"] = float(_f_band(np.arange(-100, 100)).max())
    jj = np.arange(384)[None, :]
    c["mask01"] = (((jj >= p) & (jj <= p + WIN)).astype(np.float32)).astype(nbf)
    ind = np.zeros((128, NDK, 16), np.float32)
    for dk in range(NDK):
        ind[0:64, dk, 2 * dk] = 1.0
        ind[64:128, dk, 2 * dk + 1] = 1.0
    c["ind16"] = ind.reshape(128, NDK * 16).astype(nbf)
    selb = np.zeros((16, 128), np.float32)
    selb[0, 0:64] = 1.0
    selb[8, 64:128] = 1.0
    c["selb"] = selb.astype(nbf)
    sel = np.zeros((128, 16, 16), np.float32)
    for h in range(16):
        sel[:, h, h] = 1.0
    c["sel16"] = sel.reshape(128, 256).astype(nbf)
    pr = np.zeros((128, 64), np.float32)
    pr[np.arange(128), np.arange(128) % 64] = 1.0
    c["poolind"] = pr.astype(nbf)
    half = np.arange(0, HD, 2, dtype=np.float64)
    inv_freq = 1.0 / 10000.0 ** (half / HD)
    t = np.arange(N, dtype=np.float64)
    freqs = t[:, None] * inv_freq[None, :]
    freqs = np.concatenate([freqs, freqs], -1)
    base_scale = (half + 0.4 * HD) / (1.4 * HD)
    power = (t - N // 2) / (WIN // 2)
    scale = base_scale[None, :] ** power[:, None]
    scale = np.concatenate([scale, scale], -1)
    cos = np.cos(freqs)
    sin = np.sin(freqs)
    sign = np.where(np.arange(HD) < HD // 2, -1.0, 1.0)
    cs = (cos * scale).astype(np.float32)
    ss = (sin * scale * sign[None, :]).astype(np.float32)
    ci = (cos / scale).astype(np.float32)
    si = (sin / scale * sign[None, :]).astype(np.float32)

    def fm(tab):
        return np.concatenate([tab.T, tab.T], axis=0).astype(np.float32)
    c["rot_cs"] = fm(cs)
    c["rot_ss"] = fm(ss)
    c["rot_ci"] = fm(ci)
    c["rot_si"] = fm(si)
    _HOST_CONSTS = c
    return c


def percol(v):
    return np.ascontiguousarray(np.asarray(v, np.float32).reshape(NDK, 128).T)


_CHAIN_SEMS = ("cPE", "cDVE", "cACT", "cPL")


class EC:

    def __init__(self, nc, blk):
        self.nc = nc
        self.blk = blk
        self.px = ""
        self.sems = {}
        self.cnt = {}
        self._exit = []
        self._fut = {}
        for c in _CHAIN_SEMS:
            self.sem(c)

    def _n(self, name):
        return name if name in _CHAIN_SEMS else self.px + name

    def c(self, name):
        return self.cnt.get(self._n(name), 0)

    def _mk(self, full):
        if full not in self.sems:
            cm = self.nc.semaphore(full)
            h = cm.__enter__()
            self._exit.append(cm)
            self.sems[full] = h
            self.cnt[full] = 0
        return self.sems[full]

    def sem(self, name):
        return self._mk(self._n(name))

    def inc(self, name, inst, n=1):
        name = self._n(name)
        inst.then_inc(self._mk(name), n)
        self.cnt[name] += n
        return self.cnt[name]

    def wait(self, eng, name, val=None):
        name = self._n(name)
        if val is None:
            val = self.cnt[name]
        if val <= 0:
            return
        assert val <= self.cnt.get(name, 0), (name, val, self.cnt.get(name))
        eng.wait_ge(self._mk(name), val)

    def wait_future(self, eng, name, val):
        name = self._n(name)
        if val <= 0:
            return
        self._fut[name] = max(self._fut.get(name, 0), val)
        eng.wait_ge(self._mk(name), val)

    def barrier(self):
        snap = [(n, v) for n, v in self.cnt.items() if v > 0]

        def mk(eng):
            for n, v in snap:
                eng.wait_ge(self.sems[n], v)
        for attr in ("tensor", "vector", "scalar", "gpsimd", "sync"):
            getattr(self.blk, attr)(mk)

    def close(self):
        for name, val in self._fut.items():
            assert val <= self.cnt.get(name, 0), \
                (f"unsatisfied future wait {name}", val, self.cnt.get(name))
        for cm in reversed(self._exit):
            cm.__exit__(None, None, None)


class _EngW:
    CHAIN = None

    def __init__(self, ec, raw):
        self.ec = ec
        self.r = raw
        ec.sem(self.CHAIN)

    def _do(self, f, *a, **kw):
        ec = self.ec
        c = self.CHAIN
        ec.wait(self.r, c)
        inst = f(*a, **kw)
        return ec.inc(c, inst)

    def wait(self, name, val=None):
        self.ec.wait(self.r, name, val)

    def wait_future(self, name, val):
        self.ec.wait_future(self.r, name, val)


class VW(_EngW):
    CHAIN = "cDVE"

    def tt(self, out, a, b, op=ALU.mult):
        return self._do(self.r.tensor_tensor, out, a, b, op)

    def add(self, out, a, b):
        return self._do(self.r.tensor_add, out, a, b)

    def sub(self, out, a, b):
        return self._do(self.r.tensor_sub, out, a, b)

    def ts(self, out, a, s1, s2=None, op0=ALU.mult, op1=ALU.bypass,
           accum_out=None):
        return self._do(self.r.tensor_scalar, out, a, s1, s2, op0, op1,
                        accum_out=accum_out)

    def stt(self, out, in0, scalar, in1, op0=ALU.mult, op1=ALU.mult,
            accum_out=None):
        return self._do(self.r.scalar_tensor_tensor, out, in0, scalar, in1,
                        op0, op1, accum_out=accum_out)

    def red(self, out, in_, op=ALU.max, negate=False):
        return self._do(self.r.tensor_reduce, out, in_, AX.X, op, negate=negate)

    def recip(self, out, in_):
        return self._do(self.r.reciprocal, out, in_)

    def copy(self, out, in_):
        return self._do(self.r.tensor_copy, out, in_)

    def memset(self, ap, v):
        return self._do(self.r.memset, ap, v)


class SW(_EngW):
    CHAIN = "cACT"

    def act(self, out, in_, func, bias=0.0, scale=1.0, accum_out=None):
        return self._do(self.r.activation, out, in_, func, bias=bias,
                        scale=scale, accum_out=accum_out)

    def copy(self, out, in_):
        return self._do(self.r.copy, out, in_)

    def mul(self, out, in_, c):
        return self._do(self.r.mul, out, in_, c)


class GW(_EngW):
    CHAIN = "cPL"

    def tt(self, out, a, b, op=ALU.mult):
        return self._do(self.r.tensor_tensor, out, a, b, op)

    def copy(self, out, in_):
        return self._do(self.r.tensor_copy, out, in_)

    def memset(self, ap, v):
        return self._do(self.r.memset, ap, v)

    def dma(self, sem, out, in_):
        return self.ec.inc(sem, self.r.dma_start(out=out, in_=in_), 16)


class TW:
    CHAIN = "cPE"

    def __init__(self, ec, raw):
        self.ec = ec
        self.r = raw
        ec.sem(self.CHAIN)

    def mm(self, out, lhsT, rhs, start, stop, mark=False, skip=False):
        inst = self.r.matmul(out, lhsT, rhs, start=start, stop=stop,
                             skip_group_check=skip)
        if mark:
            return self.ec.inc(self.CHAIN, inst)
        return None

    def trans(self, out, in_, idn, mark=False):
        inst = self.r.transpose(out, in_, idn)
        if mark:
            return self.ec.inc(self.CHAIN, inst)
        return None

    def wait(self, name, val=None):
        self.ec.wait(self.r, name, val)

    def wait_future(self, name, val):
        self.ec.wait_future(self.r, name, val)




def _alloc(nc, st, sbufs, psums, prefix=""):
    d = {}
    for name, shape, dt in sbufs:
        d[name] = st.enter_context(
            nc.sbuf_tensor(prefix + name, list(shape), dt))
    for name, shape in psums:
        d[name] = st.enter_context(
            nc.psum_tensor(prefix + name, list(shape), F32))
    return d


def ap3(t, off, dims):
    return bass.AP(t.tensor, t.offset + off, dims)


def emit_ln(ec, xbf_sl, sq_scr, ones_c, ones_r, ps_stats, ps_bc, sums_bf,
            mu_b, rstd_b, tmp_b, std_b, want_aug=False, n_tok=N,
            dep=None):
    blk = ec.blk
    nch = n_tok // 512
    sq_tok = [None] * NDK
    st_tok = [None]
    cons_tok = [None] * NDK

    def pool_fn(g):
        G = GW(ec, g)
        for d_ in (dep or []):
            G.wait_future(*d_) if d_[1] > ec.cnt.get(d_[0], 0) else G.wait(*d_)
        for dk in range(NDK):
            if dk >= 2:
                G.wait_future("cPE", cons_tok[dk - 2])
            x = xbf_sl(dk)
            sq_tok[dk] = G.tt(
                sq_scr[:, (dk % 2) * n_tok:(dk % 2) * n_tok + n_tok],
                x, x, ALU.mult)
    pe0 = ec.cnt["cPE"] if "cPE" in ec.cnt else 0
    ec.sem("cPE")
    pe0 = ec.cnt["cPE"]
    for dk in range(NDK):
        cons_tok[dk] = pe0 + dk + 1
    blk.gpsimd(pool_fn)

    def t_fn(t):
        T = TW(ec, t)
        for dk in range(NDK):
            T.wait("cPL", sq_tok[dk])
            for ncc in range(nch):
                sl = slice(ncc * 512, ncc * 512 + 512)
                t.matmul(ps_stats[0:1, sl], ones_c[:, :], xbf_sl(dk)[:, sl],
                         start=(dk == 0), stop=(dk == NDK - 1))
                mark = (ncc == nch - 1)
                tok = T.mm(ps_stats[32:33, sl], ones_c[:, :],
                           sq_scr[:, (dk % 2) * n_tok + ncc * 512:
                                  (dk % 2) * n_tok + ncc * 512 + 512],
                           start=(dk == 0), stop=(dk == NDK - 1), mark=mark)
                if mark:
                    assert tok == cons_tok[dk], (tok, cons_tok[dk])
        st_tok[0] = cons_tok[NDK - 1]
    blk.tensor(t_fn)

    hi_tok = [None] * nch
    lo_tok = [None] * nch
    bc_tok = [None] * nch
    be_tok = [None] * nch
    pe_now = ec.cnt["cPE"]
    for i in range(nch):
        bc_tok[i] = pe_now + i + 1

    def s_hilo(s):
        S = SW(ec, s)
        for ncc in range(nch):
            sl = slice(ncc * 512, ncc * 512 + 512)
            if ncc == 0:
                S.wait("cPE", st_tok[0])
            S.copy(sums_bf[0:1, 0:512], ps_stats[0:1, sl])
            hi_tok[ncc] = S.copy(sums_bf[0:1, 1024:1536], ps_stats[32:33, sl])
            S.wait_future("cPE", bc_tok[ncc])
            S.copy(mu_b[:, sl], ps_bc[:, 0:512])
            be_tok[ncc] = S.copy(std_b[:, sl], ps_bc[:, 512:1024])
    blk.scalar(s_hilo)

    def v_lo(v):
        V = VW(ec, v)
        for ncc in range(nch):
            sl = slice(ncc * 512, ncc * 512 + 512)
            V.wait_future("cACT", hi_tok[ncc])
            V.sub(sums_bf[0:1, 512:1024], ps_stats[0:1, sl],
                  sums_bf[0:1, 0:512])
            lo_tok[ncc] = V.sub(sums_bf[0:1, 1536:2048], ps_stats[32:33, sl],
                                sums_bf[0:1, 1024:1536])
    blk.vector(v_lo)

    def t_bc(t):
        T = TW(ec, t)
        for ncc in range(nch):
            T.wait_future("cDVE", lo_tok[ncc])
            if ncc >= 1:
                T.wait_future("cACT", be_tok[ncc - 1])
            t.matmul(ps_bc[:, 0:512], ones_r[:, :], sums_bf[0:1, 0:512],
                     start=True, stop=False)
            t.matmul(ps_bc[:, 0:512], ones_r[:, :], sums_bf[0:1, 512:1024],
                     start=False, stop=True)
            t.matmul(ps_bc[:, 512:1024], ones_r[:, :], sums_bf[0:1, 1024:1536],
                     start=True, stop=False)
            tok = T.mm(ps_bc[:, 512:1024], ones_r[:, :],
                       sums_bf[0:1, 1536:2048], start=False, stop=True,
                       mark=True)
            assert tok == bc_tok[ncc], (tok, bc_tok[ncc])
    blk.tensor(t_bc)

    out_tok = {}

    def v_fin(v):
        V = VW(ec, v)
        V.wait_future("cACT", be_tok[nch - 1])
        V.ts(mu_b[:, :], mu_b[:, :], 1.0 / D)
        V.ts(std_b[:, :], std_b[:, :], 1.0 / D)
        V.tt(tmp_b[:, :], mu_b[:, :], mu_b[:, :], ALU.mult)
        V.sub(std_b[:, :], std_b[:, :], tmp_b[:, :])
        out_tok["var"] = V.ts(std_b[:, :], std_b[:, :], 1e-5, op0=ALU.add)
    blk.vector(v_fin)

    def s_sqrt(s):
        S = SW(ec, s)
        S.wait("cDVE", out_tok["var"])
        out_tok["std"] = S.act(std_b[:, :], std_b[:, :], AF.Sqrt)
    blk.scalar(s_sqrt)

    def v_fin2(v):
        V = VW(ec, v)
        V.wait("cACT", out_tok["std"])
        out_tok["rstd"] = V.recip(rstd_b[:, :], std_b[:, :])
        if want_aug:
            out_tok["aug"] = V.stt(tmp_b[:, :], mu_b[:, :], -1.0,
                                   rstd_b[:, :], ALU.mult, ALU.mult)
        out_tok["dn"] = ec.cnt["cDVE"]
    blk.vector(v_fin2)
    return out_tok


def emit_pre(nc, blk, ec, io, prefix="p_"):
    x_h = io["x"]
    xn_h = io["xn"]
    st_ = ExitStack()
    with st_:
        _sb = [
            ("xf", (128, NDK * N), F32),
            ("xb", (128, NDK * N), BF16),
            ("sq", (128, 2 * N), BF16),
            ("sums_bf", (1, 2048), BF16),
            ("mu_b", (128, N), F32),
            ("rstd_b", (128, N), F32),
            ("tmp_b", (128, N), F32),
            ("std_b", (128, N), F32),
            ("ones_c", (128, 1), BF16),
            ("ones_r", (1, 128), BF16),
            ("xt", (128, 2 * N), F32),
        ]
        _ps = [("ps_st", (128, 2048)), ("ps_bc", (128, 1024))]
        _t = _alloc(nc, st_, _sb, _ps, prefix=prefix)
        xf = _t["xf"]; xb = _t["xb"]; sqs = _t["sq"]
        sums_bf = _t["sums_bf"]; mu_b = _t["mu_b"]; rstd_b = _t["rstd_b"]
        tmp_b = _t["tmp_b"]; std_b = _t["std_b"]; ones_c = _t["ones_c"]
        ones_r = _t["ones_r"]; xt = _t["xt"]
        ps_st = _t["ps_st"]; ps_bc = _t["ps_bc"]
        for s in ("ld", "ldb", "st0", "st1"):
            ec.sem(s)

        def g_load(g):
            g.memset(ones_c[:, :], 1.0)
            g.memset(ones_r[:, :], 1.0)
            for dk in range(NDK):
                ec.inc("ldb", g.dma_start(
                    out=xb[:, dk * N:(dk + 1) * N],
                    in_=x_h[dk * 128:(dk + 1) * 128, :]), 16)
        blk.gpsimd(g_load)

        def s_load(s):
            for dk in range(NDK):
                ec.inc("ld", s.dma_start(
                    out=xf[:, dk * N:(dk + 1) * N],
                    in_=x_h[dk * 128:(dk + 1) * 128, :]), 16)
        blk.sync(s_load)

        tok = emit_ln(ec,
                      xbf_sl=lambda dk: xb[:, dk * N:(dk + 1) * N],
                      sq_scr=sqs, ones_c=ones_c, ones_r=ones_r,
                      ps_stats=ps_st, ps_bc=ps_bc, sums_bf=sums_bf,
                      mu_b=mu_b, rstd_b=rstd_b, tmp_b=tmp_b, std_b=std_b,
                      dep=[("ldb", NDK * 16)])

        xn_tok = [None] * NDK

        def v_xn(v):
            V = VW(ec, v)
            V.wait("cDVE", tok["dn"])
            V.wait("ld")
            for dk in range(NDK):
                b = (dk % 2) * N
                if dk >= 2:
                    V.wait_future("st%d" % (dk % 2), (dk // 2) * 16)
                V.sub(xt[:, b:b + N], xf[:, dk * N:(dk + 1) * N], mu_b[:, :])
                xn_tok[dk] = V.tt(xt[:, b:b + N], xt[:, b:b + N],
                                  rstd_b[:, :], ALU.mult)
        blk.vector(v_xn)

        def s_store(s):
            for dk in range(NDK):
                b = (dk % 2) * N
                ec.wait(s, "cDVE", xn_tok[dk])
                ec.inc("st%d" % (dk % 2), s.dma_start(
                    out=xn_h[dk * 128:(dk + 1) * 128, :],
                    in_=xt[:, b:b + N]), 16)
            ec.wait(s, "st0")
            ec.wait(s, "st1")
        blk.sync(s_store)


def build_pre():
    nc = bass.Bass("TRN2", target_bir_lowering=False)
    io = {"x": nc.dram_tensor("x", [D, N], F32, kind="ExternalInput").ap(),
          "xn": nc.dram_tensor("xn", [D, N], F32,
                               kind="ExternalOutput").ap()}
    with nc.Block() as blk:
        ec = EC(nc, blk)
        emit_pre(nc, blk, ec, io)
        ec.close()
    return nc


_L0A_IO = (("xn", [D, N], F32), ("wq", [D, D], F32), ("wk", [D, D], F32),
           ("wv", [D, D], F32), ("wo", [D, D], F32),
           ("bq", [128, NDK], F32), ("bk", [128, NDK], F32),
           ("bv", [1, D], F32), ("bo", [128, NDK], F32),
           ("lam", [1], F32), ("mask8", [128, 256], F32),
           ("ind16", [128, NDK * 16], BF16), ("idnf", [16, 16], F32))


def emit_l0attn(nc, blk, ec, io, prefix="a_", nits=32, no_mask=False,
                no_rd=False, no_pvd=False, no_exp=False, no_rdb=False):
    xn_h = io["xn"]; wq_h = io["wq"]; wk_h = io["wk"]; wv_h = io["wv"]
    wo_h = io["wo"]; bq_h = io["bq"]; bk_h = io["bk"]; bv_h = io["bv"]
    bo_h = io["bo"]; lam_h = io["lam"]; m8_h = io["mask8"]
    ind_h = io["ind16"]; idnf_h = io["idnf"]; ao_h = io["ao"]
    FMAX = host_consts()["f_max"]

    st_ = ExitStack()
    with st_:
        _sb = [
            ("sc32", (128, NDK * N), BF16),
            ("q_sb", (128, NDK * N), BF16),
            ("k_sb", (128, NDK * N), BF16),
            ("v_sb", (128, NQT * 1024), BF16),
            ("war", (128, NDK * 2048), BF16),
            ("wvr", (128, NDK * 1024), BF16),
            ("q2", (128, 2 * N), BF16),
            ("eAB", (128, 4 * 512), BF16),
            ("rds", (1, 1024), F32),
            ("rds_bf", (1, 2048), BF16),
            ("hob", (1, 256), BF16),
            ("rdb", (128, 512), F32),
            ("out_t", (128, 512), F32),
            ("vb_b", (128, 1024), BF16),
            ("bq4", (128, NDK), F32),
            ("bk4", (128, NDK), F32),
            ("bo4", (128, NDK), F32),
            ("bvr", (1, D), BF16),
            ("lamc", (128, 1), F32),
            ("maskl", (128, 256), F32),
            ("nm", (16, 8), F32),
            ("cnrow", (1, 16), BF16),
            ("cneg_b", (128, 16), F32),
            ("idn16", (16, 16), F32),
            ("ind_s", (128, NDK * 16), BF16),
            ("ones_c", (128, 1), BF16),
            ("ones_r", (1, 128), BF16),
        ]
        _ps = [
            ("psA", (128, 1024)),
            ("psB", (128, 1024)),
            ("psC", (128, 1024)),
            ("psE", (128, 1024)),
        ]
        _t = _alloc(nc, st_, _sb, _ps, prefix=prefix)
        (sc32, q_sb, k_sb, v_sb, war, wvr, q2, eAB, rds, rds_bf, hob, rdb, out_t, vb_b, bq4, bk4, bo4, bvr, lamc, maskl, nm, cnrow, cneg_b, idn16, ind_s, ones_c, ones_r, psA, psB, psC, psE,) = (_t["sc32"], _t["q_sb"], _t["k_sb"], _t["v_sb"], _t["war"], _t["wvr"], _t["q2"], _t["eAB"], _t["rds"], _t["rds_bf"], _t["hob"], _t["rdb"], _t["out_t"], _t["vb_b"], _t["bq4"], _t["bk4"], _t["bo4"], _t["bvr"], _t["lamc"], _t["maskl"], _t["nm"], _t["cnrow"], _t["cneg_b"], _t["idn16"], _t["ind_s"], _t["ones_c"], _t["ones_r"], _t["psA"], _t["psB"], _t["psC"], _t["psE"],)
        for sname in ("ld", "ldg", "ldw", "ldwo", "st0", "st1"):
            ec.sem(sname)
        tok = {}

        def g_load(g):
            G = GW(ec, g)
            G.memset(ones_c[:, :], 1.0)
            G.memset(ones_r[:, :], 1.0)
            tok["hob0"] = G.memset(hob[:, :], 0.0)
            for dk in range(NDK):
                G.dma("ldg", sc32[:, dk * N:(dk + 1) * N],
                      xn_h[dk * 128:(dk + 1) * 128, :])
            for (dst, src, off) in ((war, wq_h, 0), (war, wk_h, NDK * 1024),
                                    (wvr, wv_h, 0)):
                for dk in range(NDK):
                    G.dma("ldw", dst[:, off + dk * 1024:off + (dk + 1) * 1024],
                          src[dk * 128:(dk + 1) * 128, :])
            G.dma("ldg", bvr[:, :], bv_h[:, :])
            G.dma("ldw", ind_s[:, :], ind_h[:, :])
        blk.gpsimd(g_load)

        def s_load(s):
            for (dst, src) in ((bq4, bq_h), (bk4, bk_h), (bo4, bo_h),
                               (maskl, m8_h)):
                ec.inc("ld", s.dma_start(out=dst[:, :], in_=src[:, :]), 16)
            lsrc = bass.AP(lam_h.tensor, 0, [[0, 128], [1, 1]])
            ec.inc("ld", s.dma_start(out=lamc[:, :], in_=lsrc), 16)
            ec.inc("ld", s.dma_start(out=idn16[:, :], in_=idnf_h[:, :]), 16)
        blk.sync(s_load)

        def v_init(v):
            V = VW(ec, v)
            V.wait("cPL", tok["hob0"])
            V.memset(hob[0:1, 0:64], 1.0)
            tok["hob"] = V.memset(hob[0:1, 192:256], 1.0)
            V.wait("ld")
            tok["maskl"] = V.ts(maskl[:, :], maskl[:, :], lamc[:, 0:1],
                                op0=ALU.mult)
        blk.vector(v_init)

        def t_proj(t):
            T = TW(ec, t)
            T.wait("ld")
            T.wait("ldg")
            T.wait("ldw")
            for idx in range(64):
                ti, oc, ncc = idx // 32, (idx // NC4) % NDK, idx % NC4
                woff = ti * NDK * 1024
                pp = idx % 2
                bank = psA[:, pp * 512:pp * 512 + 512]
                if idx >= 2:
                    T.wait_future("cACT", tok[("qk_base",)] + idx - 1)
                for dk in range(NDK):
                    T.mm(bank,
                         war[:, woff + dk * 1024 + oc * 128:
                             woff + dk * 1024 + oc * 128 + 128],
                         sc32[:, dk * N + ncc * 512:dk * N + ncc * 512 + 512],
                         start=(dk == 0), stop=(dk == NDK - 1),
                         mark=(dk == NDK - 1))
        tok[("qk_base",)] = ec.cnt["cACT"]
        pe0 = ec.cnt["cPE"]
        blk.tensor(t_proj)

        def s_proj(s):
            S = SW(ec, s)
            for idx in range(64):
                ti, oc, ncc = idx // 32, (idx // NC4) % NDK, idx % NC4
                dst = q_sb if ti == 0 else k_sb
                b4 = bq4 if ti == 0 else bk4
                pp = idx % 2
                bank = psA[:, pp * 512:pp * 512 + 512]
                S.wait("cPE", pe0 + idx + 1)
                t2 = S.act(dst[:, oc * N + ncc * 512:oc * N + ncc * 512 + 512],
                           bank, AF.Identity, bias=b4[:, oc:oc + 1])
                assert t2 == tok[("qk_base",)] + idx + 1
        blk.scalar(s_proj)
        qk_done = ec.cnt["cACT"]

        def t_vb(t):
            T = TW(ec, t)
            T.wait("ldg")
            for half in range(2):
                tok[("vbb", half)] = T.mm(
                    psB[:, half * 512:half * 512 + 512], ones_r[:, :],
                    bvr[0:1, half * 512:half * 512 + 512],
                    start=True, stop=True, mark=True)
        blk.tensor(t_vb)

        def s_vb(s):
            S = SW(ec, s)
            for half in range(2):
                S.wait("cPE", tok[("vbb", half)])
                tok[("vbe", half)] = S.copy(
                    vb_b[:, half * 512:half * 512 + 512],
                    psB[:, half * 512:half * 512 + 512])
        blk.scalar(s_vb)

        vev0 = ec.cnt["cDVE"]

        def t_vproj(t):
            T = TW(ec, t)
            T.wait("cACT", tok[("vbe", 1)])
            for idx in range(32):
                nt, vc = idx // 2, idx % 2
                pp = idx % 2
                bank = psB[:, pp * 512:pp * 512 + 512]
                if idx >= 2:
                    T.wait_future("cDVE", vev0 + idx - 1)
                for dk in range(NDK):
                    T.mm(bank,
                         sc32[:, dk * N + nt * 128:dk * N + nt * 128 + 128],
                         wvr[:, dk * 1024 + vc * 512:dk * 1024 + vc * 512 + 512],
                         start=(dk == 0), stop=(dk == NDK - 1),
                         mark=(dk == NDK - 1))
        pe_v0 = ec.cnt["cPE"]
        blk.tensor(t_vproj)

        def v_vproj(v):
            V = VW(ec, v)
            for idx in range(32):
                nt, vc = idx // 2, idx % 2
                pp = idx % 2
                bank = psB[:, pp * 512:pp * 512 + 512]
                V.wait("cPE", pe_v0 + idx + 1)
                t2 = V.tt(v_sb[:, nt * 1024 + vc * 512:nt * 1024 + vc * 512 + 512],
                          bank, vb_b[:, vc * 512:vc * 512 + 512], ALU.add)
                assert t2 == vev0 + idx + 1
        blk.vector(v_vproj)
        v_done = ec.cnt["cDVE"]

        for ti in range(2):
            src = q_sb if ti == 0 else k_sb
            sq_tok = [None] * NDK
            cons_base = ec.cnt["cPE"]

            def g_sq(g, src=src, ti=ti, sq_tok=sq_tok, cons_base=cons_base):
                G = GW(ec, g)
                if ti == 0:
                    G.wait("ldw")
                    G.wait_future("cACT", qk_done)
                for dk in range(NDK):
                    if dk >= 2 or ti == 1:
                        G.wait_future("cPE", cons_base + dk - 1)
                    x = src[:, dk * N:(dk + 1) * N]
                    sq_tok[dk] = G.tt(q2[:, (dk % 2) * N:(dk % 2) * N + N],
                                      x, x, ALU.mult)
            blk.gpsimd(g_sq)

            def t_nm(t, ti=ti, sq_tok=sq_tok, cons_base=cons_base):
                T = TW(ec, t)
                if ti == 0:
                    T.wait_future("cDVE", v_done)
                else:
                    T.wait("cDVE", tok[("nm", 0)])
                for dk in range(NDK):
                    T.wait("cPL", sq_tok[dk])
                    for ncc in range(NC4):
                        ps = psA if ncc < 2 else psB
                        col = (ncc % 2) * 512
                        t2 = T.mm(ps[0:16, col:col + 512],
                                  ind_s[:, dk * 16:dk * 16 + 16],
                                  q2[:, (dk % 2) * N + ncc * 512:
                                     (dk % 2) * N + ncc * 512 + 512],
                                  start=(dk == 0), stop=(dk == NDK - 1),
                                  mark=(ncc == NC4 - 1))
                        if ncc == NC4 - 1:
                            assert t2 == cons_base + dk + 1
            blk.tensor(t_nm)

            def v_nm(v, ti=ti):
                V = VW(ec, v)
                V.wait("cPE", ec.cnt["cPE"])
                V.red(nm[0:16, 4:5], psA[0:16, 0:1024], op=ALU.max)
                V.red(nm[0:16, 5:6], psB[0:16, 0:1024], op=ALU.max)
                tok[("nm", ti)] = V.tt(nm[0:16, ti:ti + 1], nm[0:16, 4:5],
                                       nm[0:16, 5:6], ALU.max)
            blk.vector(v_nm)
        def s_nm(s):
            S = SW(ec, s)
            S.wait("cDVE", tok[("nm", 1)])
            tok["nms"] = S.act(nm[0:16, 0:2], nm[0:16, 0:2], AF.Sqrt)
        blk.scalar(s_nm)

        def v_c(v):
            V = VW(ec, v)
            V.wait("cACT", tok["nms"])
            V.tt(nm[0:16, 2:3], nm[0:16, 0:1], nm[0:16, 1:2], ALU.mult)
            V.ts(nm[0:16, 3:4], lamc[0:16, 0:1], FMAX, op0=ALU.mult)
            V.stt(nm[0:16, 2:3], nm[0:16, 2:3], 0.125, nm[0:16, 3:4],
                  ALU.mult, ALU.add)
            tok["c"] = V.ts(nm[0:16, 2:3], nm[0:16, 2:3], -1.0, op0=ALU.mult)
        blk.vector(v_c)

        def t_cn(t):
            T = TW(ec, t)
            T.wait("ld")
            T.wait("cDVE", tok["c"])
            tok["ct"] = T.trans(psE[0:1, 0:16], nm[0:16, 2:3], idn16[:, :],
                                mark=True)
        blk.tensor(t_cn)

        def s_cn(s):
            S = SW(ec, s)
            S.wait("cPE", tok["ct"])
            tok["cr"] = S.copy(cnrow[0:1, :], psE[0:1, 0:16])
        blk.scalar(s_cn)

        def t_cb(t):
            T = TW(ec, t)
            T.wait("cACT", tok["cr"])
            tok["cb"] = T.mm(psE[:, 512:528], ones_r[:, :], cnrow[0:1, :],
                             start=True, stop=True, mark=True)
        blk.tensor(t_cb)

        def s_cb(s):
            S = SW(ec, s)
            S.wait("cPE", tok["cb"])
            tok["cbe"] = S.copy(cneg_b[:, :], psE[:, 512:528])
        blk.scalar(s_cb)

        def band_ovl(kt, qc):
            c0 = max(kt * 128 - 61, qc * 512, 0)
            c1 = min(kt * 128 + 189, qc * 512 + 512, N)
            return None if c0 >= c1 else (c0, c1)

        ex_base = ec.cnt["cACT"]

        def exp_tok(it, kt, j):
            return ex_base + it * 33 + kt * 2 + j + 1

        def rdbe_tok(it):
            return ex_base + it * 33 + 33

        qk_tok = {}
        mk_tok = {}
        pv_tok = {}
        rd_tok = {}
        rdb_tok = {}
        evac_tok = {}

        for it in range(nits):
            hp, qc = it // 4, it % 4

            def t_main(t, hp=hp, qc=qc, it=it):
                T = TW(ec, t)
                if it == 0:
                    T.wait("cDVE", v_done)
                    T.wait("cACT", tok["cbe"])
                    T.wait("cDVE", tok["hob"])
                    T.wait("cDVE", tok["maskl"])
                else:
                    T.wait_future("cDVE", evac_tok[it - 1])

                def pvd(kt):
                    pp = kt % 2
                    eA = eAB[:, pp * 512:pp * 512 + 512]
                    eB = eAB[:, 1024 + pp * 512:1024 + pp * 512 + 512]
                    pv = psC[:, 0:512]
                    psD = psC[:, 512:1024]
                    T.wait_future("cACT", exp_tok(it, kt, 1))
                    if no_pvd:
                        pv_tok[(it, kt)] = ec.inc("cPE", T.r.nop())
                        return
                    st = (kt == 0)
                    sp = (kt == NQT - 1)
                    T.mm(pv[0:64, :],
                         v_sb[:, kt * 1024 + (2 * hp) * 64:
                              kt * 1024 + (2 * hp) * 64 + 64],
                         eA, start=st, stop=sp, skip=True)
                    T.mm(pv[64:128, :],
                         v_sb[:, kt * 1024 + (2 * hp + 1) * 64:
                              kt * 1024 + (2 * hp + 1) * 64 + 64],
                         eB, start=st, stop=sp, skip=True)
                    T.mm(psD[0:1, :], ones_c[:, :], eA, start=st, stop=sp,
                         skip=True)
                    pv_tok[(it, kt)] = T.mm(psE[0:1, 512:1024], ones_c[:, :],
                                            eB, start=st, stop=sp, mark=True,
                                            skip=True)

                for kt in range(NQT):
                    pp = kt % 2
                    scA = psA[:, pp * 512:pp * 512 + 512]
                    scB = psB[:, pp * 512:pp * 512 + 512]
                    if it > 0 or kt >= 2:
                        pit, pkt = (it, kt - 2) if kt >= 2 else \
                                   (it - 1, NQT - 2 + kt)
                        T.wait_future("cACT", exp_tok(pit, pkt, 1))
                    T.mm(scA, k_sb[0:64, hp * N + kt * 128:hp * N + kt * 128 + 128],
                         q_sb[0:64, hp * N + qc * 512:hp * N + qc * 512 + 512],
                         start=True, stop=True)
                    qk_tok[(it, kt)] = T.mm(
                        scB, k_sb[64:128, hp * N + kt * 128:hp * N + kt * 128 + 128],
                        q_sb[64:128, hp * N + qc * 512:hp * N + qc * 512 + 512],
                        start=True, stop=True, mark=True)
                    if kt >= 1:
                        pvd(kt - 1)
                pvd(NQT - 1)
            blk.tensor(t_main)

            def v_mask(v, hp=hp, qc=qc, it=it):
                V = VW(ec, v)
                for kt in range(NQT):
                    ov = band_ovl(kt, qc)
                    if ov is None or no_mask:
                        continue
                    pp = kt % 2
                    scA = psA[:, pp * 512:pp * 512 + 512]
                    scB = psB[:, pp * 512:pp * 512 + 512]
                    c0, c1 = ov
                    lo, hi = c0 - qc * 512, c1 - qc * 512
                    t0 = c0 - (kt * 128 - 61)
                    V.wait("cPE", qk_tok[(it, kt)])
                    V.add(scA[:, lo:hi], scA[:, lo:hi],
                          maskl[:, t0:t0 + (hi - lo)])
                    mk_tok[(it, kt)] = V.add(scB[:, lo:hi], scB[:, lo:hi],
                                             maskl[:, t0:t0 + (hi - lo)])
            blk.vector(v_mask)

            def s_exp(s, hp=hp, qc=qc, it=it):
                S = SW(ec, s)
                for kt in range(NQT):
                    pp = kt % 2
                    scA = psA[:, pp * 512:pp * 512 + 512]
                    scB = psB[:, pp * 512:pp * 512 + 512]
                    eA = eAB[:, pp * 512:pp * 512 + 512]
                    eB = eAB[:, 1024 + pp * 512:1024 + pp * 512 + 512]
                    S.wait("cPE", qk_tok[(it, kt)])
                    if (it, kt) in mk_tok:
                        S.wait("cDVE", mk_tok[(it, kt)])
                    if it > 0 or kt >= 2:
                        pit, pkt = (it, kt - 2) if kt >= 2 else \
                                   (it - 1, NQT - 2 + kt)
                        S.wait("cPE", pv_tok[(pit, pkt)])
                    if no_exp:
                        t2 = S.copy(eA, scA)
                        t2 = S.copy(eB, scB)
                    else:
                        t2 = S.act(eA, scA, AF.Exp,
                                   bias=cneg_b[:, 2 * hp:2 * hp + 1],
                                   scale=0.125)
                        assert t2 == exp_tok(it, kt, 0)
                        t2 = S.act(eB, scB, AF.Exp,
                                   bias=cneg_b[:, 2 * hp + 1:2 * hp + 2],
                                   scale=0.125)
                        assert t2 == exp_tok(it, kt, 1)
            blk.scalar(s_exp)

            def v_rd(v, it=it):
                V = VW(ec, v)
                V.wait("cPE", pv_tok[(it, NQT - 1)])
                if no_rd:
                    rd_tok[it] = V.memset(rds_bf[0:1, 0:2048], 0.001)
                    return
                V.recip(rds[0:1, 0:512], psC[0:1, 512:1024])
                V.recip(rds[0:1, 512:1024], psE[0:1, 512:1024])
                V.copy(rds_bf[0:1, 0:512], rds[0:1, 0:512])
                V.sub(rds_bf[0:1, 512:1024], rds[0:1, 0:512],
                      rds_bf[0:1, 0:512])
                V.copy(rds_bf[0:1, 1024:1536], rds[0:1, 512:1024])
                rd_tok[it] = V.sub(rds_bf[0:1, 1536:2048],
                                   rds[0:1, 512:1024],
                                   rds_bf[0:1, 1024:1536])
            blk.vector(v_rd)

            def t_rdb(t, it=it):
                T = TW(ec, t)
                T.wait("cDVE", rd_tok[it])
                if it > 0:
                    T.wait("cACT", rdbe_tok(it - 1))
                psRD = psE[:, 0:512]
                if no_rdb:
                    rdb_tok[it] = ec.inc("cPE", T.r.nop())
                    return
                T.mm(psRD, hob[0:1, 0:128], rds_bf[0:1, 0:512],
                     start=True, stop=False)
                T.mm(psRD, hob[0:1, 0:128], rds_bf[0:1, 512:1024],
                     start=False, stop=False)
                T.mm(psRD, hob[0:1, 128:256], rds_bf[0:1, 1024:1536],
                     start=False, stop=False)
                rdb_tok[it] = T.mm(psRD, hob[0:1, 128:256],
                                   rds_bf[0:1, 1536:2048],
                                   start=False, stop=True, mark=True)
            blk.tensor(t_rdb)

            def s_rdb(s, it=it):
                S = SW(ec, s)
                S.wait("cPE", rdb_tok[it])
                t2 = S.copy(rdb[:, :], psE[:, 0:512])
                assert t2 == rdbe_tok(it), (t2, rdbe_tok(it))
            blk.scalar(s_rdb)

            def v_evac(v, hp=hp, qc=qc, it=it):
                V = VW(ec, v)
                V.wait("cACT", rdbe_tok(it))
                evac_tok[it] = V.tt(
                    sc32[:, hp * N + qc * 512:hp * N + qc * 512 + 512],
                    psC[:, 0:512], rdb[:, :], ALU.mult)
            blk.vector(v_evac)

        def g_wo(g):
            G = GW(ec, g)
            for dk in range(NDK):
                G.dma("ldwo", war[:, dk * 1024:(dk + 1) * 1024],
                      wo_h[dk * 128:(dk + 1) * 128, :])
        blk.gpsimd(g_wo)

        wo_e = {}
        wo_base = ec.cnt["cACT"]
        pe_wo0 = ec.cnt["cPE"]

        def t_wo(t):
            T = TW(ec, t)
            T.wait("ldwo")
            if nits:
                T.wait("cDVE", evac_tok[nits - 1])
            for idx in range(32):
                oc, ncc = idx // NC4, idx % NC4
                pp = idx % 2
                bank = psA[:, pp * 512:pp * 512 + 512]
                if idx >= 2:
                    T.wait_future("cACT", wo_base + idx - 1)
                for dk in range(NDK):
                    T.mm(bank,
                         war[:, dk * 1024 + oc * 128:dk * 1024 + oc * 128 + 128],
                         sc32[:, dk * N + ncc * 512:dk * N + ncc * 512 + 512],
                         start=(dk == 0), stop=(dk == NDK - 1),
                         mark=(dk == NDK - 1))
        blk.tensor(t_wo)

        def s_wo(s):
            S = SW(ec, s)
            for idx in range(32):
                oc, ncc = idx // NC4, idx % NC4
                pp = idx % 2
                bank = psA[:, pp * 512:pp * 512 + 512]
                S.wait("cPE", pe_wo0 + idx + 1)
                if idx >= 1:
                    S.wait_future("st0", 16 * idx)
                t2 = S.act(out_t[:, 0:512],
                           bank, AF.Identity, bias=bo4[:, oc:oc + 1])
                wo_e[idx] = t2
        blk.scalar(s_wo)

        def sy_st(s):
            for idx in range(32):
                oc, ncc = idx // NC4, idx % NC4
                ec.wait(s, "cACT", wo_e[idx])
                ec.inc("st0", s.dma_start(
                    out=ao_h[oc * 128:(oc + 1) * 128, ncc * 512:ncc * 512 + 512],
                    in_=out_t[:, 0:512]), 16)
            ec.wait(s, "st0")
        blk.sync(sy_st)


def build_l0attn(**kw):
    nc = bass.Bass("TRN2", target_bir_lowering=False)
    io = {n: nc.dram_tensor(n, sh, dt, kind="ExternalInput").ap()
          for (n, sh, dt) in _L0A_IO}
    io["ao"] = nc.dram_tensor("ao", [D, N], F32, kind="ExternalOutput").ap()
    with nc.Block() as blk:
        ec = EC(nc, blk)
        emit_l0attn(nc, blk, ec, io, **kw)
        ec.close()
    return nc


def _run(nc, in_maps, n):
    return run_bass_kernel_spmd(nc, in_maps, core_ids=list(range(n)))


NJP = 22
JW = [128] * 21 + [42]


def inner_consts():
    hc = host_consts()
    seln = np.zeros((128, 2), np.float32)
    seln[0:64, 0] = 1.0
    seln[64:128, 1] = 1.0
    seld = np.zeros((128, 4), np.float32)
    seld[:, 0] = 1.0
    seld[:, 3] = 1.0
    sel2 = np.zeros((2, 128), np.float32)
    sel2[0, 0:64] = 1.0
    sel2[1, 64:128] = 1.0
    rot4 = np.concatenate([hc["rot_cs"], hc["rot_ss"], hc["rot_ci"],
                           hc["rot_si"]], axis=1).astype(np.float32)
    return {
        "selN": seln.astype(nbf), "selD": seld.astype(nbf),
        "sel2": sel2.astype(nbf), "rot4": rot4,
        "m01": hc["mask01"], "idnf": hc["idn_f32"],
        "onesN": np.ones((1, N), np.float32).astype(nbf),
    }


def inner_inmap(i, z_fm, inp):
    ic = inner_consts()
    b1 = np.asarray(inp["L0_ffn_b1"][i], np.float32)
    b1a = np.zeros((128, NJP), np.float32)
    b1g = np.zeros((128, NJP), np.float32)
    for jp in range(NJP):
        w = JW[jp]
        b1a[0:w, jp] = b1[jp * 128:jp * 128 + w]
        b1g[0:w, jp] = b1[DI + jp * 128:DI + jp * 128 + w]
    gcol = percol(inp["L0_ffn_ln_g"][i])
    bcol = percol(inp["L0_ffn_ln_b"][i])
    gb = np.zeros((128, 2 * NDK), np.float32)
    gb[:, 0::2] = gcol
    gb[:, 1::2] = bcol
    return {
        "z": np.ascontiguousarray(z_fm, np.float32),
        "wqkv": np.ascontiguousarray(inp["L0_ffn_Wqkv"][i], np.float32),
        "wo": np.ascontiguousarray(inp["L0_ffn_Wo"][i], np.float32),
        "qs": np.tile(np.asarray(inp["L0_ffn_q_scale"][i], np.float32),
                      2).reshape(128, 1),
        "ks": np.tile(np.asarray(inp["L0_ffn_k_scale"][i], np.float32),
                      2).reshape(128, 1),
        "g8": gcol, "gb": gb,
        "w1": np.ascontiguousarray(inp["L0_ffn_W1"][i], np.float32),
        "b1a": b1a, "b1g": b1g,
        "w2": np.ascontiguousarray(inp["L0_ffn_W2"][i], np.float32),
        "b28": percol(inp["L0_ffn_b2"][i]),
        "rot4": ic["rot4"], "m01": ic["m01"], "sel2": ic["sel2"],
        "selD": ic["selD"], "selN": ic["selN"], "idnf": ic["idnf"],
        "onesN": ic["onesN"],
    }


def _inner_io(nc, nlayers):
    specs = {"wqkv": ([D, 3 * D], F32), "wo": ([D, D], F32),
             "qs": ([128, 1], F32), "ks": ([128, 1], F32),
             "g8": ([128, NDK], F32), "gb": ([128, 2 * NDK], F32),
             "w1": ([D, W1C], F32), "b1a": ([128, NJP], F32),
             "b1g": ([128, NJP], F32), "w2": ([DI, D], F32),
             "b28": ([128, NDK], F32)}
    io = {}
    for n, (sh, dt) in specs.items():
        t = nc.dram_tensor("in_" + n, [nlayers * sh[0], sh[1]], dt,
                           kind="ExternalInput").ap()
        io[n] = [t[i * sh[0]:(i + 1) * sh[0], :] for i in range(nlayers)]
    for n, sh, dt in (("rot4", [128, 4 * N], F32), ("m01", [128, 384], BF16),
                      ("sel2", [2, 128], BF16), ("selD", [128, 4], BF16),
                      ("selN", [128, 2], BF16), ("idnf", [128, 128], F32),
                      ("onesN", [1, N], BF16)):
        io[n] = nc.dram_tensor("in_" + n, sh, dt, kind="ExternalInput").ap()
    return io


def inner_chain_inmap(z_fm, inp, nlayers=IDEPTH):
    ms = [inner_inmap(i, z_fm if i == 0 else np.zeros((D, N), np.float32),
                      inp) for i in range(nlayers)]
    out = {"in_" + n: np.concatenate([m[n] for m in ms], axis=0)
           for n in ("wqkv", "wo", "qs", "ks", "g8", "gb", "w1", "b1a",
                     "b1g", "w2", "b28")}
    for n in ("rot4", "m01", "sel2", "selD", "selN", "idnf", "onesN"):
        out["in_" + n] = ms[0][n]
    out["z"] = np.ascontiguousarray(z_fm, np.float32)
    return out


def emit_inner_chain(nc, blk, ec, io, nlayers=IDEPTH, dump=None,
                     prefix="i_"):
    dbg_h = io.get("dbg")
    z_h = io["z"]
    rot_h = io["rot4"]; m01_h = io["m01"]; s2_h = io["sel2"]
    sD_h = io["selD"]; sN_h = io["selN"]; idf_h = io["idnf"]
    on_h = io["onesN"]
    zo_h = io["zo"]

    st_ = ExitStack()
    with st_:
        _sb = [
            ("z_s", (128, NDK * 2049), F32),
            ("arA", (128, NDK * N), BF16),
            ("arK", (128, NDK * N), BF16),
            ("arV", (128, 16384), BF16),
            ("arW", (128, 8192), BF16),
            ("zbt", (128, NDK * 513), BF16),
            ("q2", (128, N), BF16),
            ("rows3", (2, 3 * N), BF16),
            ("mcols", (128, 32), F32),
            ("msb", (32, 128), BF16),
            ("stb", (2, 256), BF16),
            ("rdb", (128, 512), BF16),
            ("eT", (128, 768), BF16),
            ("m01_s", (128, 384), BF16),
            ("sel2_s", (2, 128), BF16),
            ("selD_s", (128, 4), BF16),
            ("selN_s", (128, 2), BF16),
            ("idnf_s", (128, 128), F32),
            ("qs_c", (128, 1), F32),
            ("ks_c", (128, 1), F32),
            ("g8_s", (128, NDK), F32),
            ("gb_s", (128, 2 * NDK), BF16),
            ("b1a_s", (128, NJP), F32),
            ("b1g_s", (128, NJP), F32),
            ("b28_s", (128, NDK), F32),
            ("ones_c", (128, 1), BF16),
            ("ones_r", (1, 128), BF16),
            ("zlead", (128, NDK * 4), F32),
        ]
        _ps = [("psX", (128, 1024)), ("psCtx", (128, 2048)),
               ("psD2", (128, 1024))]
        _t = _alloc(nc, st_, _sb, _ps, prefix=prefix)
        z_s = _t["z_s"]; arA = _t["arA"]; arK = _t["arK"]
        arV = _t["arV"]; arW = _t["arW"]; zbt = _t["zbt"]; q2 = _t["q2"]
        rows3 = _t["rows3"]; mcols = _t["mcols"]; msb = _t["msb"]
        stb = _t["stb"]; rdb = _t["rdb"]; eT = _t["eT"]
        m01 = _t["m01_s"]; sel2 = _t["sel2_s"]; selD = _t["selD_s"]
        selN = _t["selN_s"]; idnf = _t["idnf_s"]; qs_c = _t["qs_c"]
        ks_c = _t["ks_c"]; g8 = _t["g8_s"]; gb = _t["gb_s"]; b1a = _t["b1a_s"]
        b1g = _t["b1g_s"]; b28 = _t["b28_s"]; ones_c = _t["ones_c"]
        ones_r = _t["ones_r"]; zlead = _t["zlead"]
        psX = _t["psX"]; psCtx = _t["psCtx"]; psD2 = _t["psD2"]

        laug = rows3[0:2, 0:N]

        for sname in ("ldz", "ldc", "ldk", "ldw", "ldwA", "ldwB", "ldt",
                      "lds", "ldw1A", "ldw1B", "ldw2", "ldwoA", "ldwoB",
                      "st0"):
            ec.sem(sname)
        tok = {}
        for _li in range(nlayers):
            _emit_inner_layer(
                nc, blk, ec, io, _li, _li == 0, _li == nlayers - 1, dump,
                tok, _t, dbg_h, z_h, rot_h, m01_h, s2_h, sD_h, sN_h,
                idf_h, on_h, zo_h)


def _emit_inner_layer(nc, blk, ec, io, li, first, last, dump, tok, _t,
                      dbg_h, z_h, rot_h, m01_h, s2_h, sD_h, sN_h, idf_h,
                      on_h, zo_h):
    if True:
        wqkv_h = io["wqkv"][li]; wo_h = io["wo"][li]; qs_h = io["qs"][li]
        ks_h = io["ks"][li]; g8_h = io["g8"][li]; gb_h = io["gb"][li]
        w1_h = io["w1"][li]; b1a_h = io["b1a"][li]; b1g_h = io["b1g"][li]
        w2_h = io["w2"][li]; b28_h = io["b28"][li]
        z_s = _t["z_s"]; arA = _t["arA"]; arK = _t["arK"]
        arV = _t["arV"]; arW = _t["arW"]; zbt = _t["zbt"]; q2 = _t["q2"]
        rows3 = _t["rows3"]; mcols = _t["mcols"]; msb = _t["msb"]
        stb = _t["stb"]; rdb = _t["rdb"]; eT = _t["eT"]
        m01 = _t["m01_s"]; sel2 = _t["sel2_s"]; selD = _t["selD_s"]
        selN = _t["selN_s"]; idnf = _t["idnf_s"]; qs_c = _t["qs_c"]
        ks_c = _t["ks_c"]; g8 = _t["g8_s"]; gb = _t["gb_s"]
        b1a = _t["b1a_s"]; b1g = _t["b1g_s"]; b28 = _t["b28_s"]
        ones_c = _t["ones_c"]; ones_r = _t["ones_r"]; zlead = _t["zlead"]
        psX = _t["psX"]; psCtx = _t["psCtx"]; psD2 = _t["psD2"]
        laug = rows3[0:2, 0:N]
        pe_prev = ec.cnt["cPE"]
        dve_prev = ec.cnt["cDVE"]
        tok.clear()

        def emit_dump(buf, sem_name, val):
            def sy_d(s):
                ec.wait(s, sem_name, val)
                ec.inc("st0", s.dma_start(out=dbg_h[:, :], in_=buf[:, :]), 16)
                ec.wait(s, "st0")
            blk.sync(sy_d)

        def zf(dk, c0, c1):
            return z_s[:, dk * 2049 + 1 + c0:dk * 2049 + 1 + c1]

        def zbs(dk, shifted, c0=0, c1=512):
            off = 0 if (shifted and dk >= 4) else 1
            return zbt[:, dk * 513 + off + c0:dk * 513 + off + c1]

        def build_zbt(G, ncc, key, lead=False):
            for dk in range(NDK):
                if lead and ncc > 0:
                    G.copy(zbt[:, dk * 513:dk * 513 + 1],
                           zlead[:, dk * 4 + ncc:dk * 4 + ncc + 1])
                    t2 = G.copy(zbt[:, dk * 513 + 1:dk * 513 + 513],
                                z_s[:, dk * 2049 + ncc * 512 + 1:
                                    dk * 2049 + ncc * 512 + 513])
                else:
                    t2 = G.copy(zbt[:, dk * 513:dk * 513 + 513],
                                z_s[:, dk * 2049 + ncc * 512:
                                    dk * 2049 + ncc * 512 + 513])
            tok[key] = t2

        def sy_ld(s):
            if first:
                for dk in range(NDK):
                    ec.inc("ldz", s.dma_start(
                        out=zf(dk, 0, N),
                        in_=z_h[dk * 128:(dk + 1) * 128, :]), 16)
            else:
                ec.wait(s, "cPE", pe_prev)
                ec.wait(s, "cDVE", dve_prev)
            pairs = [(qs_c, qs_h), (ks_c, ks_h), (g8, g8_h),
                     (b1a, b1a_h), (b1g, b1g_h), (b28, b28_h)]
            if first:
                pairs.append((idnf, idf_h))
            for (dst, src) in pairs:
                ec.inc("ldc", s.dma_start(out=dst[:, :], in_=src), 16)
        blk.sync(sy_ld)

        def g_ld(g):
            G = GW(ec, g)
            if first:
                G.memset(ones_c[:, :], 1.0)
                G.memset(ones_r[:, :], 1.0)
                for dk in range(NDK):
                    G.memset(z_s[:, dk * 2049:dk * 2049 + 1], 0.0)
                G.dma("ldk", m01[:, :], m01_h[:, :])
                G.dma("ldk", sel2[:, :], s2_h[:, :])
                G.dma("ldk", selD[:, :], sD_h[:, :])
                G.dma("ldk", selN[:, :], sN_h[:, :])
            else:
                G.wait("cPE", pe_prev)
            G.dma("ldk", gb[:, :], gb_h)
            tok["zzero"] = ec.cnt["cPL"]
        blk.gpsimd(g_ld)

        qke0 = ec.cnt["cACT"]
        qkp0 = ec.cnt["cPE"]
        for ncc in range(NC4):
            def g_zb(g, ncc=ncc):
                G = GW(ec, g)
                if ncc == 0:
                    if first:
                        G.wait("ldz")
                        G.wait("cPL", tok["zzero"])
                    else:
                        G.wait("cPE", pe_prev)
                        G.wait("cDVE", dve_prev)
                else:
                    G.wait("cPE", qkp0 + ncc * 16)
                build_zbt(G, ncc, ("zbt1", ncc))
            blk.gpsimd(g_zb)

            for idx in range(16):
                ti, oc = idx // 8, idx % 8
                dst = arA if ti == 0 else arK
                gidx = ncc * 16 + idx
                buf = arW[:, (gidx % 2) * 1024:(gidx % 2) * 1024 + 1024]

                def g_w(g, ti=ti, oc=oc, gidx=gidx, buf=buf):
                    G = GW(ec, g)
                    if gidx >= 2:
                        G.wait_future("cPE", qkp0 + gidx - 1)
                    tok[("wld", gidx)] = G.dma(
                        "ldwA" if gidx % 2 == 0 else "ldwB", buf,
                        bass.AP(wqkv_h.tensor, wqkv_h.offset + ti * D + oc * 128,
                                [[3 * D, 128], [128 * 3 * D, NDK],
                                 [1, 128]]))
                blk.gpsimd(g_w)

                def t_p(t, ncc=ncc, gidx=gidx, buf=buf):
                    T = TW(ec, t)
                    T.wait("ldwA" if gidx % 2 == 0 else "ldwB",
                           tok[("wld", gidx)])
                    T.wait("cPL", tok[("zbt1", ncc)])
                    bank = psX[:, (gidx % 2) * 512:(gidx % 2) * 512 + 512]
                    if gidx >= 2:
                        T.wait_future("cACT", qke0 + gidx - 1)
                    for dk in range(NDK):
                        t2 = T.mm(bank, buf[:, dk * 128:dk * 128 + 128],
                                  zbs(dk, shifted=True),
                                  start=(dk == 0), stop=(dk == NDK - 1),
                                  mark=(dk == NDK - 1))
                    assert t2 == qkp0 + gidx + 1
                blk.tensor(t_p)

                def s_e(s, ncc=ncc, ti=ti, oc=oc, gidx=gidx, dst=dst):
                    S = SW(ec, s)
                    bank = psX[:, (gidx % 2) * 512:(gidx % 2) * 512 + 512]
                    S.wait("cPE", qkp0 + gidx + 1)
                    t2 = S.copy(
                        dst[:, oc * N + ncc * 512:oc * N + ncc * 512 + 512],
                        bank)
                    assert t2 == qke0 + gidx + 1
                blk.scalar(s_e)

        vml0 = ec.c("ldw")

        def g_vw(g):
            G = GW(ec, g)
            G.wait_future("cPE", qkp0 + 64)
            for vc in range(2):
                G.dma("ldw", arW[:, vc * 4096:vc * 4096 + 4096],
                      bass.AP(wqkv_h.tensor, wqkv_h.offset + 2 * D + vc * 512,
                              [[3 * D, 128], [128 * 3 * D, NDK], [1, 512]]))
        blk.gpsimd(g_vw)

        vp0 = ec.cnt["cPE"]
        ve0 = ec.cnt["cACT"]
        for ncc in range(NC4):
            def g_zb2(g, ncc=ncc):
                G = GW(ec, g)
                G.wait("cPE", qkp0 + 64 if ncc == 0 else vp0 + ncc * 8)
                build_zbt(G, ncc, ("zbt2", ncc))
            blk.gpsimd(g_zb2)

            for li in range(8):
                nt = ncc * 4 + li // 2
                vc = li % 2
                vidx = ncc * 8 + li

                def t_v(t, ncc=ncc, nt=nt, vc=vc, vidx=vidx):
                    T = TW(ec, t)
                    if vidx == 0:
                        T.wait("ldw", vml0 + 32)
                    T.wait("cPL", tok[("zbt2", ncc)])
                    bank = psX[:, (vidx % 2) * 512:(vidx % 2) * 512 + 512]
                    if vidx >= 2:
                        T.wait_future("cACT", ve0 + vidx - 1)
                    else:
                        T.wait("cACT", qke0 + 63 + vidx)
                    loc = (nt % 4) * 128
                    for dk in range(NDK):
                        t2 = T.mm(bank,
                                  zbs(dk, True, loc, loc + 128),
                                  arW[:, vc * 4096 + dk * 512:
                                      vc * 4096 + dk * 512 + 512],
                                  start=(dk == 0), stop=(dk == NDK - 1),
                                  mark=(dk == NDK - 1))
                    assert t2 == vp0 + vidx + 1
                blk.tensor(t_v)

                def s_v(s, nt=nt, vc=vc, vidx=vidx):
                    S = SW(ec, s)
                    bank = psX[:, (vidx % 2) * 512:(vidx % 2) * 512 + 512]
                    S.wait("cPE", vp0 + vidx + 1)
                    t2 = S.copy(arV[:, nt * 1024 + vc * 512:
                                    nt * 1024 + vc * 512 + 512], bank)
                    assert t2 == ve0 + vidx + 1
                blk.scalar(s_v)

        ldt0 = ec.c("ldt")

        def g_tab(g):
            G = GW(ec, g)
            G.wait_future("cPE", vp0 + 32)
            G.dma("ldt", arW[:, 0:N], rot_h[:, 0:N])
            G.dma("ldt", arW[:, N:2 * N], rot_h[:, N:2 * N])
        blk.gpsimd(g_tab)

        for ti in range(2):
            src = arA if ti == 0 else arK
            sc_c = qs_c if ti == 0 else ks_c
            if ti == 1:
                def g_tab2(g):
                    G = GW(ec, g)
                    G.wait("cDVE", tok[("rot", 0, NDK - 1)])
                    G.dma("ldt", arW[:, 0:N], rot_h[:, 2 * N:3 * N])
                    G.dma("ldt", arW[:, N:2 * N], rot_h[:, 3 * N:4 * N])
                blk.gpsimd(g_tab2)

            for dk in range(NDK):
                def g_s(g, ti=ti, dk=dk, src=src):
                    G = GW(ec, g)
                    if ti == 0 and dk == 0:
                        G.wait_future("cACT", qke0 + 64)
                    else:
                        G.wait("cDVE", tok[("rot", ti, dk - 1)] if dk else
                               tok[("rot", ti - 1, NDK - 1)])
                    x = src[:, dk * N:(dk + 1) * N]
                    tok[("sq", ti, dk)] = G.tt(q2[:, :], x, x, ALU.mult)
                blk.gpsimd(g_s)

                def t_n(t, ti=ti, dk=dk):
                    T = TW(ec, t)
                    T.wait("cPL", tok[("sq", ti, dk)])
                    if (ti, dk) == (0, 0):
                        T.wait("ldk")
                    else:
                        T.wait("cDVE", tok[("rr", ti, dk - 1)] if dk else
                               tok[("rr", ti - 1, NDK - 1)])
                    for ncc in range(NC4):
                        t2 = T.mm(psCtx[0:2, ncc * 512:ncc * 512 + 512],
                                  selN[:, :], q2[:, ncc * 512:ncc * 512 + 512],
                                  start=True, stop=True,
                                  mark=(ncc == NC4 - 1))
                    tok[("nmm", ti, dk)] = t2
                blk.tensor(t_n)

                def s_n(s, ti=ti, dk=dk):
                    S = SW(ec, s)
                    S.wait("cPE", tok[("nmm", ti, dk)])
                    tok[("nsq", ti, dk)] = S.act(psCtx[0:2, 0:2048],
                                                 psCtx[0:2, 0:2048], AF.Sqrt)
                blk.scalar(s_n)

                def v_n(v, ti=ti, dk=dk):
                    V = VW(ec, v)
                    V.wait("cACT", tok[("nsq", ti, dk)])
                    V.ts(psCtx[0:2, 0:2048], psCtx[0:2, 0:2048], 1e-12,
                         op0=ALU.max)
                    V.recip(psCtx[0:2, 0:2048], psCtx[0:2, 0:2048])
                    tok[("rr", ti, dk)] = V.copy(rows3[0:2, 0:N],
                                                 psCtx[0:2, 0:2048])
                blk.vector(v_n)

                ub = [None]

                def t_b(t, ti=ti, dk=dk, ub=ub):
                    T = TW(ec, t)
                    T.wait("cDVE", tok[("rr", ti, dk)])
                    ub[0] = ec.cnt["cDVE"]
                    for ncc in range(NC4):
                        if ncc >= 2:
                            T.wait_future("cDVE", ub[0] + ncc - 1)
                        tok[("rb", ti, dk, ncc)] = T.mm(
                            psD2[:, (ncc % 2) * 512:(ncc % 2) * 512 + 512],
                            sel2[:, :],
                            rows3[0:2, ncc * 512:ncc * 512 + 512],
                            start=True, stop=True, mark=True)
                blk.tensor(t_b)

                def v_u(v, ti=ti, dk=dk, src=src, sc_c=sc_c, ub=ub):
                    V = VW(ec, v)
                    if ti == 0 and dk == 0:
                        V.wait("ldc")
                    t2 = None
                    for ncc in range(NC4):
                        V.wait("cPE", tok[("rb", ti, dk, ncc)])
                        sl = src[:, dk * N + ncc * 512:dk * N + ncc * 512 + 512]
                        t2 = V.stt(sl, sl, sc_c[:, 0:1],
                                   psD2[:, (ncc % 2) * 512:(ncc % 2) * 512 + 512],
                                   ALU.mult, ALU.mult)
                        assert t2 == ub[0] + ncc + 1
                    tok[("u", ti, dk)] = t2
                blk.vector(v_u)

                def g_sg(g, ti=ti, dk=dk, src=src):
                    G = GW(ec, g)
                    G.wait("cDVE", tok[("u", ti, dk)])
                    ch0 = dk * N
                    for (a, b) in ((0, 32), (32, 0), (64, 96), (96, 64)):
                        G.dma("lds", q2[a:a + 32, :],
                              src[b:b + 32, ch0:ch0 + N])
                    tok[("sg", ti, dk)] = ec.c("lds")
                blk.gpsimd(g_sg)

                def v_r1(v, ti=ti, dk=dk, src=src):
                    V = VW(ec, v)
                    if dk == 0:
                        V.wait("ldt", ldt0 + 32 * (ti + 1))
                    V.wait("lds", tok[("sg", ti, dk)])
                    ch = src[:, dk * N:(dk + 1) * N]
                    tok[("t1", ti, dk)] = V.tt(ch, ch, arW[:, 0:N], ALU.mult)
                blk.vector(v_r1)

                def g_r2(g, ti=ti, dk=dk):
                    G = GW(ec, g)
                    if dk == 0:
                        G.wait("ldt", ldt0 + 32 * (ti + 1))
                    G.wait("lds", tok[("sg", ti, dk)])
                    tok[("t2", ti, dk)] = G.tt(q2[:, :], q2[:, :],
                                               arW[:, N:2 * N], ALU.mult)
                blk.gpsimd(g_r2)

                def v_r3(v, ti=ti, dk=dk, src=src):
                    V = VW(ec, v)
                    V.wait("cPL", tok[("t2", ti, dk)])
                    ch = src[:, dk * N:(dk + 1) * N]
                    tok[("rot", ti, dk)] = V.add(ch, ch, q2[:, :])
                blk.vector(v_r3)

        if dump in ("q3", "k3", "v2"):
            src = {"q3": arA, "k3": arK, "v2": arV}[dump]
            emit_dump(src, "cDVE", tok[("rot", 1, NDK - 1)])
            return

        def p1win(qt):
            return max(0, qt * 128 - 256), qt * 128 + 128

        def p2win(kt):
            return kt * 128, min(N, kt * 128 + 384)

        segs = {}
        seen_b = set()
        for kt in range(NQT):
            lo, hi = p2win(kt)
            brk = sorted(set([lo, hi, min(lo + 128, hi), min(lo + 256, hi)] +
                             [c for c in range(512, N, 512) if lo < c < hi]))
            ps = []
            for a, b in zip(brk[:-1], brk[1:]):
                qc = a // 512
                st = qc not in seen_b
                seen_b.add(qc)
                sp = (kt == NQT - 1) or (b <= lo + 128)
                ps.append((a, b, st, sp))
            segs[kt] = ps

        for hp in range(8):
            tok[("mx_b", hp)] = ec.cnt["cDVE"]

            def t_p1(t, hp=hp):
                T = TW(ec, t)
                if hp == 0:
                    T.wait("cDVE", tok[("rot", 1, NDK - 1)])
                    T.wait("cACT", ec.cnt["cACT"])
                else:
                    T.wait("cDVE", tok[("aoev", hp - 1, 3)])
                for qt in range(NQT):
                    lo, hi = p1win(qt)
                    w = hi - lo
                    ps = psX if qt % 2 == 0 else psD2
                    if qt >= 2:
                        T.wait_future("cDVE", tok[("mx_b", hp)] + qt - 1)
                    for j in range(2):
                        rb = j * 64
                        t2 = T.mm(ps[:, j * 512:j * 512 + w],
                                  arA[rb:rb + 64,
                                      hp * N + qt * 128:hp * N + qt * 128 + 128],
                                  arK[rb:rb + 64, hp * N + lo:hp * N + hi],
                                  start=True, stop=True, mark=(j == 1))
                    tok[("p1", hp, qt)] = t2
            blk.tensor(t_p1)

            def v_mx(v, hp=hp):
                V = VW(ec, v)
                for qt in range(NQT):
                    lo, hi = p1win(qt)
                    w = hi - lo
                    ps = psX if qt % 2 == 0 else psD2
                    V.wait("cPE", tok[("p1", hp, qt)])
                    t2 = V.red(ap3(mcols.ap(), qt, [[32, 128], [16, 2]]),
                               ap3(ps[:, :], 0,
                                   [[1024, 128], [512, 2], [1, w]]),
                               op=ALU.max, negate=True)
                    assert t2 == tok[("mx_b", hp)] + qt + 1
            blk.vector(v_mx)

            def t_mt(t, hp=hp):
                T = TW(ec, t)
                T.wait("cDVE", tok[("mx_b", hp)] + NQT)
                T.wait("ldc")
                tok[("mt", hp)] = T.trans(psD2[0:32, 0:128], mcols[:, :],
                                          idnf[:, :], mark=True)
            blk.tensor(t_mt)

            def v_mse(v, hp=hp):
                V = VW(ec, v)
                V.wait("cPE", tok[("mt", hp)])
                tok[("mse", hp)] = V.copy(msb[:, :], psD2[0:32, 0:128])
            blk.vector(v_mse)

            def g_mf(g, hp=hp):
                G = GW(ec, g)
                G.wait("cDVE", tok[("mse", hp)])
                for j in range(2):
                    G.dma("lds", rows3[0:1, (j + 1) * N:(j + 2) * N],
                          msb[j * 16:j * 16 + 16, :])
                tok[("mf", hp)] = ec.c("lds")
            blk.gpsimd(g_mf)

            tok[("msk_b", hp)] = ec.cnt["cPL"]
            tok[("exp_b", hp)] = ec.cnt["cACT"]

            if dump == "e4" and hp == 0:
                def sy_e4(s):
                    ec.wait_future(s, "cPL", tok[("msk_b", 0)] + 2 * 4 + 2)
                    ec.inc("st0", s.dma_start(out=dbg_h[:, 0:768],
                                              in_=eT[:, :]), 16)
                    ec.wait(s, "st0")
                blk.sync(sy_e4)

            def t_p2(t, hp=hp):
                T = TW(ec, t)

                def emit_pvd(kt):
                    lo, hi = p2win(kt)
                    T.wait_future("cPL", tok[("msk_b", hp)] + 2 * kt + 2)
                    np_ = len(segs[kt])
                    for pi, (c0, c1, st, sp) in enumerate(segs[kt]):
                        qc = c0 // 512
                        bk, rb = qc // 2, (qc % 2) * 32
                        for j in range(2):
                            h = 2 * hp + j
                            T.mm(psCtx[j * 64:j * 64 + 64, c0:c1],
                                 arV[:, kt * 1024 + h * 64:
                                     kt * 1024 + h * 64 + 64],
                                 eT[:, j * 384 + (c0 - lo):
                                    j * 384 + (c1 - lo)],
                                 start=st, stop=sp, skip=True)
                            is_pvd = (j == 1 and sp and kt == 4 * qc + 3)
                            is_lst = (j == 1 and pi == np_ - 1)
                            t2 = T.mm(
                                psD2[rb:rb + 2,
                                     bk * 512 + (c0 - qc * 512):
                                     bk * 512 + (c1 - qc * 512)],
                                selD[:, j * 2:j * 2 + 2],
                                eT[:, j * 384 + (c0 - lo):
                                   j * 384 + (c1 - lo)],
                                start=(st and j == 0), stop=(sp and j == 1),
                                skip=True, mark=(is_pvd or is_lst))
                            if is_pvd:
                                tok[("pvd", hp, qc)] = t2
                            if is_lst:
                                tok[("pvk", hp, kt)] = t2

                T.wait("lds", tok[("mf", hp)])
                for kt in range(NQT):
                    lo, hi = p2win(kt)
                    w = hi - lo
                    for j in range(2):
                        rb = j * 64
                        mrow = rows3[0:1, (j + 1) * N + lo:(j + 1) * N + hi]
                        T.mm(psX[:, j * 512:j * 512 + w],
                             arK[rb:rb + 64,
                                 hp * N + kt * 128:hp * N + kt * 128 + 128],
                             arA[rb:rb + 64, hp * N + lo:hp * N + hi],
                             start=True, stop=False)
                        tok[("p2", hp, kt, j)] = T.mm(
                            psX[:, j * 512:j * 512 + w], ones_r[:, :], mrow,
                            start=False, stop=True, mark=True)
                    emit_pvd(kt)
            blk.tensor(t_p2)

            def s_ex(s, hp=hp):
                S = SW(ec, s)
                for kt in range(NQT):
                    lo, hi = p2win(kt)
                    w = hi - lo
                    for j in range(2):
                        S.wait("cPE", tok[("p2", hp, kt, j)])
                        if kt >= 1 and j == 0:
                            S.wait("cPE", tok[("pvk", hp, kt - 1)])
                        if dump == "e4" and hp == 0 and kt == 5 and j == 0:
                            S.wait("st0", 16)
                        t2 = S.act(eT[:, j * 384:j * 384 + w],
                                   psX[:, j * 512:j * 512 + w], AF.Exp,
                                   scale=8.0)
                        assert t2 == tok[("exp_b", hp)] + 2 * kt + j + 1
            blk.scalar(s_ex)

            def g_mk(g, hp=hp):
                G = GW(ec, g)
                for kt in range(NQT):
                    lo, hi = p2win(kt)
                    w = hi - lo
                    for j in range(2):
                        G.wait("cACT", tok[("exp_b", hp)] + 2 * kt + j + 1)
                        t2 = G.tt(eT[:, j * 384:j * 384 + w],
                                  eT[:, j * 384:j * 384 + w],
                                  m01[:, 0:w], ALU.mult)
                        assert t2 == tok[("msk_b", hp)] + 2 * kt + j + 1
            blk.gpsimd(g_mk)

            def v_rd(v, hp=hp):
                V = VW(ec, v)
                for qc in range(4):
                    bk, rb = qc // 2, (qc % 2) * 32
                    V.wait("cPE", tok[("pvd", hp, qc)])
                    V.recip(psD2[rb:rb + 2, bk * 512:bk * 512 + 512],
                            psD2[rb:rb + 2, bk * 512:bk * 512 + 512])
                    tok[("rdr", hp, qc)] = V.copy(
                        rows3[0:2, qc * 512:qc * 512 + 512],
                        psD2[rb:rb + 2, bk * 512:bk * 512 + 512])
            blk.vector(v_rd)

            def t_rb(t, hp=hp):
                T = TW(ec, t)
                T.wait("cACT", tok[("exp_b", hp)] + 2 * NQT)
                for qc in range(4):
                    T.wait("cDVE", tok[("rdr", hp, qc)])
                    if qc > 0:
                        T.wait_future("cACT", tok[("rbe_b", hp)] + qc)
                    tok[("rbm", hp, qc)] = T.mm(
                        psX[:, 0:512], sel2[:, :],
                        rows3[0:2, qc * 512:qc * 512 + 512],
                        start=True, stop=True, mark=True)
            tok[("rbe_b", hp)] = ec.cnt["cACT"]
            blk.tensor(t_rb)
            tok[("aoev_b", hp)] = ec.cnt["cDVE"]

            def s_rb(s, hp=hp):
                S = SW(ec, s)
                for qc in range(4):
                    S.wait("cPE", tok[("rbm", hp, qc)])
                    t2 = S.copy(rdb[:, :], psX[:, 0:512])
                    assert t2 == tok[("rbe_b", hp)] + qc + 1
                    S.wait_future("cDVE", tok[("aoev_b", hp)] + qc + 1)
            blk.scalar(s_rb)

            def v_ao(v, hp=hp):
                V = VW(ec, v)
                for qc in range(4):
                    V.wait("cACT", tok[("rbe_b", hp)] + qc + 1)
                    t2 = V.tt(arA[:, hp * N + qc * 512:hp * N + qc * 512 + 512],
                              psCtx[:, qc * 512:qc * 512 + 512],
                              rdb[:, :], ALU.mult)
                    tok[("aoev", hp, qc)] = t2
                    assert t2 == tok[("aoev_b", hp)] + qc + 1
            blk.vector(v_ao)

        if dump == "ao":
            emit_dump(arA, "cDVE", tok[("aoev", 7, 3)])
            return
        if dump == "rd7":
            def sy_d7(s):
                ec.wait(s, "cDVE", tok[("aoev", 7, 3)])
                ec.inc("st0", s.dma_start(out=dbg_h[0:2, 0:N],
                                          in_=rows3[0:2, 0:N]), 16)
                ec.inc("st0", s.dma_start(out=dbg_h[0:1, N:3 * N],
                                          in_=rows3[0:1, N:3 * N]), 16)
                ec.wait(s, "st0")
            blk.sync(sy_d7)
            return

        wo_p0 = ec.cnt["cPE"]
        wo_z0 = ec.cnt["cDVE"]
        for oc in range(NDK):
            buf = arW[:, (oc % 2) * 1024:(oc % 2) * 1024 + 1024]

            def g_wo(g, oc=oc, buf=buf):
                G = GW(ec, g)
                if oc == 0:
                    G.wait("cDVE", tok[("rot", 1, NDK - 1)])
                if oc >= 2:
                    G.wait_future("cPE", wo_p0 + (oc - 1) * NC4)
                tok[("wol", oc)] = G.dma(
                    "ldwoA" if oc % 2 == 0 else "ldwoB", buf,
                    bass.AP(wo_h.tensor, wo_h.offset + oc * 128,
                            [[D, 128], [128 * D, NDK], [1, 128]]))
            blk.gpsimd(g_wo)

            def t_wo(t, oc=oc, buf=buf):
                T = TW(ec, t)
                T.wait("ldwoA" if oc % 2 == 0 else "ldwoB", tok[("wol", oc)])
                if oc == 0:
                    T.wait("cDVE", tok[("aoev", 7, 3)])
                for ncc in range(NC4):
                    gidx = oc * NC4 + ncc
                    bank = psX[:, (gidx % 2) * 512:(gidx % 2) * 512 + 512]
                    if gidx >= 2:
                        T.wait_future("cDVE", wo_z0 + gidx - 1)
                    for dk in range(NDK):
                        t2 = T.mm(bank, buf[:, dk * 128:dk * 128 + 128],
                                  arA[:, dk * N + ncc * 512:
                                      dk * N + ncc * 512 + 512],
                                  start=(dk == 0), stop=(dk == NDK - 1),
                                  mark=(dk == NDK - 1))
                    assert t2 == wo_p0 + gidx + 1
            blk.tensor(t_wo)

            def v_zu(v, oc=oc):
                V = VW(ec, v)
                for ncc in range(NC4):
                    gidx = oc * NC4 + ncc
                    bank = psX[:, (gidx % 2) * 512:(gidx % 2) * 512 + 512]
                    V.wait("cPE", wo_p0 + gidx + 1)
                    if gidx == 0:
                        V.wait("ldz")
                    sl = zf(oc, ncc * 512, ncc * 512 + 512)
                    t2 = V.stt(sl, bank, 1.0, sl, ALU.mult, ALU.add)
                    assert t2 == wo_z0 + gidx + 1
            blk.vector(v_zu)

        if dump == "z5":
            def sy_z5(s):
                ec.wait(s, "cDVE", wo_z0 + 32)
                for oc in range(NDK):
                    ec.inc("st0", s.dma_start(
                        out=zo_h[oc * 128:(oc + 1) * 128, :],
                        in_=zf(oc, 0, N)), 16)
                ec.wait(s, "st0")
            blk.sync(sy_z5)
            return

        mu_b = arA[:, 0:4096].bitcast(F32)
        std_b = arA[:, 4096:8192].bitcast(F32)
        rstd_b = arA[:, 8192:12288].bitcast(F32)
        tmp_b = arA[:, 12288:16384].bitcast(F32)

        def g_zcap(g):
            G = GW(ec, g)
            G.wait("cDVE", ec.cnt["cDVE"])
            for dk in range(NDK):
                G.copy(ap3(zlead.ap(), dk * 4 + 1, [[NDK * 4, 128], [1, 3]]),
                       ap3(z_s.ap(), dk * 2049 + 512,
                           [[NDK * 2049, 128], [512, 3]]))
        blk.gpsimd(g_zcap)

        ln_sq0 = ec.cnt["cPL"]
        ln_mm0 = [None]

        def ln_mm_tok(ncc, dk):
            if dk < 0:
                if ncc == 0:
                    return 0
                ncc, dk = ncc - 1, dk + NDK
            return ln_mm0[0] + ncc * NDK + dk + 1

        def g_lnsq(g):
            G = GW(ec, g)
            G.wait("cDVE", ec.cnt["cDVE"])
            G.wait("cPE", wo_p0 + 32)
            for ncc in range(NC4):
                if ncc >= 1:
                    G.wait_future("cPE", ln_mm_tok(ncc - 1, NDK - 1))
                build_zbt(G, ncc, ("zbt3", ncc))
                for dk in range(NDK):
                    G.wait_future("cPE", ln_mm_tok(ncc, dk - 2))
                    G.tt(q2[:, (dk % 2) * 512:(dk % 2) * 512 + 512],
                         zbs(dk, True), zbs(dk, True), ALU.mult)
        ln_mm0[0] = ec.cnt["cPE"]
        blk.gpsimd(g_lnsq)

        def t_lnst(t):
            T = TW(ec, t)
            for ncc in range(NC4):
                sl = slice(ncc * 512, ncc * 512 + 512)
                for dk in range(NDK):
                    T.wait("cPL", ln_sq0 + ncc * 16 + NDK + dk + 1)
                    T.mm(psCtx[0:1, sl], ones_c[:, :], zbs(dk, True),
                         start=(dk == 0), stop=(dk == NDK - 1), skip=True)
                    t2 = T.mm(psCtx[32:33, sl], ones_c[:, :],
                              q2[:, (dk % 2) * 512:(dk % 2) * 512 + 512],
                              start=(dk == 0), stop=(dk == NDK - 1),
                              skip=True, mark=True)
                    assert t2 == ln_mm_tok(ncc, dk)
        blk.tensor(t_lnst)

        sums_bf = q2[0:1, 0:2048]
        hl_tok = {}
        bc0 = ec.cnt["cPE"]
        be0 = ec.cnt["cACT"]
        lo0 = ec.cnt["cDVE"]

        def s_hilo(s):
            S = SW(ec, s)
            for ncc in range(NC4):
                sl = slice(ncc * 512, ncc * 512 + 512)
                if ncc == 0:
                    S.wait("cPE", ln_mm_tok(NC4 - 1, NDK - 1))
                S.copy(sums_bf[0:1, 0:512], psCtx[0:1, sl])
                hl_tok[ncc] = S.copy(sums_bf[0:1, 1024:1536],
                                     psCtx[32:33, sl])
                S.wait_future("cPE", bc0 + ncc + 1)
                S.copy(mu_b[:, sl], psX[:, 0:512])
                t2 = S.copy(std_b[:, sl], psX[:, 512:1024])
                assert t2 == be0 + ncc * 4 + 4
        blk.scalar(s_hilo)

        def v_lo(v):
            V = VW(ec, v)
            for ncc in range(NC4):
                sl = slice(ncc * 512, ncc * 512 + 512)
                V.wait("cACT", hl_tok[ncc])
                V.sub(sums_bf[0:1, 512:1024], psCtx[0:1, sl],
                      sums_bf[0:1, 0:512])
                t2 = V.sub(sums_bf[0:1, 1536:2048], psCtx[32:33, sl],
                           sums_bf[0:1, 1024:1536])
                assert t2 == lo0 + 2 * ncc + 2
        blk.vector(v_lo)

        def t_bc(t):
            T = TW(ec, t)
            for ncc in range(NC4):
                T.wait_future("cDVE", lo0 + 2 * ncc + 2)
                if ncc >= 1:
                    T.wait_future("cACT", be0 + ncc * 4)
                T.mm(psX[:, 0:512], ones_r[:, :], sums_bf[0:1, 0:512],
                     start=True, stop=False)
                T.mm(psX[:, 0:512], ones_r[:, :], sums_bf[0:1, 512:1024],
                     start=False, stop=True)
                T.mm(psX[:, 512:1024], ones_r[:, :], sums_bf[0:1, 1024:1536],
                     start=True, stop=False)
                t2 = T.mm(psX[:, 512:1024], ones_r[:, :],
                          sums_bf[0:1, 1536:2048],
                          start=False, stop=True, mark=True)
                assert t2 == bc0 + ncc + 1
        blk.tensor(t_bc)

        lnf = {}

        def v_fin(v):
            V = VW(ec, v)
            V.wait("cACT", be0 + 16)
            V.ts(mu_b[:, :], mu_b[:, :], 1.0 / D)
            V.ts(std_b[:, :], std_b[:, :], 1.0 / D)
            V.tt(tmp_b[:, :], mu_b[:, :], mu_b[:, :], ALU.mult)
            V.sub(std_b[:, :], std_b[:, :], tmp_b[:, :])
            lnf["var"] = V.ts(std_b[:, :], std_b[:, :], 1e-5, op0=ALU.add)
        blk.vector(v_fin)

        def s_sq2(s):
            S = SW(ec, s)
            S.wait("cDVE", lnf["var"])
            lnf["std"] = S.act(std_b[:, :], std_b[:, :], AF.Sqrt)
        blk.scalar(s_sq2)

        def v_fin2(v):
            V = VW(ec, v)
            V.wait("cACT", lnf["std"])
            V.recip(rstd_b[:, :], std_b[:, :])
            V.stt(tmp_b[:, :], mu_b[:, :], -1.0, rstd_b[:, :],
                  ALU.mult, ALU.mult)
            lnf["laug"] = V.copy(laug[0:1, :], tmp_b[0:1, :])
        blk.vector(v_fin2)

        def g_l1(g):
            G = GW(ec, g)
            G.wait("cDVE", lnf["laug"])
            G.dma("ldk", laug[1:2, :], on_h[:, :])
        blk.gpsimd(g_l1)
        ldk_l1 = ec.c("ldk")

        zs3 = arK[:, 0:4096]
        gg = arV[:, 11776:12288]

        def w1b(pp, half):
            return arW[:, pp * 2048 + half * 1024:
                       pp * 2048 + half * 1024 + 1024]

        w2b = arW[:, 4096:4096 + 2816]

        for qtr in range(4):
            def g_zb3(g, qtr=qtr):
                G = GW(ec, g)
                if qtr == 0:
                    G.wait("ldk", ldk_l1)
                else:
                    G.wait("cPE", tok[("w1done", qtr - 1)])
                build_zbt(G, qtr, ("zbt4", qtr), lead=True)
            blk.gpsimd(g_zb3)

            def v_z3(v, qtr=qtr):
                V = VW(ec, v)
                V.wait("cPL", tok[("zbt4", qtr)])
                if qtr == 0:
                    V.wait("cDVE", lnf["laug"])
                t2 = None
                for dk in range(NDK):
                    sl = zs3[:, dk * 512:dk * 512 + 512]
                    V.ts(sl, zbs(dk, True), g8[:, dk:dk + 1], op0=ALU.mult)
                    t2 = V.tt(sl, sl, rstd_b[:, qtr * 512:qtr * 512 + 512],
                              ALU.mult)
                tok[("zs3", qtr)] = t2
            blk.vector(v_z3)

            for jp in range(NJP):
                w = JW[jp]
                pp = jp % 2
                idx = qtr * NJP + jp

                def g_w1(g, qtr=qtr, jp=jp, w=w, pp=pp, idx=idx):
                    G = GW(ec, g)
                    if idx == 0:
                        G.wait("cPE", wo_p0 + 32)
                    if qtr > 0 and jp == 0:
                        G.wait("cPE", tok[("w2c", qtr * NDK - 1)])
                    if idx >= 2:
                        G.wait("cPE", tok[("w1c", idx - 2)])
                    sname = "ldw1A" if pp == 0 else "ldw1B"
                    for half in range(2):
                        dst = w1b(pp, half)
                        tok[("w1l", idx)] = G.dma(
                            sname,
                            dst[:, 0:NDK * w] if w < 128 else dst,
                            bass.AP(w1_h.tensor, w1_h.offset + half * DI + jp * 128,
                                    [[W1C, 128], [128 * W1C, NDK], [1, w]]))
                blk.gpsimd(g_w1)

                def wsl(pp, half, dk, w):
                    b = w1b(pp, half)
                    return (b[:, dk * w:dk * w + w] if w < 128 else
                            b[:, dk * 128:dk * 128 + w])

                def t_w1(t, qtr=qtr, jp=jp, w=w, pp=pp, idx=idx):
                    T = TW(ec, t)
                    T.wait("ldw1A" if pp == 0 else "ldw1B", tok[("w1l", idx)])
                    T.wait("cDVE", tok[("zs3", qtr)])
                    if idx >= 1:
                        T.wait("cACT", tok[("stb", idx - 1)])
                    for dk in range(NDK):
                        T.mm(psD2[0:2, 0:w], gb[:, 2 * dk:2 * dk + 2],
                             wsl(pp, 0, dk, w),
                             start=(dk == 0), stop=(dk == NDK - 1))
                        t2 = T.mm(psD2[0:2, 512:512 + w],
                                  gb[:, 2 * dk:2 * dk + 2],
                                  wsl(pp, 1, dk, w),
                                  start=(dk == 0), stop=(dk == NDK - 1),
                                  mark=(dk == NDK - 1))
                    tok[("stm", idx)] = t2
                blk.tensor(t_w1)

                def s_st(s, idx=idx, w=w):
                    S = SW(ec, s)
                    S.wait("cPE", tok[("stm", idx)])
                    if idx >= 1:
                        S.wait("cPE", tok[("hgm", idx - 1)])
                    S.copy(stb[0:2, 0:w], psD2[0:2, 0:w])
                    tok[("stb", idx)] = S.copy(stb[0:2, 128:128 + w],
                                               psD2[0:2, 512:512 + w])
                blk.scalar(s_st)

                def t_hg(t, qtr=qtr, jp=jp, w=w, pp=pp, idx=idx):
                    T = TW(ec, t)
                    T.wait("cACT", tok[("stb", idx)])
                    if idx >= 2:
                        T.wait("cDVE", tok[("hst", idx - 2)])
                    pa = psCtx[:, (pp * 2) * 512:(pp * 2) * 512 + 512]
                    pg = psCtx[:, (pp * 2 + 1) * 512:(pp * 2 + 1) * 512 + 512]
                    for dk in range(NDK):
                        T.mm(pa[0:w, :], wsl(pp, 0, dk, w),
                             zs3[:, dk * 512:dk * 512 + 512],
                             start=(dk == 0), stop=False)
                        T.mm(pg[0:w, :], wsl(pp, 1, dk, w),
                             zs3[:, dk * 512:dk * 512 + 512],
                             start=(dk == 0), stop=False)
                    T.mm(pa[0:w, :], stb[0:2, 0:w],
                         laug[0:2, qtr * 512:qtr * 512 + 512],
                         start=False, stop=True)
                    t2 = T.mm(pg[0:w, :], stb[0:2, 128:128 + w],
                              laug[0:2, qtr * 512:qtr * 512 + 512],
                              start=False, stop=True, mark=True)
                    tok[("hgm", idx)] = t2
                    tok[("w1c", idx)] = t2
                blk.tensor(t_hg)

                def s_gg(s, jp=jp, w=w, pp=pp, idx=idx):
                    S = SW(ec, s)
                    S.wait("cPE", tok[("hgm", idx)])
                    if idx >= 1:
                        S.wait_future("cDVE", tok.get(("hst", idx - 1), 0))
                    pg = psCtx[:, (pp * 2 + 1) * 512:(pp * 2 + 1) * 512 + 512]
                    tok[("gg", idx)] = S.act(gg[0:w, :], pg[0:w, :], AF.Gelu,
                                             bias=b1g[0:w, jp:jp + 1])
                blk.scalar(s_gg)

                def v_h(v, qtr=qtr, jp=jp, w=w, pp=pp, idx=idx):
                    V = VW(ec, v)
                    V.wait("cACT", tok[("gg", idx)])
                    if qtr > 0 and jp == 0:
                        V.wait("cPE", tok[("w2c", qtr * NDK - 1)])
                    pa = psCtx[:, (pp * 2) * 512:(pp * 2) * 512 + 512]
                    tok[("hst", idx)] = V.stt(
                        arV[0:w, jp * 512:jp * 512 + 512],
                        pa[0:w, :], b1a[0:w, jp:jp + 1], gg[0:w, :],
                        ALU.add, ALU.mult)
                blk.vector(v_h)
            tok[("w1done", qtr)] = ec.cnt["cPE"]

            for oc in range(NDK):
                widx = qtr * NDK + oc

                def g_w2(g, qtr=qtr, oc=oc, widx=widx):
                    G = GW(ec, g)
                    if widx >= 1:
                        G.wait("cPE", tok[("w2c", widx - 1)])
                    else:
                        G.wait("cPE", tok[("w1c", NJP - 1)])
                    G.dma("ldw2", w2b[:, 0:21 * 128],
                          bass.AP(w2_h.tensor, w2_h.offset + oc * 128,
                                  [[D, 128], [128 * D, 21], [1, 128]]))
                    tok[("w2l", widx)] = G.dma(
                        "ldw2", w2b[0:42, 21 * 128:21 * 128 + 128],
                        bass.AP(w2_h.tensor,
                                w2_h.offset + 2688 * D + oc * 128,
                                [[D, 42], [1, 128]]))
                blk.gpsimd(g_w2)

                def t_w2(t, qtr=qtr, oc=oc, widx=widx):
                    T = TW(ec, t)
                    T.wait("ldw2", tok[("w2l", widx)])
                    T.wait("cDVE", tok[("hst", qtr * NJP + NJP - 1)])
                    if widx >= 2:
                        T.wait("cDVE", tok[("zu2", widx - 2)])
                    bank = psX[:, (widx % 2) * 512:(widx % 2) * 512 + 512]
                    for ik in range(NJP):
                        kw = JW[ik]
                        t2 = T.mm(bank,
                                  w2b[0:kw, ik * 128:ik * 128 + 128],
                                  arV[0:kw, ik * 512:ik * 512 + 512],
                                  start=(ik == 0), stop=(ik == NJP - 1),
                                  mark=(ik == NJP - 1))
                    tok[("w2c", widx)] = t2
                blk.tensor(t_w2)

                def v_z2(v, qtr=qtr, oc=oc, widx=widx):
                    V = VW(ec, v)
                    V.wait("cPE", tok[("w2c", widx)])
                    bank = psX[:, (widx % 2) * 512:(widx % 2) * 512 + 512]
                    sl = zf(oc, qtr * 512, qtr * 512 + 512)
                    tok[("zu2", widx)] = V.stt(sl, bank, b28[:, oc:oc + 1],
                                               sl, ALU.add, ALU.add)
                blk.vector(v_z2)

        if last:
            def sy_st(s):
                for oc in range(NDK):
                    for qtr in range(4):
                        ec.wait(s, "cDVE", tok[("zu2", qtr * NDK + oc)])
                    ec.inc("st0", s.dma_start(
                        out=zo_h[oc * 128:(oc + 1) * 128, :],
                        in_=zf(oc, 0, N)), 16)
                ec.wait(s, "st0")
            blk.sync(sy_st)


def build_inner(dump=None, nlayers=1):
    nc = bass.Bass("TRN2", target_bir_lowering=False)
    io = _inner_io(nc, nlayers)
    io["z"] = nc.dram_tensor("z", [D, N], F32, kind="ExternalInput").ap()
    io["zo"] = nc.dram_tensor("zo", [D, N], F32, kind="ExternalOutput").ap()
    if dump:
        io["dbg"] = nc.dram_tensor("dbg", [128, 16384], BF16,
                                   kind="ExternalOutput").ap()
    with nc.Block() as blk:
        ec = EC(nc, blk)
        emit_inner_chain(nc, blk, ec, io, nlayers=nlayers, dump=dump)
        ec.close()
    return nc


def l1_consts():
    poolind = np.zeros((128, 64), np.float32)
    for p in range(128):
        poolind[p, p % 64] = 1.0
    return {"poolind": poolind.astype(nbf)}


def l1_inmap(x_fm, ao_fm, z6_fm, inp):
    ic = l1_consts()
    qw = np.asarray(inp["L1_attn_q_w"], np.float32)
    kw = np.asarray(inp["L1_attn_k_w"], np.float32)
    vw = np.asarray(inp["L1_attn_v_w"], np.float32)
    w3 = np.stack([np.transpose(qw, (0, 2, 1)),
                   np.transpose(kw, (0, 2, 1)),
                   np.transpose(vw, (0, 2, 1))], axis=1)
    qb = np.asarray(inp["L1_attn_q_b"], np.float32).reshape(16, 64)
    kb = np.asarray(inp["L1_attn_k_b"], np.float32).reshape(16, 64)
    qkb2 = np.zeros((64, 32), np.float32)
    qkb2[:, 0::2] = qb.T
    qkb2[:, 1::2] = kb.T
    vb = np.asarray(inp["L1_attn_v_b"], np.float32).reshape(1, 16 * 64)
    return {
        "x": np.ascontiguousarray(x_fm, np.float32),
        "ao": np.ascontiguousarray(ao_fm, np.float32),
        "z6": np.ascontiguousarray(z6_fm, np.float32),
        "w3": np.ascontiguousarray(w3.reshape(16, 3 * 1024 * 64)),
        "qkb2": qkb2, "vb": vb,
        "lwT": np.ascontiguousarray(
            np.asarray(inp["L1_attn_l_w"], np.float32).T),
        "lb": percol(np.asarray(inp["L1_attn_l_b"],
                                np.float32).reshape(-1)),
        "wT": np.ascontiguousarray(
            np.asarray(inp["L1_ffn_weight"], np.float32).T),
        "disha": np.ascontiguousarray(inp["L1_ffn_disha"], np.float32),
        "poolind": ic["poolind"],
    }


_L1_IO = (("w3", [16, 3 * D * 64], F32), ("qkb2", [64, 32], F32),
          ("vb", [1, 1024], F32), ("lwT", [D, D], F32),
          ("lb", [128, NDK], F32), ("wT", [D, D], F32),
          ("disha", [64, D], F32), ("poolind", [128, 64], BF16))


def emit_l1(nc, blk, ec, io, prefix="f_"):
    x_h = io["x"]; ao_h = io["ao"]; z6_h = io["z6"]
    w3_h = io["w3"]; qkb_h = io["qkb2"]; vb_h = io["vb"]
    lwT_h = io["lwT"]; lb_h = io["lb"]; wT_h = io["wT"]
    dsh_h = io["disha"]; pi_h = io["poolind"]; out_h = io["out"]

    st_ = ExitStack()
    with st_:
        _sb = [
            ("x1", (128, NDK * N), F32),
            ("xn2", (128, NDK * N), BF16),
            ("ao2", (128, NDK * N), BF16),
            ("qA", (64, N), BF16), ("kA", (64, N), BF16),
            ("qB", (64, N), BF16), ("kB", (64, N), BF16),
            ("vA", (128, 1024), BF16), ("vB", (128, 1024), BF16),
            ("eT", (128, N), BF16),
            ("vt", (128, 64), BF16),
            ("wqkv", (128, 2 * 1536), BF16),
            ("sqs", (128, 2 * N), BF16),
            ("sums", (1, 2048), BF16),
            ("qkb2_s", (64, 32), F32),
            ("vbr", (1, 1024), BF16),
            ("mxc", (128, 1), F32),
            ("dsc", (128, 1), F32),
            ("warA", (128, 2048), BF16),
            ("warB", (128, 2048), BF16),
            ("dishaB", (64, 1024), BF16),
            ("poolind_s", (128, 64), BF16),
            ("pooled", (64, N), BF16),
            ("lbc", (128, NDK), F32),
            ("outs", (128, 1024), F32),
            ("ones_c", (128, 1), BF16),
            ("ones_r", (1, 128), BF16),
        ]
        _ps = [("psS", (128, 2048)), ("psC", (128, 2048))]
        _t = _alloc(nc, st_, _sb, _ps, prefix=prefix)
        x1 = _t["x1"]; xn2 = _t["xn2"]; ao2 = _t["ao2"]
        qA = _t["qA"]; kA = _t["kA"]; qB = _t["qB"]; kB = _t["kB"]
        vA = _t["vA"]; vB = _t["vB"]; eT = _t["eT"]; vt = _t["vt"]
        wqkv = _t["wqkv"]; sqs = _t["sqs"]; sums = _t["sums"]
        qkb2 = _t["qkb2_s"]; vbr = _t["vbr"]; mxc = _t["mxc"]
        dsc = _t["dsc"]; warA = _t["warA"]; warB = _t["warB"]
        dishaB = _t["dishaB"]; poolind = _t["poolind_s"]
        pooled = _t["pooled"]; lbc = _t["lbc"]; outs = _t["outs"]
        ones_c = _t["ones_c"]; ones_r = _t["ones_r"]
        psS = _t["psS"]; psC = _t["psC"]

        mu_b = ao2[:, 0:4096].bitcast(F32)
        std_b = ao2[:, 4096:8192].bitcast(F32)
        rstd_b = ao2[:, 8192:12288].bitcast(F32)
        tmp_b = ao2[:, 12288:16384].bitcast(F32)
        tA = xn2[:, 0:8192].bitcast(F32)
        tB = xn2[:, 8192:16384].bitcast(F32)

        for sname in ("ldx", "ldaA", "ldaB", "ldc", "ldk", "ldwA", "ldwB",
                      "ldfA", "ldfB", "st0", "st1"):
            ec.sem(sname)
        tok = {}
        cmb0 = [None]

        def tok_cmb(dk):
            return cmb0[0] + 2 * dk + 2

        cmb0[0] = ec.cnt["cDVE"]

        def sy_ld(s):
            for dk in range(NDK):
                ec.inc("ldx", s.dma_start(
                    out=x1[:, dk * N:(dk + 1) * N],
                    in_=x_h[dk * 128:(dk + 1) * 128, :]), 16)
            for dk in range(NDK):
                sl = tA if dk % 2 == 0 else tB
                sname = "ldaA" if dk % 2 == 0 else "ldaB"
                if dk >= 2:
                    ec.wait_future(s, "cDVE", tok_cmb(dk - 2))
                ec.inc(sname, s.dma_start(
                    out=sl[:, 0:N], in_=ao_h[dk * 128:(dk + 1) * 128, :]), 16)
                ec.inc(sname, s.dma_start(
                    out=sl[:, N:2 * N],
                    in_=z6_h[dk * 128:(dk + 1) * 128, :]), 16)
            ec.inc("ldc", s.dma_start(out=qkb2[:, :], in_=qkb_h[:, :]), 16)
            ec.inc("ldc", s.dma_start(out=lbc[:, :], in_=lb_h[:, :]), 16)
        blk.sync(sy_ld)

        def v_cmb(v):
            V = VW(ec, v)
            assert cmb0[0] == ec.cnt["cDVE"]
            V.wait("ldx")
            for dk in range(NDK):
                sl = tA if dk % 2 == 0 else tB
                V.wait("ldaA" if dk % 2 == 0 else "ldaB",
                       32 * (dk // 2 + 1))
                xs = x1[:, dk * N:(dk + 1) * N]
                V.add(xs, xs, sl[:, 0:N])
                t2 = V.add(xs, xs, sl[:, N:2 * N])
                assert t2 == tok_cmb(dk)
        blk.vector(v_cmb)

        def g_ld(g):
            G = GW(ec, g)
            G.memset(ones_c[:, :], 1.0)
            G.memset(ones_r[:, :], 1.0)
            G.dma("ldk", vbr[:, :], vb_h[:, :])
            G.dma("ldk", poolind[:, :], pi_h[:, :])
            G.dma("ldk", dishaB[:, :], dsh_h[:, :])
            G.wait("cDVE", tok_cmb(NDK - 1))
            for dk in range(NDK):
                tok[("xb", dk)] = G.copy(xn2[:, dk * N:(dk + 1) * N],
                                         x1[:, dk * N:(dk + 1) * N])
        blk.gpsimd(g_ld)

        ln = emit_ln(
            ec,
            xbf_sl=lambda dk: xn2[:, dk * N:(dk + 1) * N],
            sq_scr=sqs, ones_c=ones_c, ones_r=ones_r,
            ps_stats=psS, ps_bc=psC[:, 0:1024], sums_bf=sums,
            mu_b=mu_b, rstd_b=rstd_b, tmp_b=tmp_b, std_b=std_b,
            want_aug=True, n_tok=N,
            dep=[("cPL", tok[("xb", NDK - 1)])])

        xnt = {}

        def v_xn(v):
            V = VW(ec, v)
            V.wait("cDVE", ln["dn"])
            for dk in range(NDK):
                sl = xn2[:, dk * N:(dk + 1) * N]
                V.tt(sl, x1[:, dk * N:(dk + 1) * N], rstd_b[:, :], ALU.mult)
                xnt[dk] = V.add(sl, sl, tmp_b[:, :])
            xnt["all"] = xnt[NDK - 1]
        blk.vector(v_xn)

        def wqkv_half(h):
            return wqkv[:, (h % 2) * 1536:(h % 2) * 1536 + 1536]

        for h in range(2):
            def g_w0(g, h=h):
                G = GW(ec, g)
                tok[("w3l", h)] = G.dma(
                    "ldwA" if h % 2 == 0 else "ldwB", wqkv_half(h),
                    bass.AP(w3_h.tensor, h * 3 * D * 64,
                            [[64, 128], [D * 64, 3], [128 * 64, NDK],
                             [1, 64]]))
            blk.gpsimd(g_w0)

        for hp in range(8):
            for j in range(2):
                h = 2 * hp + j
                qh = qA if j == 0 else qB
                kh = kA if j == 0 else kB
                vh = vA if j == 0 else vB
                buf = wqkv_half(h)

                if h >= 2:
                    def g_w2(g, h=h):
                        G = GW(ec, g)
                        G.wait_future("cPE", tok[("prj", h - 2)])
                        tok[("w3l", h)] = G.dma(
                            "ldwA" if h % 2 == 0 else "ldwB", wqkv_half(h),
                            bass.AP(w3_h.tensor, h * 3 * D * 64,
                                    [[64, 128], [D * 64, 3], [128 * 64, NDK],
                                     [1, 64]]))
                    blk.gpsimd(g_w2)

                def t_prj(t, h=h, buf=buf):
                    T = TW(ec, t)
                    T.wait("ldwA" if h % 2 == 0 else "ldwB", tok[("w3l", h)])
                    if h == 0:
                        T.wait("cDVE", xnt["all"])
                        T.wait("cPL")
                    else:
                        T.wait_future("cACT", tok[("exp", h - 1, NQT - 1)])
                    for ncc in range(NC4):
                        sl = slice(ncc * 512, ncc * 512 + 512)
                        for ti in range(2):
                            for dk in range(NDK):
                                t2 = T.mm(
                                    psS[ti * 64:ti * 64 + 64, sl],
                                    buf[:, ti * 512 + dk * 64:
                                        ti * 512 + dk * 64 + 64],
                                    xn2[:, dk * N + ncc * 512:
                                        dk * N + ncc * 512 + 512],
                                    start=(dk == 0), stop=(dk == NDK - 1),
                                    skip=True,
                                    mark=(dk == NDK - 1))
                            tok[("qk", h, ncc, ti)] = t2
                    tok[("qkall", h)] = t2
                blk.tensor(t_prj)

                def s_qke(s, h=h, qh=qh, kh=kh):
                    S = SW(ec, s)
                    if h == 0:
                        S.wait("ldc")
                    for ncc in range(NC4):
                        sl = slice(ncc * 512, ncc * 512 + 512)
                        S.wait("cPE", tok[("qk", h, ncc, 1)])
                        S.act(qh[:, sl], psS[0:64, sl], AF.Identity,
                              bias=qkb2[:, 2 * h:2 * h + 1])
                        t2 = S.act(kh[:, sl], psS[64:128, sl], AF.Identity,
                                   bias=qkb2[:, 2 * h + 1:2 * h + 2])
                    tok[("qke", h)] = t2
                blk.scalar(s_qke)

                def t_vp(t, h=h, buf=buf):
                    T = TW(ec, t)
                    T.wait("cACT", tok[("qke", h)])
                    if h == 0:
                        T.wait("ldk")
                    for kt in range(NQT):
                        bank = kt // 8
                        c0 = bank * 512 + (kt % 8) * 64
                        for dk in range(NDK):
                            T.mm(psS[:, c0:c0 + 64],
                                 xn2[:, dk * N + kt * 128:
                                     dk * N + kt * 128 + 128],
                                 buf[:, 1024 + dk * 64:1024 + dk * 64 + 64],
                                 start=(dk == 0 and kt % 8 == 0),
                                 stop=False, skip=True)
                        t2 = T.mm(psS[:, c0:c0 + 64], ones_r[:, :],
                                  vbr[0:1, h * 64:h * 64 + 64],
                                  start=False, stop=(kt % 8 == 7),
                                  skip=True, mark=(kt % 8 == 7))
                    tok[("prj", h)] = t2
                blk.tensor(t_vp)

                def s_ve(s, h=h, vh=vh):
                    S = SW(ec, s)
                    S.wait("cPE", tok[("prj", h)])
                    S.copy(vh[:, 0:512], psS[:, 0:512])
                    tok[("pev", h)] = S.copy(vh[:, 512:1024],
                                             psS[:, 512:1024])
                blk.scalar(s_ve)

                def t_sc(t, hp=hp, j=j, h=h, qh=qh, kh=kh):
                    T = TW(ec, t)
                    for kt in range(NQT):
                        if kt == 0:
                            T.wait("cACT", tok[("pev", h)])
                        else:
                            T.wait_future("cACT", tok[("exp", h, kt - 1)])
                        for b in range(4):
                            t2 = T.mm(psS[:, b * 512:b * 512 + 512],
                                      kh[:, kt * 128:kt * 128 + 128],
                                      qh[:, b * 512:b * 512 + 512],
                                      start=True, stop=True, skip=True,
                                      mark=(b == 3))
                        tok[("qkm", h, kt)] = t2
                        T.wait_future("cDVE", tok[("vt", h, kt)])
                        for b in range(4):
                            t2 = T.mm(psC[j * 64:j * 64 + 64,
                                          b * 512:b * 512 + 512],
                                      vt[:, :],
                                      eT[:, b * 512:b * 512 + 512],
                                      start=(kt == 0), stop=(kt == NQT - 1),
                                      skip=True, mark=(b == 3))
                        tok[("pvm", h, kt)] = t2
                vt0 = ec.cnt["cDVE"]
                ex0 = ec.cnt["cACT"]
                for kt in range(NQT):
                    tok[("vt", h, kt)] = vt0 + 4 * kt + 4
                    tok[("expf", h, kt)] = ex0 + kt + 1
                    tok[("exp", h, kt)] = ex0 + kt + 1
                blk.tensor(t_sc)

                def v_sc(v, h=h, vh=vh, vt0=vt0):
                    V = VW(ec, v)
                    for kt in range(NQT):
                        V.wait("cPE", tok[("qkm", h, kt)])
                        V.red(mxc[:, 0:1],
                              ap3(psS[:, :], 0, [[2048, 128], [1, 2048]]),
                              op=ALU.max, negate=True)
                        V.ts(mxc[:, 0:1], mxc[:, 0:1], 0.125)
                        V.wait_future("cACT", tok[("expf", h, kt)])
                        V.recip(dsc[:, 0:1], dsc[:, 0:1])
                        if kt >= 1:
                            V.wait("cPE", tok[("pvm", h, kt - 1)])
                        t2 = V.ts(vt[:, :], vh[:, kt * 64:kt * 64 + 64],
                                  dsc[:, 0:1], op0=ALU.mult)
                        assert t2 == tok[("vt", h, kt)]
                blk.vector(v_sc)

                def s_sc(s, h=h, ex0=ex0):
                    S = SW(ec, s)
                    for kt in range(NQT):
                        S.wait("cDVE", vt0 + 4 * kt + 2)
                        if kt >= 1:
                            S.wait("cPE", tok[("pvm", h, kt - 1)])
                        t2 = S.act(eT[:, :], psS[:, :], AF.Exp,
                                   bias=mxc[:, 0:1], scale=0.125,
                                   accum_out=dsc[:, 0:1])
                        assert t2 == ex0 + kt + 1
                blk.scalar(s_sc)

            def s_ce(s, hp=hp):
                S = SW(ec, s)
                S.wait("cPE", tok[("pvm", 2 * hp + 1, NQT - 1)])
                for b in range(4):
                    t2 = S.copy(ao2[:, hp * N + b * 512:hp * N + b * 512 + 512],
                                psC[:, b * 512:b * 512 + 512])
                tok[("ctxev", hp)] = t2
            blk.scalar(s_ce)

        def t_pool(t):
            T = TW(ec, t)
            T.wait("cACT", tok[("exp", 15, NQT - 1)])
            for ncc in range(NC4):
                sl = slice(ncc * 512, ncc * 512 + 512)
                for dk in range(NDK):
                    t2 = T.mm(psS[0:64, sl], poolind[:, :],
                              xn2[:, dk * N + ncc * 512:
                                  dk * N + ncc * 512 + 512],
                              start=(dk == 0), stop=(dk == NDK - 1),
                              skip=True, mark=(dk == NDK - 1))
                tok[("pool", ncc)] = t2
        blk.tensor(t_pool)

        def s_pe(s):
            S = SW(ec, s)
            for ncc in range(NC4):
                sl = slice(ncc * 512, ncc * 512 + 512)
                S.wait("cPE", tok[("pool", ncc)])
                t2 = S.copy(pooled[:, sl], psS[0:64, sl])
            tok["poole"] = t2
        blk.scalar(s_pe)

        for oc in range(NDK):
            def g_f(g, oc=oc):
                G = GW(ec, g)
                if oc >= 2:
                    G.wait_future("cPE", tok[("fin", (oc - 1) * NC4 - 1)])
                h0 = (oc % 2) * 1024
                sname = "ldfA" if oc % 2 == 0 else "ldfB"
                G.dma(sname, warA[:, h0:h0 + 1024],
                      bass.AP(lwT_h.tensor, oc * 128,
                              [[D, 128], [128 * D, NDK], [1, 128]]))
                tok[("fl", oc)] = G.dma(
                    sname, warB[:, h0:h0 + 1024],
                    bass.AP(wT_h.tensor, oc * 128,
                            [[D, 128], [128 * D, NDK], [1, 128]]))
            blk.gpsimd(g_f)

            for ncc in range(NC4):
                idx = oc * NC4 + ncc
                bank = idx % 4
                half = idx % 2

                def t_fin(t, oc=oc, ncc=ncc, idx=idx, bank=bank):
                    T = TW(ec, t)
                    h0 = (oc % 2) * 1024
                    if ncc == 0:
                        T.wait("ldfA" if oc % 2 == 0 else "ldfB",
                               tok[("fl", oc)])
                    if idx == 0:
                        T.wait("cACT", tok["poole"])
                        T.wait("cACT", tok[("ctxev", 7)])
                    if idx >= 4:
                        T.wait_future("cDVE", tok[("fst", idx - 4)])
                    ps = psC[:, bank * 512:bank * 512 + 512]
                    for dk in range(NDK):
                        T.mm(ps, warA[:, h0 + dk * 128:h0 + dk * 128 + 128],
                             ao2[:, dk * N + ncc * 512:
                                 dk * N + ncc * 512 + 512],
                             start=(dk == 0), stop=False, skip=True)
                    for dk in range(NDK):
                        T.mm(ps, warB[:, h0 + dk * 128:h0 + dk * 128 + 128],
                             xn2[:, dk * N + ncc * 512:
                                 dk * N + ncc * 512 + 512],
                             start=False, stop=False, skip=True)
                    t2 = T.mm(ps, dishaB[:, oc * 128:oc * 128 + 128],
                              pooled[:, ncc * 512:ncc * 512 + 512],
                              start=False, stop=True, skip=True, mark=True)
                    tok[("fin", idx)] = t2
                blk.tensor(t_fin)

                def v_fin(v, oc=oc, ncc=ncc, idx=idx, bank=bank, half=half):
                    V = VW(ec, v)
                    V.wait("cPE", tok[("fin", idx)])
                    if idx >= 2:
                        sname = "st0" if half == 0 else "st1"
                        V.wait_future(sname, 16 * (idx // 2))
                    ps = psC[:, bank * 512:bank * 512 + 512]
                    t2 = V.stt(outs[:, half * 512:half * 512 + 512],
                               ps, lbc[:, oc:oc + 1],
                               x1[:, oc * N + ncc * 512:
                                  oc * N + ncc * 512 + 512],
                               ALU.add, ALU.add)
                    tok[("fst", idx)] = t2
                blk.vector(v_fin)

        def sy_st(s):
            for oc in range(NDK):
                for ncc in range(NC4):
                    idx = oc * NC4 + ncc
                    half = idx % 2
                    ec.wait(s, "cDVE", tok[("fst", idx)])
                    ec.inc("st0" if half == 0 else "st1", s.dma_start(
                        out=bass.AP(out_h.tensor, oc * 128 * N + ncc * 512,
                                    [[N, 128], [1, 512]]),
                        in_=outs[:, half * 512:half * 512 + 512]), 16)
            ec.wait(s, "st0")
            ec.wait(s, "st1")
        blk.sync(sy_st)


def build_l1():
    nc = bass.Bass("TRN2", target_bir_lowering=False)
    io = {n: nc.dram_tensor(n, sh, dt, kind="ExternalInput").ap()
          for (n, sh, dt) in _L1_IO}
    for n in ("x", "ao", "z6"):
        io[n] = nc.dram_tensor(n, [D, N], F32, kind="ExternalInput").ap()
    io["out"] = nc.dram_tensor("out", [D, N], F32,
                               kind="ExternalOutput").ap()
    with nc.Block() as blk:
        ec = EC(nc, blk)
        emit_l1(nc, blk, ec, io)
        ec.close()
    return nc


def build_full(phases=("pre", "l0attn", "inner", "l1"), nlayers=IDEPTH):
    nc = bass.Bass("TRN2", target_bir_lowering=False)
    x_h = nc.dram_tensor("x", [D, N], F32, kind="ExternalInput").ap()
    out_h = nc.dram_tensor("out", [D, N], F32, kind="ExternalOutput").ap()
    xn_i = nc.dram_tensor("xn_i", [D, N], F32, kind="Internal").ap()
    ao_i = nc.dram_tensor("ao_i", [D, N], F32, kind="Internal").ap()
    z6_i = nc.dram_tensor("z6_i", [D, N], F32, kind="Internal").ap()
    a_io = {n: nc.dram_tensor(n, sh, dt, kind="ExternalInput").ap()
            for (n, sh, dt) in _L0A_IO if n != "xn"}
    a_io["xn"] = xn_i
    a_io["ao"] = ao_i
    i_io = _inner_io(nc, IDEPTH)
    i_io["z"] = xn_i
    i_io["zo"] = z6_i
    f_io = {n: nc.dram_tensor(n, sh, dt, kind="ExternalInput").ap()
            for (n, sh, dt) in _L1_IO}
    f_io["x"] = x_h
    f_io["ao"] = ao_i
    f_io["z6"] = z6_i
    f_io["out"] = out_h
    if "l1" not in phases:
        if "inner" in phases:
            i_io["zo"] = out_h
        elif "l0attn" in phases:
            a_io["ao"] = out_h
        else:
            pass
    with nc.Block() as blk:
        ec = EC(nc, blk)
        if "pre" in phases:
            ec.px = "p_"
            emit_pre(nc, blk, ec,
                     {"x": x_h, "xn": xn_i if len(phases) > 1 else out_h},
                     prefix="p_")
            ec.barrier()
        if "l0attn" in phases:
            ec.px = "a_"
            emit_l0attn(nc, blk, ec, a_io, prefix="a_")
            ec.barrier()
        if "inner" in phases:
            ec.px = "i_"
            emit_inner_chain(nc, blk, ec, i_io, nlayers=nlayers,
                             prefix="i_")
            ec.barrier()
        if "l1" in phases:
            ec.px = "f_"
            emit_l1(nc, blk, ec, f_io, prefix="f_")
        ec.close()
    return nc


def full_inmap(b, x, inp, hc):
    T_ = lambda a: np.ascontiguousarray(np.asarray(a, np.float32).T)
    m = {"x": np.ascontiguousarray(x[b].T)}
    m.update({
        "wq": T_(inp["L0_attn_Wq"]), "wk": T_(inp["L0_attn_Wk"]),
        "wv": T_(inp["L0_attn_Wv"]), "wo": T_(inp["L0_attn_Wo"]),
        "bq": percol(inp["L0_attn_bq"]), "bk": percol(inp["L0_attn_bk"]),
        "bo": percol(inp["L0_attn_bo"]),
        "bv": np.asarray(inp["L0_attn_bv"], np.float32).reshape(1, -1),
        "lam": np.asarray(inp["L0_attn_prior_lambda"],
                          np.float32).reshape(1),
        "mask8": hc["mask8"], "ind16": hc["ind16"],
        "idnf": np.eye(16, dtype=np.float32),
    })
    ic = inner_chain_inmap(np.zeros((D, N), np.float32), inp)
    del ic["z"]
    m.update(ic)
    lm = l1_inmap(np.zeros((D, N), np.float32), np.zeros((D, N), np.float32),
                  np.zeros((D, N), np.float32), inp)
    for n in ("x", "ao", "z6"):
        del lm[n]
    m.update(lm)
    return m


def kernel(**inputs):
    inp = {k: np.asarray(v) for k, v in inputs.items()}
    x = np.asarray(inp["L0_attn_x"], np.float32)
    B = x.shape[0]
    hc = host_consts()
    nc = build_full()
    maps = [full_inmap(b, x, inp, hc) for b in range(B)]
    res = run_bass_kernel_spmd(nc, maps, core_ids=list(range(B)))
    out = np.stack([np.asarray(res.results[b]["out"]).T for b in range(B)])
    return np.ascontiguousarray(out, np.float32)
